# version 17
# baseline (speedup 1.0000x reference)
import numpy as np
import ml_dtypes
import concourse.bass as bass
import concourse.mybir as mybir
from concourse.bass_utils import run_bass_kernel_spmd
from contextlib import ExitStack

F32 = mybir.dt.float32
BF16 = mybir.dt.bfloat16
I16 = mybir.dt.int16
U16 = mybir.dt.uint16
AF = mybir.ActivationFunctionType
ALU = mybir.AluOpType
AX = mybir.AxisListType

T = 2048
D = 1024
NT = 16
H = 8
HB = 16
DHB = 64
II = 2816
NI = 22
LN_EPS = 1e-5
ENGINES = ("tensor", "vector", "scalar", "gpsimd", "sync")
BFNP = np.dtype(ml_dtypes.bfloat16)


def host_constants():
    c = {}
    i_idx = np.arange(128, dtype=np.float32)[:, None]
    j_idx = np.arange(128, dtype=np.float32)[None, :]
    c["eye128"] = np.eye(128, dtype=np.float32).astype(BFNP)
    c["eye8"] = np.tile(np.eye(128, dtype=np.float32), (1, 8)).astype(BFNP)
    c["iota_dif"] = np.ascontiguousarray(np.broadcast_to(j_idx - i_idx, (128, 128)), dtype=np.float32)
    c["iota_colp1"] = np.ascontiguousarray(i_idx + 1.0, dtype=np.float32)
    c["iota_colrev"] = np.ascontiguousarray(127.0 - i_idx, dtype=np.float32)
    c["m_up_s_neg"] = np.ascontiguousarray(-(j_idx > i_idx).astype(np.float32))
    c["m_low_s_neg"] = np.ascontiguousarray(-(j_idx < i_idx).astype(np.float32))
    c["m_up_i"] = np.ascontiguousarray((j_idx >= i_idx).astype(np.float32))
    c["ones_col"] = np.ones((1, 64), dtype=np.float32).astype(BFNP)
    inv_freq = 1.0 / 10000.0 ** (np.arange(0, DHB, 2, dtype=np.float32) / DHB)
    t_pos = np.arange(T, dtype=np.float32)
    freqs = np.outer(t_pos, inv_freq)
    cosw = np.cos(freqs)
    sinw = np.sin(freqs)
    HD2 = DHB // 2
    c["ropec"] = np.ascontiguousarray(
        cosw.reshape(NT, 128, HD2).transpose(1, 0, 2).reshape(128, NT * HD2)).astype(BFNP)
    c["ropes"] = np.ascontiguousarray(
        sinw.reshape(NT, 128, HD2).transpose(1, 0, 2).reshape(128, NT * HD2)).astype(BFNP)
    return c


CONST_SPECS = [
    ("eye128", [128, 128], BF16), ("eye8", [128, 1024], BF16),
    ("iota_dif", [128, 128], F32),
    ("iota_colp1", [128, 1], F32), ("iota_colrev", [128, 1], F32),
    ("m_up_s_neg", [128, 128], F32), ("m_low_s_neg", [128, 128], F32),
    ("m_up_i", [128, 128], F32), ("ones_col", [1, 64], BF16),
    ("ropec", [128, NT * DHB // 2], BF16), ("ropes", [128, NT * DHB // 2], BF16),
]


class DmaQ:
    def __init__(self, em, eng, name):
        self.em = em
        self.eng = eng
        self.h = em.ctx.enter_context(em.nc.semaphore(name))
        self.cnt = 0
        self.waited = {e: 0 for e in ENGINES}

    def start(self, out, in_, wait=()):
        for tk in wait:
            self.em.wait(self.eng, tk)
        if self.cnt > 0 and self.waited[self.eng] < self.cnt:
            self.wait(self.eng, (self, self.cnt))
        self.cnt += 16
        h = self.h

        def fn(e, out=out, in_=in_):
            e.dma_start(out=out, in_=in_).then_inc(h, 16)
        self.em.q[self.eng].append(fn)
        self.em.ninstr += 1
        return (self, self.cnt)

    def start_t(self, out, in_, wait=()):
        for tk in wait:
            self.em.wait(self.eng, tk)
        if self.cnt > 0 and self.waited[self.eng] < self.cnt:
            self.wait(self.eng, (self, self.cnt))
        self.cnt += 16
        h = self.h

        def fn(e, out=out, in_=in_):
            e.dma_start_transpose(out=out, in_=in_).then_inc(h, 16)
        self.em.q[self.eng].append(fn)
        self.em.ninstr += 1
        return (self, self.cnt)

    def start_t_batch(self, pairs, wait=()):
        for tk in wait:
            self.em.wait(self.eng, tk)
        if self.cnt > 0 and self.waited[self.eng] < self.cnt:
            self.wait(self.eng, (self, self.cnt))
        h = self.h
        for out, in_ in pairs:
            self.cnt += 16

            def fn(e, out=out, in_=in_):
                e.dma_start_transpose(out=out, in_=in_).then_inc(h, 16)
            self.em.q[self.eng].append(fn)
            self.em.ninstr += 1
        return (self, self.cnt)

    def wait(self, eng, token):
        q, val = token
        assert q is self
        if val <= self.waited[eng]:
            return
        self.waited[eng] = val
        self.em.q[eng].append(lambda e, h=self.h, val=val: e.wait_ge(h, val))


class Em:

    def __init__(self, nc, ctx):
        self.nc = nc
        self.ctx = ctx
        self.q = {e: [] for e in ENGINES}
        self.sem = {e: ctx.enter_context(nc.semaphore(f"c_{e}")) for e in ENGINES}
        self.cnt = {e: 0 for e in ENGINES}
        self.waited = {e: {} for e in ENGINES}
        self.ninstr = 0

    AUTOSYNC = ("vector", "scalar", "gpsimd")

    def emit(self, eng, fn, inc=False):
        self.ninstr += 1
        if eng in self.AUTOSYNC:
            prev = self.cnt[eng]
            sem = self.sem[eng]
            if prev > self.waited[eng].get(eng, 0):
                self.waited[eng][eng] = prev
                self.q[eng].append(lambda e, sem=sem, prev=prev: e.wait_ge(sem, prev))
                self.ninstr += 1
            inc = True
        if inc:
            self.cnt[eng] += 1
            sem = self.sem[eng]

            def wrapped(e, fn=fn, sem=sem):
                fn(e).then_inc(sem, 1)
            self.q[eng].append(wrapped)
        else:
            self.q[eng].append(fn)
        return (eng, self.cnt[eng])

    def tok(self, eng):
        return (eng, self.cnt[eng])

    def wait(self, eng, token):
        if token is None:
            return
        src, val = token
        if isinstance(src, DmaQ):
            src.wait(eng, token)
            return
        if val <= 0:
            return
        if val <= self.waited[eng].get(src, 0):
            return
        self.waited[eng][src] = val
        sem = self.sem[src]
        self.q[eng].append(lambda e, sem=sem, val=val: e.wait_ge(sem, val))

    def bump(self, eng):
        self.cnt[eng] += 1
        sem = self.sem[eng]
        self.q[eng].append(lambda e, sem=sem: e.drain().then_inc(sem, 1))
        self.ninstr += 1
        return (eng, self.cnt[eng])

    def checkpoint(self):
        toks = [self.bump(e) for e in ENGINES]
        for e in ENGINES:
            for tk in toks:
                self.wait(e, tk)

    def play(self):
        with self.nc.Block() as blk:
            for eng in ENGINES:
                ops = self.q[eng]

                def body(e, ops=ops):
                    for fn in ops:
                        fn(e)
                getattr(blk, eng)(body)


def build_program(parts=4):
    nc = bass.Bass("TRN2", target_bir_lowering=False)
    ctx = ExitStack()

    x_in = nc.dram_tensor("x", [T, D], F32, kind="ExternalInput").ap()
    w = {
        "Ww": nc.dram_tensor("Ww", [D, D], F32, kind="ExternalInput").ap(),
        "Wr": nc.dram_tensor("Wr", [D, D], F32, kind="ExternalInput").ap(),
        "Wb": nc.dram_tensor("Wb", [H, D], F32, kind="ExternalInput").ap(),
        "decay_t": nc.dram_tensor("decay_t", [128, H], F32, kind="ExternalInput").ap(),
        "Wqkv": nc.dram_tensor("Wqkv", [D, 3 * D], F32, kind="ExternalInput").ap(),
        "Wo": nc.dram_tensor("Wo", [D, D], F32, kind="ExternalInput").ap(),
        "wg": nc.dram_tensor("wg", [II, D], F32, kind="ExternalInput").ap(),
        "wu": nc.dram_tensor("wu", [II, D], F32, kind="ExternalInput").ap(),
        "wd": nc.dram_tensor("wd", [D, II], F32, kind="ExternalInput").ap(),
    }
    consts_d = {}
    for name, shape, dt_ in CONST_SPECS:
        consts_d[name] = nc.dram_tensor(name, shape, dt_, kind="ExternalInput").ap()
    y_out = nc.dram_tensor("y", [T, D], F32, kind="ExternalOutput").ap()
    scr = {
        "wgT": nc.dram_tensor("wgT_scr", [D, II], BF16, kind="Internal").ap(),
        "wuT": nc.dram_tensor("wuT_scr", [D, II], BF16, kind="Internal").ap(),
        "wdT": nc.dram_tensor("wdT_scr", [II, D], BF16, kind="Internal").ap(),
        "wg16": nc.dram_tensor("wg16_scr", [II, D], BF16, kind="Internal").ap(),
        "wu16": nc.dram_tensor("wu16_scr", [II, D], BF16, kind="Internal").ap(),
        "wd16": nc.dram_tensor("wd16_scr", [D, II], BF16, kind="Internal").ap(),
        "qkv16": nc.dram_tensor("qkv16_scr", [D, 3 * D], BF16, kind="Internal").ap(),
        "wo16": nc.dram_tensor("wo16_scr", [D, D], BF16, kind="Internal").ap(),
        "aob": nc.dram_tensor("aob_scr", [T, D], BF16, kind="Internal").ap(),
    }

    def sbuf(name, shape, dt_):
        return ctx.enter_context(nc.sbuf_tensor(name, shape, dt_))

    env = dict(nc=nc, x_in=x_in, w=w, consts_d=consts_d, y_out=y_out, scr=scr)
    env["X"] = sbuf("xres", [128, NT * D], F32)
    env["BA"] = sbuf("bigA", [128, 16384], BF16)
    env["BB"] = sbuf("bigB", [128, 16384], BF16)
    env["BC"] = sbuf("bigC", [128, 17408], BF16)
    env["MED"] = sbuf("med", [128, 8192], BF16)
    env["MED2"] = sbuf("med2", [128, 4096], BF16)
    env["STAT"] = sbuf("stat", [128, 96], F32)
    env["COLS"] = sbuf("cols", [128, 128], F32)
    env["QCOL"] = sbuf("qcol", [128, 528], F32)
    env["HC"] = sbuf("hcols", [128, 64], F32)
    env["MASKU"] = sbuf("masku", [128, 1024], BF16)
    env["MASKL"] = sbuf("maskl", [128, 1024], BF16)
    env["MASKA"] = sbuf("maska", [128, 1024], BF16)
    env["DEALL"] = sbuf("deall", [128, 1024], BF16)
    env["CTDE"] = sbuf("ctde", [128, 1024], BF16)
    env["SST"] = sbuf("Sstate", [128, 1024], BF16)
    env["CST"] = {name: sbuf("c_" + name, shape, dt_) for name, shape, dt_ in CONST_SPECS}
    env["PS"] = ctx.enter_context(nc.psum_tensor("PS", [128, 4096], F32))

    em = Em(nc, ctx)
    env["em"] = em

    b = Builder(env)
    b.emit_all(parts)
    em.play()
    ctx.close()
    nc._n_instr = em.ninstr
    return nc


class Builder:
    def __init__(self, env):
        self.__dict__.update(env)
        self.bank_w = {}
        self.bank_r = {}
        self._dqs = {}

    def dq(self, name, eng="sync"):
        if name not in self._dqs:
            self._dqs[name] = DmaQ(self.em, eng, "dq_" + name)
        return self._dqs[name]

    def t(self, fn, inc=False):
        return self.em.emit("tensor", fn, inc)

    def v(self, fn, inc=False):
        ivq = getattr(self, "_iv_queue", None)
        if ivq and self._iv_on:
            self._iv_on = False
            for _ in range(min(self._iv_k, len(ivq))):
                ivq.pop(0)()
            self._iv_on = True
        return self.em.emit("vector", fn, inc)

    def v_crit(self, fn, inc=False):
        return self.em.emit("vector", fn, inc)

    def s(self, fn, inc=False):
        return self.em.emit("scalar", fn, inc)

    def w8(self, eng, tok):
        self.em.wait(eng, tok)

    def bankf(self, i, ncols=512):
        return self.PS[:, 512 * i:512 * i + ncols]

    def bankb(self, i, ncols=1024):
        return self.PS[:].bitcast(BF16)[:, 1024 * i:1024 * i + ncols]

    def use_bank(self, i):
        self.w8("tensor", self.bank_r.get(i))

    def wrote_bank(self, i):
        self.bank_w[i] = self.em.tok("tensor")

    def read_bank(self, i, eng):
        self.w8(eng, self.bank_w.get(i))

    def bank_read_done(self, i, eng):
        self.bank_r[i] = self.em.tok(eng)

    def transpose8(self, srcs, bank, dst, evac_eng="vector", prow=128):
        n = len(srcs)
        self.use_bank(bank)
        for i, sap in enumerate(srcs):
            self.t(lambda t, sap=sap, i=i: t.transpose(
                self.bankb(bank)[0:prow, i * 128:(i + 1) * 128], sap,
                self.CST["eye128"][:]), inc=(i == n - 1))
        self.wrote_bank(bank)
        self.read_bank(bank, evac_eng)
        src_ap = self.bankb(bank)[0:prow, 0:n * 128]
        if evac_eng == "vector":
            self.v(lambda v, dst=dst, src_ap=src_ap: v.tensor_copy(out=dst, in_=src_ap),
                   inc=True)
        else:
            self.s(lambda s, dst=dst, src_ap=src_ap: s.copy(out=dst, in_=src_ap),
                   inc=True)
        self.bank_read_done(bank, evac_eng)
        return self.em.tok(evac_eng)

    def emit_all(self, parts):
        em = self.em
        self.load_x_and_consts()
        if parts >= 0.1:
            self.precast_weights()
        if parts >= 0.2:
            self.attn_a_weight_prep()
        if parts >= 0.3:
            self.decay_consts()
        if parts >= 0.1:
            em.checkpoint()
        if parts >= 0.6:
            self.ln_prologue(first=True)
        if parts >= 1:
            if parts >= 1.5:
                self.prep_overlay_init()
            self.attn_a()
            em.checkpoint()
        if parts == 1.5:
            self.prep_drain()
            em.checkpoint()
        if parts >= 2:
            self.ln_prologue()
            self.ffn(first=True)
            em.checkpoint()
        if parts >= 3:
            self.ln_prologue()
            self.attn_b()
            em.checkpoint()
        if parts >= 4:
            self.ln_prologue()
            self.ffn(first=False)
            em.checkpoint()
        em.checkpoint()
        yr = self.y_out.rearrange("(n p) d -> p n d", p=128)
        qy = self.dq("y")
        tok = qy.start(yr, self.X[:].rearrange("p (n d) -> p n d", d=D),
                       wait=[self.tok_xload])
        qy.wait("sync", tok)

    def load_x_and_consts(self):
        xr = self.x_in.rearrange("(n p) d -> p n d", p=128)
        self.tok_xload = self.dq("x").start(
            self.X[:].rearrange("p (n d) -> p n d", d=D), xr)
        for name, shape, dt_ in CONST_SPECS:
            self.tok_consts = self.dq("const").start(self.CST[name][:],
                                                     self.consts_d[name])

    def precast_weights(self):
        self.tok_cast = {}
        for src, dst in [("Wqkv", "qkv16"), ("Wo", "wo16")]:
            tok = self.dq("pre", "gpsimd").start(self.scr[dst], self.w[src])
        for dst in ("qkv16", "wo16"):
            self.tok_cast[dst] = tok

    def attn_a_weight_prep(self):
        self.tok_WwT = self._transpose_1024sq(self.w["Ww"], self.MED[:, 0:8192])
        self.tok_WrT = self._transpose_1024sq(self.w["Wr"], self.BC[:, 2048:10240])
        tok = self.dq("wb", "gpsimd").start(self.MED2[0:H, 0:D], self.w["Wb"],
                                            wait=[self.em.tok("tensor")])
        self.w8("tensor", tok)
        self.w8("tensor", self.tok_consts)
        self.use_bank(0)
        for cidx in range(8):
            self.t(lambda t, cidx=cidx: t.transpose(
                self.bankb(0)[:, cidx * 8:cidx * 8 + 8],
                self.MED2[0:H, cidx * 128:(cidx + 1) * 128],
                self.CST["eye128"][0:H, 0:H]), inc=(cidx == 7))
        self.wrote_bank(0)
        self.read_bank(0, "vector")
        self.v(lambda v: v.tensor_copy(out=self.MED2[:, 3584:3648],
                                       in_=self.bankb(0)[:, 0:64]), inc=True)
        self.bank_read_done(0, "vector")
        self.tok_wbT = self.em.tok("vector")

    def _transpose_1024sq(self, src_f32, dst):
        src_r = src_f32.rearrange("(c p) d -> c p d", p=128)
        entry = self.em.tok("tensor")
        tr = {-2: entry, -1: entry}
        for r in range(8):
            slot = self.MED2[:, (r % 2) * 1024:(r % 2) * 1024 + 1024]
            ltok = self.dq(f"stg{r % 2}", "gpsimd").start(slot, src_r[r],
                                                          wait=[tr.get(r - 2)])
            bank = r % 2
            self.w8("tensor", ltok)
            self.w8("tensor", self.tok_consts)
            self.use_bank(bank)
            for cdx in range(8):
                self.t(lambda t, slot=slot, bank=bank, cdx=cdx: t.transpose(
                    self.bankb(bank)[:, cdx * 128:(cdx + 1) * 128],
                    slot[:, cdx * 128:(cdx + 1) * 128],
                    self.CST["eye128"][:]), inc=(cdx == 7))
            tr[r] = self.em.tok("tensor")
            self.wrote_bank(bank)
            self.read_bank(bank, "vector")
            out = dst.rearrange("p (c q) -> p c q", q=1024)[:, :, r * 128:(r + 1) * 128]
            self.v(lambda v, bank=bank, out=out: v.tensor_copy(
                out=out, in_=self.bankb(bank)[:].rearrange("p (c q) -> p c q", q=128)),
                inc=True)
            self.bank_read_done(bank, "vector")
        return self.em.tok("vector")

    def decay_consts(self):
        CST = self.CST

        def hc(a, n=1):
            return self.HC[:, a:a + n]
        tokd = self.dq("decay", "gpsimd").start(hc(0, 8), self.w["decay_t"])
        self.w8("scalar", tokd)
        self.w8("scalar", self.tok_consts)
        stok = self.s(lambda s: s.activation(out=hc(8, 8), in_=hc(0, 8),
                                             func=AF.Sigmoid), inc=True)
        self.w8("scalar", stok)
        self.s(lambda s: s.activation(out=hc(16, 8), in_=hc(8, 8), func=AF.Ln), inc=True)
        self.w8("vector", self.em.tok("scalar"))
        self.v(lambda v: v.tensor_scalar(out=hc(24, 8), in0=hc(16, 8), scalar1=-1.0,
                                         scalar2=None, op0=ALU.mult), inc=True)
        self.w8("scalar", self.em.tok("vector"))
        for h in range(H):
            self.s(lambda s, h=h: s.activation(out=hc(32 + h), in_=CST["iota_colp1"][:],
                                               func=AF.Exp, scale=hc(16 + h)))
            self.s(lambda s, h=h: s.activation(out=hc(48 + h), in_=CST["iota_colrev"][:],
                                               func=AF.Exp, scale=hc(16 + h)))
            self.s(lambda s, h=h: s.activation(out=hc(56 + h), in_=hc(16 + h),
                                               func=AF.Exp, scale=128.0), inc=(h == H - 1))
        self.w8("vector", self.em.tok("scalar"))
        self.v(lambda v: v.tensor_scalar(out=hc(40, 8), in0=hc(32, 8), scalar1=-1.0,
                                         scalar2=None, op0=ALU.mult), inc=True)
        e1 = self.BA[:, 0:256].bitcast(F32)
        e2 = self.BA[:, 256:512].bitcast(F32)
        for h in range(H):
            sl = slice(h * 128, (h + 1) * 128)
            self.w8("scalar", self.em.tok("vector"))
            self.s(lambda s, h=h, e1=e1: s.activation(out=e1, in_=CST["iota_dif"][:],
                                                      func=AF.Exp, scale=hc(16 + h)))
            self.s(lambda s, h=h, e2=e2: s.activation(out=e2, in_=CST["iota_dif"][:],
                                                      func=AF.Exp, scale=hc(24 + h)))
            self.w8("vector", self.em.tok("scalar"))
            self.v(lambda v, h=h: v.tensor_tensor(
                out=self.COLS[:, 120:121], in0=hc(16 + h),
                in1=CST["iota_colp1"][:], op=ALU.mult), inc=True)
            self.w8("scalar", self.em.tok("vector"))
            self.s(lambda s, h=h, sl=sl: s.activation(out=self.DEALL[:, sl],
                                                      in_=CST["iota_dif"][:],
                                                      func=AF.Exp, scale=hc(16 + h),
                                                      bias=self.COLS[:, 120:121]),
                   inc=True)
            self.w8("vector", self.em.tok("scalar"))
            self.v(lambda v, sl=sl, e1=e1: v.tensor_tensor(
                out=self.MASKU[:, sl], in0=e1, in1=CST["m_up_s_neg"][:], op=ALU.mult))
            self.v(lambda v, sl=sl, e1=e1: v.tensor_tensor(
                out=self.MASKA[:, sl], in0=e1, in1=CST["m_up_i"][:], op=ALU.mult))
            self.v(lambda v, sl=sl, e2=e2: v.tensor_tensor(
                out=self.MASKL[:, sl], in0=e2, in1=CST["m_low_s_neg"][:], op=ALU.mult))
            self.v(lambda v, sl=sl, h=h: v.tensor_scalar(
                out=self.CTDE[:, sl], in0=CST["eye128"][:],
                scalar1=hc(56 + h), scalar2=None, op0=ALU.mult), inc=True)
        self.v(lambda v: v.memset(self.SST[:], 0.0))
        self.v(lambda v: v.memset(self.STAT[:, 64:80], 0.0), inc=True)
        self.tok_dk = self.em.tok("vector")

    def ln_prologue(self, first=False):
        X, STAT = self.X, self.STAT
        scrf = self.BA[:, 0:2048].bitcast(F32)
        if first:
            self.w8("vector", self.tok_xload)
            for n in range(NT):
                self.v(lambda v, n=n: v.tensor_reduce(
                    out=STAT[:, n:n + 1], in_=X[:, n * D:(n + 1) * D],
                    axis=AX.X, op=ALU.add), inc=(n == NT - 1))
        else:
            self.v(lambda v: v.tensor_tensor(out=STAT[:, 0:16], in0=STAT[:, 0:16],
                                             in1=STAT[:, 64:80], op=ALU.add), inc=True)
        self.w8("scalar", self.em.tok("vector"))
        self.w8("scalar", self.tok_xload)
        for n in range(NT):
            stok = self.s(lambda s, n=n: s.activation(
                out=scrf, in_=X[:, n * D:(n + 1) * D], func=AF.Square,
                accum_out=STAT[:, 16 + n:17 + n]), inc=True)
            self.w8("scalar", stok)
        self.w8("vector", self.em.tok("scalar"))
        self.v(lambda v: v.tensor_scalar(out=STAT[:, 32:48], in0=STAT[:, 0:16],
                                         scalar1=-1.0 / D, scalar2=None, op0=ALU.mult))
        self.v(lambda v: v.tensor_scalar(out=STAT[:, 64:80], in0=STAT[:, 16:32],
                                         scalar1=1.0 / D, scalar2=None, op0=ALU.mult))
        self.v(lambda v: v.tensor_tensor(out=STAT[:, 80:96], in0=STAT[:, 32:48],
                                         in1=STAT[:, 32:48], op=ALU.mult))
        self.v(lambda v: v.scalar_tensor_tensor(out=STAT[:, 80:96], in0=STAT[:, 80:96],
                                                scalar=-1.0, in1=STAT[:, 64:80],
                                                op0=ALU.mult, op1=ALU.add))
        self.v(lambda v: v.tensor_scalar(out=STAT[:, 80:96], in0=STAT[:, 80:96],
                                         scalar1=LN_EPS, scalar2=None, op0=ALU.add),
               inc=True)
        self.w8("scalar", self.em.tok("vector"))
        stok = self.s(lambda s: s.activation(out=STAT[:, 64:80], in_=STAT[:, 80:96],
                                             func=AF.Ln), inc=True)
        self.w8("scalar", stok)
        self.s(lambda s: s.activation(out=STAT[:, 48:64], in_=STAT[:, 64:80],
                                      func=AF.Exp, scale=-0.5), inc=True)
        self.tok_ln = self.em.tok("scalar")
        self.w8("vector", self.tok_ln)
        self.v(lambda v: v.memset(self.STAT[:, 0:16], 0.0))
        self.v(lambda v: v.memset(self.STAT[:, 64:80], 0.0), inc=True)

    def ln_apply(self, n, out_ap):
        self.v(lambda v, n=n, out_ap=out_ap: v.tensor_scalar(
            out=out_ap, in0=self.X[:, n * D:(n + 1) * D],
            scalar1=self.STAT[:, 32 + n:33 + n], scalar2=self.STAT[:, 48 + n:49 + n],
            op0=ALU.add, op1=ALU.mult), inc=True)
        return self.em.tok("vector")

    def resid_update(self, n, half, in0_ap, extra=None):
        xs = self.X[:, n * D + half * 512:n * D + half * 512 + 512]
        acc_col = (n) if half == 0 else (64 + n)
        acc = self.STAT[:, acc_col:acc_col + 1]
        if extra is not None:
            scx = self.BB[:, 14336:15360].bitcast(F32)
            self.v(lambda v, in0_ap=in0_ap, extra=extra, scx=scx: v.tensor_tensor(
                out=scx, in0=in0_ap, in1=extra, op=ALU.add), inc=True)
            in0_ap = scx
        self.v(lambda v, in0_ap=in0_ap, xs=xs, acc=acc: v.scalar_tensor_tensor(
            out=xs, in0=in0_ap, scalar=1.0, in1=xs, op0=ALU.mult, op1=ALU.add,
            accum_out=acc), inc=True)

    def attn_a(self):
        em, CST = self.em, self.CST
        BA, BB, MED2, COLS, HC = self.BA, self.BB, self.MED2, self.COLS, self.HC
        X, SST = self.X, self.SST

        def slot(idx, p):
            return BA[:, (idx * 2 + p) * 1024:(idx * 2 + p) * 1024 + 1024]

        def bbs(a, n=1024):
            return BB[:, a:a + n]

        def kT(p):
            return bbs(p * 1024)

        def kbT(p):
            return bbs(2048 + p * 1024)

        def kdeT(p):
            return bbs(4096 + p * 1024)
        U8 = bbs(6144)
        UT8 = bbs(7168)
        AT8 = bbs(8192)
        P8 = bbs(9216)
        PT8 = bbs(10240)
        Bm = bbs(11264)
        kcdT8 = bbs(12288)
        vnew8 = bbs(13312)

        def ccol(p, a, n=8):
            return COLS[:, p * 64 + a:p * 64 + a + n]
        scrq = self.BC[:, 0:2048].bitcast(F32)

        self.w8("vector", self.tok_dk)
        aa_end = {}
        eTok = {}

        def make_E(n):
            p = n % 2
            xn_t = slot(0, p)
            xnT_t = MED2[:, p * 1024:(p + 1) * 1024]
            ge = aa_end.get(p)
            rn_prev = eTok.get(("rn", n - 1))

            def s1():
                if ge:
                    self.w8("vector", ge["t"])
                    self.w8("vector", ge["s"])
                tok_xn = self.ln_apply(n, xn_t)
                eTok[("ln", n)] = tok_xn
                self.w8("tensor", tok_xn)
                self.transpose8([xn_t[:, c * 128:(c + 1) * 128] for c in range(8)],
                                p, xnT_t, "vector")
                eTok[("xnT", n)] = self.em.tok("vector")

            def s2():
                self.use_bank(p)
                self.w8("tensor", self.tok_wbT)
                self.w8("tensor", eTok[("xnT", n)])
                for kc in range(8):
                    self.t(lambda t, kc=kc: t.matmul(
                        self.bankf(p, 8), xnT_t[:, kc * 128:(kc + 1) * 128],
                        self.MED2[:, 3584 + kc * 8:3592 + kc * 8],
                        start=(kc == 0), stop=(kc == 7)), inc=(kc == 7))
                self.wrote_bank(p)
                self.read_bank(p, "scalar")
                if ge:
                    self.w8("scalar", ge["t"])
                    self.w8("scalar", ge["v"])
                self.s(lambda s_: s_.activation(out=ccol(p, 0), in_=self.bankf(p, 8),
                                                func=AF.Sigmoid), inc=True)
                self.bank_read_done(p, "scalar")

            def s3():
                self.w8("scalar", eTok[("ln", n)])
                if rn_prev:
                    self.w8("scalar", rn_prev)
                self.s(lambda s_: s_.activation(out=scrq, in_=xn_t,
                                                func=AF.Square), inc=True)
                self.w8("vector", self.em.tok("scalar"))
                self.v(lambda v: v.tensor_reduce(
                    out=ccol(p, 40).rearrange("p (h o) -> p h o", o=1),
                    in_=scrq.rearrange("p (h q) -> p h q", q=128), axis=AX.X,
                    op=ALU.add))
                self.v(lambda v: v.tensor_scalar(
                    out=ccol(p, 40), in0=ccol(p, 40), scalar1=1e-24, scalar2=None,
                    op0=ALU.max), inc=True)
                eTok[("rn", n)] = self.em.tok("vector")

            def s4():
                self.w8("scalar", eTok[("rn", n)])
                stok = self.s(lambda s_: s_.activation(out=ccol(p, 48),
                                                       in_=ccol(p, 40),
                                                       func=AF.Ln), inc=True)
                self.w8("scalar", stok)
                self.s(lambda s_: s_.activation(out=ccol(p, 8), in_=ccol(p, 48),
                                                func=AF.Exp, scale=-0.5), inc=True)
                eTok[("exp", n)] = self.em.tok("scalar")

            def s5():
                self.w8("vector", eTok[("exp", n)])
                self.v(lambda v: v.tensor_tensor(out=ccol(p, 16), in0=ccol(p, 8),
                                                 in1=ccol(p, 0), op=ALU.mult))
                self.v(lambda v: v.tensor_tensor(out=ccol(p, 24), in0=ccol(p, 16),
                                                 in1=HC[:, 40:48], op=ALU.mult))
                self.v(lambda v: v.tensor_tensor(out=ccol(p, 32), in0=ccol(p, 8),
                                                 in1=HC[:, 48:56], op=ALU.mult),
                       inc=True)

                def kbuild(dst, colap):
                    self.v(lambda v, dst=dst, colap=colap: v.tensor_tensor(
                        out=dst.rearrange("p (h q) -> p h q", q=128),
                        in0=xn_t.rearrange("p (h q) -> p h q", q=128),
                        in1=colap.rearrange("p (h o) -> p h o", o=1)
                            .broadcast_to((128, 8, 128)),
                        op=ALU.mult), inc=True)
                kbuild(slot(1, p), ccol(p, 8))
                kbuild(slot(2, p), ccol(p, 16))
                kbuild(slot(3, p), ccol(p, 24))
                kbuild(slot(4, p), ccol(p, 32))
                eTok[("kb", n)] = self.em.tok("vector")

            def s6():
                self.w8("tensor", eTok[("kb", n)])
                eTok[("kTe", n)] = self.transpose8(
                    [slot(1, p)[:, c * 128:(c + 1) * 128] for c in range(8)], p,
                    kT(p), "vector")
                self.v(lambda v: v.tensor_tensor(out=kdeT(p), in0=kT(p),
                                                 in1=self.DEALL[:], op=ALU.mult),
                       inc=True)
                eTok[("kdeT", n)] = self.em.tok("vector")
                self.bg_fill(4)

            def s7():
                eTok[("kbTe", n)] = self.transpose8(
                    [slot(2, p)[:, c * 128:(c + 1) * 128] for c in range(8)], p,
                    kbT(p), "scalar")

            def s8():
                self.w8("tensor", self.tok_WwT)
                for half in range(2):
                    self.use_bank(p)
                    for kc in range(8):
                        self.t(lambda t, kc=kc, half=half: t.matmul(
                            self.bankf(p), xnT_t[:, kc * 128:(kc + 1) * 128],
                            self.MED[:, kc * 1024 + half * 512:
                                      kc * 1024 + half * 512 + 512],
                            start=(kc == 0), stop=(kc == 7)), inc=(kc == 7))
                    self.wrote_bank(p)
                    self.read_bank(p, "vector")
                    self.v(lambda v, half=half: v.tensor_tensor(
                        out=slot(5, p)[:, half * 512:(half + 1) * 512]
                            .rearrange("p (h q) -> p h q", q=128),
                        in0=self.bankf(p).rearrange("p (h q) -> p h q", q=128),
                        in1=ccol(p, 4 * half, 4).rearrange("p (h o) -> p h o", o=1)
                            .broadcast_to((128, 4, 128)),
                        op=ALU.mult), inc=True)
                    self.bank_read_done(p, "vector")
                eTok[("vb", n)] = self.em.tok("vector")
                self.bg_fill(4)
            return [s1, s2, s3, s4, s5, s6, s7, s8]

        for seg in make_E(0):
            seg()
        for n in range(NT):
            p = n % 2
            eq = make_E(n + 1) if n + 1 < NT else []

            def ef(k=1):
                for _ in range(k):
                    if eq:
                        eq.pop(0)()

            def pf(i, h):
                return self.PS[:, 512 * i + h * 128:512 * i + h * 128 + 128]

            def pairf(i):
                return self.PS[:, 512 * i:512 * i + 1024]
            prs = [(2, kT, kbT), (4, kbT, kT), (6, kT, kT)]
            self.w8("tensor", eTok[("kTe", n)])
            self.w8("tensor", eTok[("kbTe", n)])
            for bk, lhs, rhs in prs:
                self.use_bank(bk)
                self.use_bank(bk + 1)
                for h in range(H):
                    self.t(lambda t, bk=bk, lhs=lhs, rhs=rhs, h=h, p=p: t.matmul(
                        pf(bk, h),
                        lhs(p)[:, h * 128:(h + 1) * 128],
                        rhs(p)[:, h * 128:(h + 1) * 128],
                        start=True, stop=True), inc=(h == H - 1))
                self.wrote_bank(bk)
                self.wrote_bank(bk + 1)
            for bk, dst, msk in [(2, U8, self.MASKU), (4, UT8, self.MASKL),
                                 (6, AT8, self.MASKA)]:
                self.read_bank(bk, "vector")
                self.read_bank(bk + 1, "vector")
                self.v(lambda v, bk=bk, dst=dst, msk=msk: v.tensor_tensor(
                    out=dst, in0=pairf(bk), in1=msk[:], op=ALU.mult), inc=True)
                self.bank_read_done(bk, "vector")
                self.bank_read_done(bk + 1, "vector")
            self.v(lambda v: v.tensor_tensor(out=Bm, in0=U8, in1=CST["eye8"][:],
                                             op=ALU.add), inc=True)
            tok_B0 = self.em.tok("vector")
            ef()
            curP, curPT = U8, UT8
            for j in range(1, 7):
                ef()
                self.bg_fill(6)
                self.use_bank(2)
                self.use_bank(3)
                self.w8("tensor", tok_B0)
                for h in range(H):
                    self.t(lambda t, h=h, curP=curP, curPT=curPT: t.matmul(
                        pf(2, h),
                        curP[:, h * 128:(h + 1) * 128],
                        curPT[:, h * 128:(h + 1) * 128],
                        start=True, stop=True), inc=(h == H - 1))
                self.wrote_bank(2)
                self.wrote_bank(3)
                if j < 6:
                    self.use_bank(4)
                    self.use_bank(5)
                    for h in range(H):
                        self.t(lambda t, h=h, curP=curP, curPT=curPT: t.matmul(
                            pf(4, h),
                            curPT[:, h * 128:(h + 1) * 128],
                            curP[:, h * 128:(h + 1) * 128],
                            start=True, stop=True), inc=(h == H - 1))
                    self.wrote_bank(4)
                    self.wrote_bank(5)
                self.read_bank(2, "scalar")
                self.read_bank(3, "scalar")
                self.w8("scalar", self.em.tok("tensor"))
                self.s(lambda s: s.copy(out=PT8, in_=pairf(2)), inc=True)
                self.bank_read_done(2, "scalar")
                self.bank_read_done(3, "scalar")
                if j < 6:
                    self.read_bank(4, "scalar")
                    self.read_bank(5, "scalar")
                    self.s(lambda s: s.copy(out=P8, in_=pairf(4)), inc=True)
                    self.bank_read_done(4, "scalar")
                    self.bank_read_done(5, "scalar")
                tok_P = self.em.tok("scalar")
                self.use_bank(6)
                self.use_bank(7)
                self.w8("tensor", tok_P)
                for h in range(H):
                    self.t(lambda t, h=h: t.matmul(
                        pf(6, h),
                        PT8[:, h * 128:(h + 1) * 128],
                        Bm[:, h * 128:(h + 1) * 128],
                        start=True, stop=True), inc=(h == H - 1))
                self.wrote_bank(6)
                self.wrote_bank(7)
                self.read_bank(6, "vector")
                self.read_bank(7, "vector")
                self.v(lambda v: v.tensor_tensor(out=Bm, in0=pairf(6), in1=Bm,
                                                 op=ALU.add), inc=True)
                self.bank_read_done(6, "vector")
                self.bank_read_done(7, "vector")
                tok_B0 = self.em.tok("vector")
                curP, curPT = P8, PT8
            self.use_bank(6)
            self.use_bank(7)
            self.w8("tensor", tok_B0)
            for h in range(H):
                self.t(lambda t, h=h, p=p: t.matmul(
                    pf(6, h),
                    slot(3, p)[:, h * 128:(h + 1) * 128],
                    Bm[:, h * 128:(h + 1) * 128],
                    start=True, stop=True), inc=(h == H - 1))
            self.wrote_bank(6)
            self.wrote_bank(7)
            self.read_bank(6, "scalar")
            self.read_bank(7, "scalar")
            self.s(lambda s: s.copy(out=kcdT8, in_=pairf(6)), inc=True)
            self.bank_read_done(6, "scalar")
            self.bank_read_done(7, "scalar")
            tok_kcd = self.em.tok("scalar")
            ef()
            for half in range(2):
                hs = list(range(4 * half, 4 * half + 4))
                self.use_bank(2)
                self.w8("tensor", tok_kcd)
                self.w8("tensor", eTok[("vb", n)])
                for h in hs:
                    q = (h % 4) * 128
                    self.t(lambda t, h=h, q=q, p=p: t.matmul(
                        self.bankf(2)[:, q:q + 128],
                        Bm[:, h * 128:(h + 1) * 128],
                        slot(5, p)[:, h * 128:(h + 1) * 128],
                        start=True, stop=False))
                    self.t(lambda t, h=h, q=q: t.matmul(
                        self.bankf(2)[:, q:q + 128],
                        kcdT8[:, h * 128:(h + 1) * 128],
                        SST[:, h * 128:(h + 1) * 128],
                        start=False, stop=True), inc=(h == hs[-1]))
                self.wrote_bank(2)
                self.read_bank(2, "vector")
                self.v(lambda v, half=half: v.tensor_copy(
                    out=vnew8[:, half * 512:(half + 1) * 512], in_=self.bankf(2)),
                    inc=True)
                self.bank_read_done(2, "vector")
                tok_vn = self.em.tok("vector")
                self.use_bank(3)
                self.w8("tensor", tok_vn)
                self.w8("tensor", eTok[("kdeT", n)])
                for h in hs:
                    q = (h % 4) * 128
                    self.t(lambda t, h=h, q=q, p=p: t.matmul(
                        self.bankf(3)[:, q:q + 128],
                        kdeT(p)[:, h * 128:(h + 1) * 128],
                        SST[:, h * 128:(h + 1) * 128],
                        start=True, stop=False))
                    self.t(lambda t, h=h, q=q: t.matmul(
                        self.bankf(3)[:, q:q + 128],
                        AT8[:, h * 128:(h + 1) * 128],
                        vnew8[:, h * 128:(h + 1) * 128],
                        start=False, stop=True), inc=(h == hs[-1]))
                self.wrote_bank(3)
                self.use_bank(4)
                for h in hs:
                    q = (h % 4) * 128
                    self.t(lambda t, h=h, q=q, p=p: t.matmul(
                        self.bankf(4)[:, q:q + 128],
                        slot(4, p)[:, h * 128:(h + 1) * 128],
                        vnew8[:, h * 128:(h + 1) * 128],
                        start=True, stop=False))
                    self.t(lambda t, h=h, q=q: t.matmul(
                        self.bankf(4)[:, q:q + 128],
                        self.CTDE[:, h * 128:(h + 1) * 128],
                        SST[:, h * 128:(h + 1) * 128],
                        start=False, stop=True), inc=(h == hs[-1]))
                self.wrote_bank(4)
                self.read_bank(3, "scalar")
                self.s(lambda s, half=half, p=p: s.copy(
                    out=slot(6, p)[:, half * 512:(half + 1) * 512],
                    in_=self.bankf(3)), inc=True)
                self.bank_read_done(3, "scalar")
                self.read_bank(4, "vector")
                self.v(lambda v, half=half: v.tensor_copy(
                    out=SST[:, half * 512:(half + 1) * 512], in_=self.bankf(4)),
                    inc=True)
                self.bank_read_done(4, "vector")
                ef()
                self.bg_fill(6)
            tok_o = self.em.tok("scalar")
            self.w8("tensor", tok_o)
            tok_oT = self.transpose8(
                [slot(6, p)[:, c * 128:(c + 1) * 128] for c in range(8)], 1 - p,
                slot(7, p), "vector")
            self.w8("tensor", self.tok_WrT)
            for half in range(2):
                bk = 5 + half
                self.use_bank(bk)
                self.w8("tensor", tok_oT)
                for kc in range(8):
                    self.t(lambda t, bk=bk, kc=kc, half=half, p=p: t.matmul(
                        self.bankf(bk), slot(7, p)[:, kc * 128:(kc + 1) * 128],
                        self.BC[:, 2048 + kc * 1024 + half * 512:
                                2048 + kc * 1024 + half * 512 + 512],
                        start=(kc == 0), stop=(kc == 7)), inc=(kc == 7))
                self.wrote_bank(bk)
                self.read_bank(bk, "vector")
                self.resid_update(n, half, self.bankf(bk),
                                  extra=slot(0, p)[:, half * 512:(half + 1) * 512])
                self.bank_read_done(bk, "vector")
            while eq:
                eq.pop(0)()
            aa_end[p] = dict(t=self.em.tok("tensor"), v=self.em.tok("vector"),
                             s=self.em.tok("scalar"))
            self.bg_fill(10)

    def ffn(self, first):
        if first:
            self.prep_drain()
        em, X, STAT = self.em, self.X, self.STAT
        BA, BB, BC, MED, MED2 = self.BA, self.BB, self.BC, self.MED, self.MED2

        def hgrp(i):
            return BC[:, i * II:(i + 1) * II]

        def xn_t(p):
            return BC[:, 11264 + p * 2048:11264 + p * 2048 + 2048].bitcast(F32)

        def xq_t(p):
            return BC[:, 15360 + p * 1024:15360 + p * 1024 + 1024]

        def xqT(c, i):
            return MED2[:, c * 512 + i * 128:c * 512 + i * 128 + 128]

        def hq_t(p):
            return BA[:, 0:II] if p == 0 else BB[:, 11264:11264 + II]

        def hqT_g(i):
            return BB[:, i * 2816:(i + 1) * 2816]

        def wslice(b):
            return MED[:, 0:8192] if b == 0 else BB[:, 0:8192]
        sil = BB[:, 14336:14848]
        wg_r = self.scr["wgT"].rearrange("(c p) i -> p c i", p=128)
        wu_r = self.scr["wuT"].rearrange("(c p) i -> p c i", p=128)
        wd_r = self.scr["wdT"].rearrange("(c p) d -> c p d", p=128)

        wdslot = [BA[:, 14080:15104], BA[:, 15104:16128], BB[:, 15360:16384]]
        g0t = em.tok("tensor")
        g0v = em.tok("vector")
        self._tok_gu = g0t
        self._tok_hqp = {0: g0t, 1: g0t}
        dtoks = {}
        wdq = {"q": 0}
        xn_pair = BC[:, 11264:15360].bitcast(F32)
        xq_pair = BC[:, 15360:17408]

        def emit_xq(grp, guard):
            tiles = list(range(grp * 4, grp * 4 + 4))
            for pi in range(2):
                self.w8("vector", guard)
                self.ln_apply(tiles[2 * pi], xn_t(0))
                self.ln_apply(tiles[2 * pi + 1], xn_t(1))
                self.quant_tile(xn_pair, xq_pair, 2048, src_f32=True)
                tok_xq = self.em.tok("vector")
                self.w8("tensor", tok_xq)
                for half in range(2):
                    i = 2 * pi + half
                    dst3 = MED2[:].rearrange("p (c q) -> p c q", q=512) \
                        [:, :, i * 128:(i + 1) * 128]
                    self.transpose8([xq_t(half)[:, c * 128:(c + 1) * 128]
                                     for c in range(8)],
                                    4 + (i % 2), dst3, "vector")
            self._tok_xqT = self.em.tok("vector")
            self._ffn_gtok = self.em.tok("tensor")

        def hquant(i, n):
            p = n % 2
            self.w8("vector", self._tok_hqp[p])
            self.quant_tile(hgrp(i), hq_t(p), II)
            self.w8("tensor", self.em.tok("vector"))
            for bi, lo in enumerate((0, 8, 16)):
                cnt = 8 if bi < 2 else 6
                self.transpose8(
                    [hq_t(p)[:, (lo + c) * 128:(lo + c + 1) * 128]
                     for c in range(cnt)],
                    4 + (bi % 2), hqT_g(i)[:, lo * 128:(lo + cnt) * 128], "vector")
            self._tok_hqp[p] = self.em.tok("tensor")

        def down_pass(pair):
            for li, i, n in pair:
                self.use_bank(2 * li)
                self.use_bank(2 * li + 1)
            self.w8("tensor", self.em.tok("vector"))
            for kc in range(NI):
                q = wdq["q"]
                slot = wdslot[q % 3]
                guard = [dtoks.get(q - 3), *self.tok_ffnprep]
                if q < 3:
                    guard += [g0t, g0v]
                dtok = self.dq(f"wd{q % 3}").start(slot, wd_r[kc], wait=guard)
                self.w8("tensor", dtok)
                for li, i, n in pair:
                    for half in range(2):
                        self.t(lambda t, li=li, i=i, kc=kc, half=half, slot=slot:
                               t.matmul(
                                   self.bankf(2 * li + half),
                                   hqT_g(i)[:, kc * 128:(kc + 1) * 128],
                                   slot[:, half * 512:(half + 1) * 512],
                                   start=(kc == 0), stop=(kc == NI - 1)),
                               inc=(li == pair[-1][0] and half == 1))
                dtoks[q] = self.em.tok("tensor")
                wdq["q"] = q + 1
            for li, i, n in pair:
                self.wrote_bank(2 * li)
                self.wrote_bank(2 * li + 1)

        def resid_pass(pair):
            for li, i, n in pair:
                for half in range(2):
                    bk = 2 * li + half
                    self.read_bank(bk, "vector")
                    self.resid_update(n, half, self.bankf(bk))
                    self.bank_read_done(bk, "vector")

        emit_xq(0, em.tok("tensor"))
        for grp in range(4):
            tiles = list(range(grp * 4, grp * 4 + 4))
            for nch in range(6):
                ncols = 512 if nch < 5 else 256
                wb = nch % 2
                wsl = wslice(wb)
                if wb == 0:
                    guard = [*self.tok_ffnprep, self._tok_gu]
                else:
                    guard = [*self.tok_ffnprep, self.em.tok("tensor")]
                tok_wg = self.dq(f"wsl{wb}").start(
                    wsl[:, 0:8 * ncols].rearrange("p (c q) -> p c q", q=ncols),
                    wg_r[:, :, nch * 512:nch * 512 + ncols], wait=guard)
                tok_wu = self.dq(f"wslu{wb}").start(
                    wsl[:, 4096:4096 + 8 * ncols].rearrange("p (c q) -> p c q", q=ncols),
                    wu_r[:, :, nch * 512:nch * 512 + ncols], wait=guard)
                self.w8("tensor", tok_wg)
                self.w8("tensor", tok_wu)
                self.w8("tensor", self._tok_xqT)
                for i, n in enumerate(tiles):
                    bkg, bku = (0, 1) if i % 2 == 0 else (2, 3)
                    self.use_bank(bkg)
                    self.use_bank(bku)
                    for kc in range(8):
                        self.t(lambda t, bkg=bkg, kc=kc, i=i, wsl=wsl, ncols=ncols:
                               t.matmul(self.bankf(bkg, ncols), xqT(kc, i),
                                        wsl[:, kc * ncols:(kc + 1) * ncols],
                                        start=(kc == 0), stop=(kc == 7)))
                        self.t(lambda t, bku=bku, kc=kc, i=i, wsl=wsl, ncols=ncols:
                               t.matmul(self.bankf(bku, ncols), xqT(kc, i),
                                        wsl[:, 4096 + kc * ncols:4096 + (kc + 1) * ncols],
                                        start=(kc == 0), stop=(kc == 7)),
                               inc=(kc == 7))
                    self.wrote_bank(bkg)
                    self.wrote_bank(bku)
                    self.read_bank(bkg, "scalar")
                    self.w8("scalar", self.em.tok("vector"))
                    self.s(lambda s, bkg=bkg, ncols=ncols: s.activation(
                        out=sil[:, 0:ncols], in_=self.bankf(bkg, ncols),
                        func=AF.Silu), inc=True)
                    self.bank_read_done(bkg, "scalar")
                    self.read_bank(bku, "vector")
                    self.w8("vector", self.em.tok("scalar"))
                    self.v(lambda v, bku=bku, i=i, nch=nch, ncols=ncols:
                           v.tensor_tensor(
                               out=hgrp(i)[:, nch * 512:nch * 512 + ncols],
                               in0=self.bankf(bku, ncols), in1=sil[:, 0:ncols],
                               op=ALU.mult), inc=True)
                    self.bank_read_done(bku, "vector")
            self._tok_gu = self.em.tok("tensor")
            pair0 = [(0, 0, tiles[0]), (1, 1, tiles[1])]
            pair1 = [(0, 2, tiles[2]), (1, 3, tiles[3])]
            hquant(0, tiles[0])
            hquant(1, tiles[1])
            down_pass(pair0)
            hquant(2, tiles[2])
            hquant(3, tiles[3])
            resid_pass(pair0)
            down_pass(pair1)
            if grp < 3:
                emit_xq(grp + 1, self._ffn_gtok)
            resid_pass(pair1)

    def quant_tile(self, src, dst, F, src_f32=False, sbase=None, eng="vector",
                   qc=None):
        nb = F // 16
        ST, c0 = sbase if sbase is not None else (self.BA, 2816)
        f32A = ST[:, c0:c0 + 2 * F].bitcast(F32)
        f32B = ST[:, c0 + 2 * F:c0 + 4 * F].bitcast(F32)
        rB = ST[:, c0 + 2 * F:c0 + 3 * F]
        d1B = ST[:, c0 + 3 * F:c0 + 4 * F]
        d2B = ST[:, c0:c0 + F]
        magB = ST[:, c0 + F:c0 + 2 * F]
        habs = ST[:, c0 + 3 * F:c0 + 4 * F]
        QC = qc if qc is not None else self.QCOL

        def v(fn, inc=False):
            return self.em.emit(eng, fn, inc)
        src3 = src.rearrange("p (b k) -> p b k", k=16)
        self.v(lambda v_: v_.tensor_reduce(
            out=QC[:, 0:nb].rearrange("p (b o) -> p b o", o=1), in_=src3,
            axis=AX.X, op=ALU.max, apply_absolute_value=True))
        self.v(lambda v_: v_.tensor_scalar(out=QC[:, 0:nb], in0=QC[:, 0:nb],
                                           scalar1=6e-12, scalar2=None, op0=ALU.max))
        self.v(lambda v_: v_.reciprocal(out=QC[:, 176:176 + nb], in_=QC[:, 0:nb]))
        self.v(lambda v_: v_.tensor_scalar(out=QC[:, 176:176 + nb],
                                           in0=QC[:, 176:176 + nb],
                                           scalar1=12.0, scalar2=None, op0=ALU.mult))
        self.v(lambda v_: v_.tensor_scalar(out=QC[:, 352:352 + nb], in0=QC[:, 0:nb],
                                           scalar1=1.0 / 12.0, scalar2=None,
                                           op0=ALU.mult), inc=True)
        if eng != "vector":
            self.em.wait(eng, self.em.tok("vector"))
        if eng != "vector":
            assert src_f32
            self._quant_pool_body(src, dst, F, QC, ST, c0, nb, v)
            return
        inv3 = QC[:, 176:176 + nb].rearrange("p (b o) -> p b o", o=1) \
            .broadcast_to((128, nb, 16))
        sc3 = QC[:, 352:352 + nb].rearrange("p (b o) -> p b o", o=1) \
            .broadcast_to((128, nb, 16))
        if src_f32 and eng != "vector":
            v(lambda v_: v_.tensor_scalar(out=f32A, in0=src, scalar1=0.0,
                                          scalar2=2.0, op0=ALU.is_ge, op1=ALU.mult))
            v(lambda v_: v_.scalar_tensor_tensor(out=f32A, in0=f32A, scalar=-1.0,
                                                 in1=src, op0=ALU.add, op1=ALU.mult))
            v(lambda v_: v_.scalar_tensor_tensor(
                out=f32A.rearrange("p (b k) -> p b k", k=16),
                in0=f32A.rearrange("p (b k) -> p b k", k=16), scalar=12.0, in1=inv3,
                op0=ALU.mult, op1=ALU.mult))
        elif src_f32:
            v(lambda v_: v_.tensor_scalar(
                out=f32A.bitcast(mybir.dt.uint32), in0=src.bitcast(mybir.dt.uint32),
                scalar1=0x7FFFFFFF, scalar2=None, op0=ALU.bitwise_and))
            v(lambda v_: v_.tensor_tensor(
                out=f32A.rearrange("p (b k) -> p b k", k=16),
                in0=f32A.rearrange("p (b k) -> p b k", k=16), in1=inv3,
                op=ALU.mult))
        else:
            v(lambda v_: v_.tensor_scalar(out=habs.bitcast(U16), in0=src.bitcast(U16),
                                          scalar1=0x7FFF, scalar2=None,
                                          op0=ALU.bitwise_and))
            v(lambda v_: v_.scalar_tensor_tensor(
                out=f32A.rearrange("p (b k) -> p b k", k=16),
                in0=habs.rearrange("p (b k) -> p b k", k=16), scalar=1.0, in1=inv3,
                op0=ALU.mult, op1=ALU.mult))
        M23 = 8388608.0
        v(lambda v_: v_.tensor_scalar(out=f32B, in0=f32A, scalar1=8.0, scalar2=12.0,
                                      op0=ALU.max, op1=ALU.min))
        v(lambda v_: v_.scalar_tensor_tensor(out=f32B, in0=f32B, scalar=-0.75,
                                             in1=f32A, op0=ALU.mult, op1=ALU.add))
        v(lambda v_: v_.tensor_scalar(out=f32A, in0=f32A, scalar1=4.0, scalar2=8.0,
                                      op0=ALU.max, op1=ALU.min))
        v(lambda v_: v_.scalar_tensor_tensor(out=f32B, in0=f32A, scalar=-0.5,
                                             in1=f32B, op0=ALU.mult, op1=ALU.add))
        v(lambda v_: v_.tensor_scalar(out=f32A, in0=f32B, scalar1=16.0 + M23,
                                      scalar2=None, op0=ALU.add))
        v(lambda v_: v_.tensor_scalar(out=rB, in0=f32A, scalar1=-M23, scalar2=None,
                                      op0=ALU.add))
        v(lambda v_: v_.tensor_scalar(out=d1B, in0=rB, scalar1=12.0, scalar2=14.0,
                                      op0=ALU.max, op1=ALU.min))
        v(lambda v_: v_.tensor_scalar(out=d2B, in0=rB, scalar1=14.0, scalar2=15.0,
                                      op0=ALU.max, op1=ALU.min))
        v(lambda v_: v_.tensor_tensor(out=d1B, in0=d1B, in1=rB, op=ALU.add))
        v(lambda v_: v_.scalar_tensor_tensor(out=d1B, in0=d2B, scalar=3.0, in1=d1B,
                                             op0=ALU.mult, op1=ALU.add))
        v(lambda v_: v_.scalar_tensor_tensor(
            out=magB.rearrange("p (b k) -> p b k", k=16),
            in0=d1B.rearrange("p (b k) -> p b k", k=16), scalar=-62.0, in1=sc3,
            op0=ALU.add, op1=ALU.mult))
        if src_f32:
            shi = src.bitcast(U16).rearrange("p (k two) -> p k two", two=2)[:, :, 1:2] \
                .rearrange("p k o -> p (k o)")
        else:
            shi = src.bitcast(U16)
        if eng != "vector":
            self.w8("vector", self.em.tok(eng))
        self.v(lambda v_: v_.tensor_scalar(out=dst.bitcast(U16), in0=shi,
                                           scalar1=0x8000, scalar2=None,
                                           op0=ALU.bitwise_and))
        self.v(lambda v_: v_.tensor_tensor(out=dst.bitcast(U16),
                                           in0=dst.bitcast(U16),
                                           in1=magB.bitcast(U16),
                                           op=ALU.bitwise_or), inc=True)
        if eng != "vector":
            self.em.wait(eng, self.em.tok("vector"))

    def _quant_pool_body(self, src, dst, F, QC, ST, c0, nb, v, src_f32=True):
        f32A = ST[:, c0:c0 + 2 * F].bitcast(F32)
        f32B = ST[:, c0 + 2 * F:c0 + 4 * F].bitcast(F32)
        rB = ST[:, c0 + 2 * F:c0 + 3 * F]
        d1B = ST[:, c0 + 3 * F:c0 + 4 * F]
        d2B = ST[:, c0:c0 + F]
        magB = ST[:, c0 + F:c0 + 2 * F]
        inv3 = QC[:, 176:176 + nb].rearrange("p (b o) -> p b o", o=1) \
            .broadcast_to((128, nb, 16))
        sc3 = QC[:, 352:352 + nb].rearrange("p (b o) -> p b o", o=1) \
            .broadcast_to((128, nb, 16))
        M23 = 8388608.0
        v(lambda v_: v_.tensor_scalar(out=f32B, in0=src, scalar1=0.0, scalar2=2.0,
                                      op0=ALU.is_ge, op1=ALU.mult))
        v(lambda v_: v_.tensor_scalar(out=f32B, in0=f32B, scalar1=-1.0,
                                      scalar2=None, op0=ALU.add))
        v(lambda v_: v_.tensor_tensor(out=f32A, in0=f32B, in1=src, op=ALU.mult))
        v(lambda v_: v_.tensor_tensor(
            out=f32A.rearrange("p (b k) -> p b k", k=16),
            in0=f32A.rearrange("p (b k) -> p b k", k=16), in1=inv3, op=ALU.mult))
        v(lambda v_: v_.tensor_scalar(out=f32B, in0=f32A, scalar1=8.0, scalar2=12.0,
                                      op0=ALU.max, op1=ALU.min))
        v(lambda v_: v_.tensor_scalar(out=f32B, in0=f32B, scalar1=-0.75,
                                      scalar2=None, op0=ALU.mult))
        v(lambda v_: v_.tensor_tensor(out=f32B, in0=f32B, in1=f32A, op=ALU.add))
        v(lambda v_: v_.tensor_scalar(out=f32A, in0=f32A, scalar1=4.0, scalar2=8.0,
                                      op0=ALU.max, op1=ALU.min))
        v(lambda v_: v_.tensor_scalar(out=f32A, in0=f32A, scalar1=-0.5,
                                      scalar2=16.0, op0=ALU.mult, op1=ALU.add))
        v(lambda v_: v_.tensor_tensor(out=f32A, in0=f32A, in1=f32B, op=ALU.add),
          inc=True)
        self.w8("vector", self.em.tok("gpsimd"))
        self.v(lambda v_: v_.tensor_scalar(out=f32B, in0=f32A, scalar1=M23,
                                           scalar2=None, op0=ALU.add))
        self.v(lambda v_: v_.tensor_scalar(out=magB, in0=f32B, scalar1=-M23,
                                           scalar2=None, op0=ALU.add), inc=True)
        self.em.wait("gpsimd", self.em.tok("vector"))
        v(lambda v_: v_.tensor_scalar(out=d1B, in0=magB, scalar1=12.0, scalar2=14.0,
                                      op0=ALU.max, op1=ALU.min))
        v(lambda v_: v_.tensor_tensor(out=d1B, in0=d1B, in1=magB, op=ALU.add))
        v(lambda v_: v_.tensor_scalar(out=d2B, in0=magB, scalar1=14.0, scalar2=15.0,
                                      op0=ALU.max, op1=ALU.min))
        v(lambda v_: v_.tensor_scalar(out=d2B, in0=d2B, scalar1=3.0, scalar2=-62.0,
                                      op0=ALU.mult, op1=ALU.add))
        v(lambda v_: v_.tensor_tensor(out=d1B, in0=d1B, in1=d2B, op=ALU.add))
        v(lambda v_: v_.tensor_tensor(
            out=magB.rearrange("p (b k) -> p b k", k=16),
            in0=d1B.rearrange("p (b k) -> p b k", k=16), in1=sc3, op=ALU.mult),
          inc=True)
        if src_f32:
            shi = src.bitcast(U16).rearrange("p (k two) -> p k two", two=2) \
                [:, :, 1:2].rearrange("p k o -> p (k o)")
        else:
            shi = src.bitcast(U16)
        self.w8("vector", self.em.tok("gpsimd"))
        self.v(lambda v_: v_.tensor_scalar(out=dst.bitcast(U16), in0=shi,
                                           scalar1=0x8000, scalar2=None,
                                           op0=ALU.bitwise_and))
        self.v(lambda v_: v_.tensor_tensor(out=dst.bitcast(U16),
                                           in0=dst.bitcast(U16),
                                           in1=magB.bitcast(U16),
                                           op=ALU.bitwise_or), inc=True)
        self.em.wait("gpsimd", self.em.tok("vector"))

    def prep_overlay_init(self):
        tasks = []
        for nm_src, nm_dst in (("wg", "wgT"), ("wu", "wuT")):
            src = self.w[nm_src].rearrange("(c p) d -> c p d", p=128)
            dst = self.scr[nm_dst].rearrange("(c p) i -> p c i", p=128)
            for r in range(NI):
                tasks.append(dict(src=src[r], F=1024, cnt=8,
                                  dst=dst[:, :, r * 128:(r + 1) * 128]))
        srcd = self.w["wd"].rearrange("(c p) i -> c p i", p=128)
        dstd = self.scr["wdT"].rearrange("(c p) d -> p c d", p=128)
        for r in range(8):
            for lo, F in ((0, 1024), (1024, 1024), (2048, 768)):
                tasks.append(dict(
                    src=srcd[r][:, lo:lo + F], F=F, cnt=F // 128,
                    dst=dstd[:, lo // 128:(lo + F) // 128, r * 128:(r + 1) * 128]))
        self._ptasks = tasks
        self._ptok = {}
        self._pi = 0
        self._iv_queue = []
        self._iv_on = True
        self._iv_k = 1
        self._pring = [self.BC[:, 16384:17408], self.BB[:, 15360:16384],
                       self.MED2[:, 2048:3072]]
        self._prep_S1(0)
        self._prep_S1(1)
        self.bg_fill(0)

    def _pstage(self, i):
        return self.BC[:, 10240 + (i % 2) * 2048:10240 + (i % 2) * 2048 + 2048] \
            .bitcast(F32)

    def _pqdst(self, i, F):
        return self._pring[i % 3][:, 0:F]

    def _prep_S1(self, i):
        t = self._ptasks[i]
        F = t["F"]
        slot = self._pstage(i)[:, 0:F]
        prev = self._ptok.get(("o", i - 2), [])
        h = F // 2
        ta = self.dq(f"fstg{i % 2}a", "gpsimd").start(
            slot[:, 0:h], t["src"][:, 0:h], wait=prev)
        tb = self.dq(f"fstg{i % 2}b", "gpsimd").start(
            slot[:, h:F], t["src"][:, h:F], wait=prev)
        self._ptok[("l", i)] = [ta, tb]

    def _ptrp(self, i, F):
        stg0 = 10240 + (i % 2) * 2048
        return self.BC[:, stg0:stg0 + F]

    def _prep_batch(self, i):
        t = self._ptasks[i]
        F, cnt = t["F"], t["cnt"]
        qdst = self._pqdst(i, F)
        trp = self._ptrp(i, F)
        w = [self._ptok[("qv", i)], *self._ptok.get(("o", i - 2), [])]
        hc = (cnt + 1) // 2
        ta = self.dq("bgt0").start_t_batch(
            [(trp[:, c * 128:(c + 1) * 128], qdst[:, c * 128:(c + 1) * 128])
             for c in range(hc)], wait=w)
        tb = self.dq("bgt1").start_t_batch(
            [(trp[:, c * 128:(c + 1) * 128], qdst[:, c * 128:(c + 1) * 128])
             for c in range(hc, cnt)], wait=w)
        self._ptok[("t", i)] = [ta, tb]

    def _prep_store(self, i):
        t = self._ptasks[i]
        cnt = t["cnt"]
        trp3 = self._ptrp(i, t["F"]).rearrange("p (c q) -> p c q", q=128)
        tt = self._ptok[("t", i)]
        hc = (cnt + 1) // 2
        oa = self.dq("prepout0", "gpsimd").start(
            t["dst"][:, 0:hc, :], trp3[:, 0:hc], wait=[tt[0]])
        ob = self.dq("prepout1", "gpsimd").start(
            t["dst"][:, hc:cnt, :], trp3[:, hc:cnt], wait=[tt[1]])
        self._ptok[("o", i)] = [oa, ob]

    def _enqueue_task(self, i):
        q = self._iv_queue
        n = len(self._ptasks)
        q.extend(self._quant_ov_ops(i))
        q.append(lambda i=i: self._prep_batch(i))
        if i - 1 >= 0:
            q.append(lambda i=i: self._prep_store(i - 1))
        if 2 <= i + 1 < n:
            q.append(lambda i=i: self._prep_S1(i + 1))

    def bg_fill(self, k):
        if not hasattr(self, "_ptasks"):
            return
        n = len(self._ptasks)
        while len(self._iv_queue) < 48 and self._pi < n:
            self._enqueue_task(self._pi)
            self._pi += 1
        self._iv_on = False
        for _ in range(min(k, len(self._iv_queue))):
            self._iv_queue.pop(0)()
        self._iv_on = True

    def prep_drain(self):
        n = len(self._ptasks)
        self._iv_on = False
        while self._pi < n:
            self._enqueue_task(self._pi)
            self._pi += 1
        while self._iv_queue:
            self._iv_queue.pop(0)()
        self._prep_store(n - 1)
        for tk in self._ptok[("o", n - 1)]:
            self.w8("vector", tk)
        self.tok_ffnprep = [
            (self.dq("prepout0", "gpsimd"), self.dq("prepout0", "gpsimd").cnt),
            (self.dq("prepout1", "gpsimd"), self.dq("prepout1", "gpsimd").cnt)]

    def _quant_ov_ops(self, i):
        t = self._ptasks[i]
        F = t["F"]
        nb = F // 16
        QC = self.QCOL
        M23 = 8388608.0
        stg0 = 10240 + (i % 2) * 2048
        srcf = self._pstage(i)[:, 0:F]
        dst = self._pqdst(i, F)
        f32A = self.BC[:, 14336:14336 + 2 * F].bitcast(F32)
        f32B = srcf
        rB = self.BC[:, stg0:stg0 + F]
        d1B = self.BC[:, stg0 + F:stg0 + 2 * F]
        d2B = self.BC[:, 14336:14336 + F]
        magB = self.BC[:, 14336 + F:14336 + 2 * F]
        src3 = srcf.rearrange("p (b k) -> p b k", k=16)
        f32A3 = f32A.rearrange("p (b k) -> p b k", k=16)
        inv3 = QC[:, 176:176 + nb].rearrange("p (b o) -> p b o", o=1) \
            .broadcast_to((128, nb, 16))
        sc3 = QC[:, 352:352 + nb].rearrange("p (b o) -> p b o", o=1) \
            .broadcast_to((128, nb, 16))
        shi = srcf.bitcast(U16).rearrange("p (k two) -> p k two", two=2)[:, :, 1:2] \
            .rearrange("p k o -> p (k o)")
        U32 = mybir.dt.uint32
        ops = []

        def first():
            for tk in self._ptok.get(("l", i), []):
                self.w8("vector", tk)
            self.v(lambda v: v.tensor_reduce(
                out=QC[:, 0:nb].rearrange("p (b o) -> p b o", o=1), in_=src3,
                axis=AX.X, op=ALU.max, apply_absolute_value=True))
        ops.append(first)
        ops.append(lambda: self.v(lambda v: v.tensor_scalar(
            out=QC[:, 0:nb], in0=QC[:, 0:nb], scalar1=6e-12, scalar2=None,
            op0=ALU.max)))
        ops.append(lambda: self.v(lambda v: v.reciprocal(
            out=QC[:, 176:176 + nb], in_=QC[:, 0:nb])))
        ops.append(lambda: self.v(lambda v: v.tensor_scalar(
            out=QC[:, 176:176 + nb], in0=QC[:, 176:176 + nb], scalar1=12.0,
            scalar2=None, op0=ALU.mult)))
        ops.append(lambda: self.v(lambda v: v.tensor_scalar(
            out=QC[:, 352:352 + nb], in0=QC[:, 0:nb], scalar1=1.0 / 12.0,
            scalar2=None, op0=ALU.mult)))

        def sign():
            for tk in self._ptok.get(("t", i - 3), []):
                self.w8("vector", tk)
            self.v(lambda v: v.tensor_scalar(
                out=dst.bitcast(U16), in0=shi, scalar1=0x8000, scalar2=None,
                op0=ALU.bitwise_and))
        ops.append(sign)
        ops.append(lambda: self.v(lambda v: v.tensor_scalar(
            out=f32A.bitcast(U32), in0=srcf.bitcast(U32), scalar1=0x7FFFFFFF,
            scalar2=None, op0=ALU.bitwise_and)))
        ops.append(lambda: self.v(lambda v: v.tensor_tensor(
            out=f32A3, in0=f32A3, in1=inv3, op=ALU.mult)))
        ops.append(lambda: self.v(lambda v: v.tensor_scalar(
            out=f32B, in0=f32A, scalar1=8.0, scalar2=12.0, op0=ALU.max,
            op1=ALU.min)))
        ops.append(lambda: self.v(lambda v: v.scalar_tensor_tensor(
            out=f32B, in0=f32B, scalar=-0.75, in1=f32A, op0=ALU.mult,
            op1=ALU.add)))
        ops.append(lambda: self.v(lambda v: v.tensor_scalar(
            out=f32A, in0=f32A, scalar1=4.0, scalar2=8.0, op0=ALU.max,
            op1=ALU.min)))
        ops.append(lambda: self.v(lambda v: v.scalar_tensor_tensor(
            out=f32B, in0=f32A, scalar=-0.5, in1=f32B, op0=ALU.mult,
            op1=ALU.add)))
        ops.append(lambda: self.v(lambda v: v.tensor_scalar(
            out=f32A, in0=f32B, scalar1=16.0 + M23, scalar2=None, op0=ALU.add)))
        ops.append(lambda: self.v(lambda v: v.tensor_scalar(
            out=rB, in0=f32A, scalar1=-M23, scalar2=None, op0=ALU.add)))
        ops.append(lambda: self.v(lambda v: v.tensor_scalar(
            out=d1B, in0=rB, scalar1=12.0, scalar2=14.0, op0=ALU.max,
            op1=ALU.min)))
        ops.append(lambda: self.v(lambda v: v.tensor_scalar(
            out=d2B, in0=rB, scalar1=14.0, scalar2=15.0, op0=ALU.max,
            op1=ALU.min)))
        ops.append(lambda: self.v(lambda v: v.tensor_tensor(
            out=d1B, in0=d1B, in1=rB, op=ALU.add)))
        ops.append(lambda: self.v(lambda v: v.scalar_tensor_tensor(
            out=d1B, in0=d2B, scalar=3.0, in1=d1B, op0=ALU.mult, op1=ALU.add)))
        ops.append(lambda: self.v(lambda v: v.scalar_tensor_tensor(
            out=magB.rearrange("p (b k) -> p b k", k=16),
            in0=d1B.rearrange("p (b k) -> p b k", k=16), scalar=-62.0, in1=sc3,
            op0=ALU.add, op1=ALU.mult)))

        def last():
            self.v(lambda v: v.tensor_tensor(
                out=dst.bitcast(U16), in0=dst.bitcast(U16),
                in1=magB.bitcast(U16), op=ALU.bitwise_or), inc=True)
            self._ptok[("qv", i)] = self.em.tok("vector")
        ops.append(last)
        return ops

    def attn_b(self):
        em, CST, X, STAT = self.em, self.CST, self.X, self.STAT
        BA, BB, BC, MED, MED2 = self.BA, self.BB, self.BC, self.MED, self.MED2
        qkv_r = self.scr["qkv16"].rearrange("(c p) e -> p c e", p=128)
        wo_r64 = self.scr["wo16"].rearrange("(c p) d -> c p d", p=64)
        aob_r = self.scr["aob"].rearrange("(n p) d -> p n d", p=128)

        def qro(n):
            return BA[:, n * 512:(n + 1) * 512]

        def kro(n):
            return BB[:, n * 512:(n + 1) * 512]

        def vpk(kt, hh=None):
            base = 8192 + (kt // 2) * 520 + (hh * 65 if hh is not None else 0)
            buf = BB if kt % 2 == 0 else BA
            return buf[:, base:base + (65 if hh is not None else 520)]

        def aoT(hh):
            return BC[0:64, hh * 2048:(hh + 1) * 2048]
        qTh = MED2[:, 0:2048]
        kTh = MED2[:, 2048:4096]

        def pT(b):
            return MED[:, b * 1024:(b + 1) * 1024]
        xn_scr = BC[:, 16384:17408]
        ZB = BC[0:64, 16384:17408]
        ZR = BB[:, 14336:16384].bitcast(F32)[0:1, :]
        ZRB = BB[0:1, 13312:14336]
        self._aob_st = {}
        self._pend_epi = None
        self._pend_epi2 = None
        self._last_pv = None
        self._tok_repl = None
        self._tok_z = None
        for hp in range(2):
            self.w8("vector", self._aob_st.get(0))
            self.w8("vector", self._aob_st.get(1))
            for n in range(NT):
                self.w8("vector", self.em.tok("tensor"))
                tok_xn = self.ln_apply(n, xn_scr)
                self.w8("tensor", tok_xn)
                dst = BC[:, 0:16384].rearrange("p (c q) -> p c q", q=2048)[:, :, n * 128:(n + 1) * 128]
                self.transpose8([xn_scr[:, c * 128:(c + 1) * 128] for c in range(8)],
                                n % 4, dst, "vector")
            tok_xnT = self.em.tok("vector")

            def xnT(c):
                return BC[:, c * 2048:(c + 1) * 2048]
            for part in range(3):
                base_col = part * 1024 + hp * 512
                wsl = MED[:, 0:4096]
                _qg = [self.tok_cast["qkv16"], self.em.tok("tensor")]
                tok_w = self.dq("qkv").start(
                    wsl[:, 0:2048].rearrange("p (c q) -> p c q", q=512),
                    qkv_r[:, 0:4, base_col:base_col + 512], wait=_qg)
                tok_w2 = self.dq("qkv2").start(
                    wsl[:, 2048:4096].rearrange("p (c q) -> p c q", q=512),
                    qkv_r[:, 4:8, base_col:base_col + 512], wait=_qg)
                self.w8("tensor", tok_w)
                self.w8("tensor", tok_w2)
                self.w8("tensor", tok_xnT)
                for n in range(NT):
                    bk = n % 2
                    self.use_bank(bk)
                    for kc in range(8):
                        self.t(lambda t, bk=bk, kc=kc, n=n, wsl=wsl: t.matmul(
                            self.bankf(bk), xnT(kc)[:, n * 128:(n + 1) * 128],
                            wsl[:, kc * 512:(kc + 1) * 512],
                            start=(kc == 0), stop=(kc == 7)), inc=(kc == 7))
                    self.wrote_bank(bk)
                    self.read_bank(bk, "vector")
                    if part < 2:
                        dstt = qro(n) if part == 0 else kro(n)
                        ps4 = self.bankf(bk).rearrange("p (h s q) -> p h s q",
                                                       s=2, q=32)
                        cos4 = CST["ropec"][:, n * 32:(n + 1) * 32] \
                            .unsqueeze(1).unsqueeze(2).broadcast_to((128, 8, 2, 32))
                        sin3 = CST["ropes"][:, n * 32:(n + 1) * 32] \
                            .unsqueeze(1).broadcast_to((128, 8, 32))
                        t1 = BB[:, 14336:14848]
                        t2 = BB[:, 14848:15360]
                        t1v = t1.rearrange("p (h s q) -> p h s q", s=2, q=32)
                        t2v = t2.rearrange("p (h s q) -> p h s q", s=2, q=32)
                        dsv = dstt.rearrange("p (h s q) -> p h s q", s=2, q=32)
                        self.v(lambda v, t1v=t1v, ps4=ps4, cos4=cos4: v.tensor_tensor(
                            out=t1v, in0=ps4, in1=cos4, op=ALU.mult))
                        self.v(lambda v, t2v=t2v, ps4=ps4, sin3=sin3: v.tensor_tensor(
                            out=t2v[:, :, 0, :], in0=ps4[:, :, 1, :], in1=sin3,
                            op=ALU.mult))
                        self.v(lambda v, t2v=t2v, ps4=ps4, sin3=sin3: v.tensor_tensor(
                            out=t2v[:, :, 1, :], in0=ps4[:, :, 0, :], in1=sin3,
                            op=ALU.mult))
                        self.v(lambda v, dsv=dsv, t1v=t1v, t2v=t2v:
                               v.scalar_tensor_tensor(
                                   out=dsv[:, :, 0, :], in0=t2v[:, :, 0, :],
                                   scalar=-1.0, in1=t1v[:, :, 0, :],
                                   op0=ALU.mult, op1=ALU.add))
                        self.v(lambda v, dsv=dsv, t1v=t1v, t2v=t2v: v.tensor_tensor(
                            out=dsv[:, :, 1, :], in0=t1v[:, :, 1, :],
                            in1=t2v[:, :, 1, :], op=ALU.add), inc=True)
                    else:
                        self.v(lambda v, kt=n, bk=bk: v.tensor_copy(
                            out=vpk(kt).rearrange("p (h q) -> p h q", q=65)[:, :, 0:64],
                            in_=self.bankf(bk).rearrange("p (h q) -> p h q", q=64)))
                        self.v(lambda v, kt=n: v.memset(
                            vpk(kt).rearrange("p (h q) -> p h q", q=65)[:, :, 64:65],
                            1.0), inc=True)
                    self.bank_read_done(bk, "vector")
            tok_proj = self.em.tok("vector")
            for hh in range(8):
                for si, (srcf, dstf) in enumerate(((qro, qTh), (kro, kTh))):
                    self.w8("tensor", tok_proj)
                    for bi in range(2):
                        self.transpose8(
                            [srcf(nn)[:, hh * 64:(hh + 1) * 64]
                             for nn in range(bi * 8, bi * 8 + 8)],
                            2 + si * 2 + bi,
                            dstf[0:64, bi * 1024:(bi + 1) * 1024],
                            "vector", prow=64)
                tok_qkT = self.em.tok("vector")
                for qtc in range(2):
                    exp_toks = {}
                    pv_toks = {}
                    prev_pv = self._last_pv

                    def pv_mms(kt):
                        ptb = pT(kt % 3)
                        self.w8("tensor", exp_toks[kt])
                        for sub in range(2):
                            self.t(lambda t, kt=kt, sub=sub, ptb=ptb, hh=hh: t.matmul(
                                self.bankf(6 + sub)[0:65, :],
                                vpk(kt, hh),
                                ptb[:, sub * 512:(sub + 1) * 512],
                                start=(kt == 0), stop=(kt == NT - 1)),
                                inc=(sub == 1))
                        pv_toks[kt] = self.em.tok("tensor")
                    for mt in range(0, NT, 2):
                        for kt in (mt, mt + 1):
                            sb = 2 if kt % 2 == 0 else 4
                            self.use_bank(sb)
                            self.use_bank(sb + 1)
                            self.w8("tensor", tok_qkT)
                            for sub in range(2):
                                self.t(lambda t, sb=sb, sub=sub, kt=kt, qtc=qtc:
                                       t.matmul(
                                           self.bankf(sb + sub),
                                           kTh[0:64, kt * 128:(kt + 1) * 128],
                                           qTh[0:64, qtc * 1024 + sub * 512:
                                               qtc * 1024 + sub * 512 + 512],
                                           start=True, stop=True), inc=(sub == 1))
                            self.wrote_bank(sb)
                            self.wrote_bank(sb + 1)
                        if mt == 2:
                            if self._pend_epi is not None:
                                self._pend_epi()
                                self._pend_epi = None
                            self.use_bank(6)
                            self.use_bank(7)
                        if mt == 6 and self._pend_epi2 is not None:
                            self._pend_epi2()
                            self._pend_epi2 = None
                        if mt >= 2:
                            pv_mms(mt - 2)
                            pv_mms(mt - 1)
                        for kt in (mt, mt + 1):
                            sb = 2 if kt % 2 == 0 else 4
                            ptb = pT(kt % 3)
                            self.read_bank(sb, "scalar")
                            self.read_bank(sb + 1, "scalar")
                            if kt >= 3:
                                self.w8("scalar", pv_toks[kt - 3])
                            elif prev_pv is not None:
                                self.w8("scalar", prev_pv[{0: 15, 1: 13, 2: 14}[kt]])
                            self.s(lambda s, sb=sb, ptb=ptb: s.activation(
                                out=ptb, in_=self.PS[:, sb * 512:(sb + 2) * 512],
                                func=AF.Exp, scale=0.125), inc=True)
                            self.bank_read_done(sb, "scalar")
                            self.bank_read_done(sb + 1, "scalar")
                            exp_toks[kt] = self.em.tok("scalar")
                    pv_mms(NT - 2)
                    pv_mms(NT - 1)
                    self.wrote_bank(6)
                    self.wrote_bank(7)
                    self._last_pv = pv_toks

                    def epi(hh=hh, qtc=qtc):
                        pvb = BA[0:64, 12352:13376]
                        self.read_bank(6, "vector")
                        self.read_bank(7, "vector")
                        self.v(lambda v, pvb=pvb: v.tensor_copy(
                            out=pvb, in_=self.PS[0:64, 3072:4096]))
                        self.v(lambda v: v.tensor_copy(
                            out=ZR, in_=self.PS[64:65, 3072:4096]), inc=True)
                        self.bank_read_done(6, "vector")
                        self.bank_read_done(7, "vector")
                        self.w8("vector", self._tok_repl)
                        self.v(lambda v: v.reciprocal(out=ZR, in_=ZR))
                        self.v(lambda v: v.tensor_copy(out=ZRB, in_=ZR),
                               inc=True)
                        self._tok_z = self.em.tok("vector")

                        def epi2(hh=hh, qtc=qtc, pvb=pvb):
                            self.use_bank(0)
                            self.use_bank(1)
                            self.w8("tensor", self._tok_z)
                            for sub in range(2):
                                self.t(lambda t, sub=sub: t.matmul(
                                    self.bankf(0 + sub)[0:64, :],
                                    CST["ones_col"][:],
                                    ZRB[:, sub * 512:(sub + 1) * 512],
                                    start=True, stop=True), inc=(sub == 1))
                            self._tok_repl = self.em.tok("tensor")
                            self.wrote_bank(0)
                            self.wrote_bank(1)
                            self.read_bank(0, "vector")
                            self.read_bank(1, "vector")
                            self.v(lambda v: v.tensor_copy(
                                out=ZB, in_=self.PS[0:64, 0:1024]), inc=True)
                            self.bank_read_done(0, "vector")
                            self.bank_read_done(1, "vector")
                            self.v(lambda v, hh=hh, qtc=qtc, pvb=pvb:
                                   v.tensor_tensor(
                                       out=aoT(hh)[:, qtc * 1024:(qtc + 1) * 1024],
                                       in0=pvb, in1=ZB,
                                       op=ALU.mult), inc=True)
                        self._pend_epi2 = epi2
                    self._pend_epi = epi
            if self._pend_epi is not None:
                self._pend_epi()
                self._pend_epi = None
            if self._pend_epi2 is not None:
                self._pend_epi2()
                self._pend_epi2 = None
            wo_5d = self.scr["wo16"].rearrange(
                "(a c p) (hf q) -> a p c hf q", p=64, c=8, q=512)
            wosl = MED[:, 4096:8192]
            tok_ao = self.em.tok("vector")
            for half in range(2):
                tok_wo = self.dq("wo").start(
                    wosl[0:64, :].rearrange("p (c q) -> p c q", q=512),
                    wo_5d[hp, :, :, half],
                    wait=[self.tok_cast["wo16"], self.em.tok("tensor")])
                self.w8("tensor", tok_wo)
                self.w8("tensor", tok_ao)
                for n in range(NT):
                    bk = n % 2
                    self.use_bank(bk)
                    for hh in range(8):
                        self.t(lambda t, bk=bk, hh=hh, n=n: t.matmul(
                            self.bankf(bk),
                            aoT(hh)[:, n * 128:(n + 1) * 128],
                            wosl[0:64, hh * 512:(hh + 1) * 512],
                            start=(hh == 0), stop=(hh == 7)), inc=(hh == 7))
                    self.wrote_bank(bk)
                    self.read_bank(bk, "vector")
                    if hp == 0:
                        stage = BC[:, 16384 + bk * 512:16384 + bk * 512 + 512]
                        self.w8("vector", self._aob_st.get(bk))
                        self.v(lambda v, bk=bk, stage=stage: v.tensor_copy(
                            out=stage, in_=self.bankf(bk)), inc=True)
                        self.bank_read_done(bk, "vector")
                        self._aob_st[bk] = self.dq(f"aos{bk}").start(
                            aob_r[:, n, half * 512:half * 512 + 512], stage,
                            wait=[self.em.tok("vector")])
                    else:
                        lslot = BC[:, 16384 + bk * 512:16384 + bk * 512 + 512]
                        ltok = self.dq(f"aol{bk}").start(
                            lslot, aob_r[:, n, half * 512:half * 512 + 512],
                            wait=[self.em.tok("vector"), self._aob_st.get(0),
                                  self._aob_st.get(1)])
                        self.w8("vector", ltok)
                        self.resid_update(n, half, self.bankf(bk), extra=lslot)
                        self.bank_read_done(bk, "vector")


def kernel(L0_attn_a_out, L0_attn_a_W_write, L0_attn_a_W_read, L0_attn_a_W_beta,
           L0_attn_a_decay, L0_attn_b_Wqkv, L0_attn_b_Wo, L0_ffn_w_gate,
           L0_ffn_w_up, L0_ffn_w_down, _parts=4):
    consts = host_constants()
    nc = build_program(parts=_parts)
    decay_t = np.ascontiguousarray(
        np.tile(np.asarray(L0_attn_a_decay, dtype=np.float32)[None, :], (128, 1)))
    shared = dict(
        Ww=np.ascontiguousarray(L0_attn_a_W_write, dtype=np.float32),
        Wr=np.ascontiguousarray(L0_attn_a_W_read, dtype=np.float32),
        Wb=np.ascontiguousarray(L0_attn_a_W_beta, dtype=np.float32),
        decay_t=decay_t,
        Wqkv=np.ascontiguousarray(L0_attn_b_Wqkv, dtype=np.float32),
        Wo=np.ascontiguousarray(L0_attn_b_Wo, dtype=np.float32),
        wg=np.ascontiguousarray(L0_ffn_w_gate, dtype=np.float32),
        wu=np.ascontiguousarray(L0_ffn_w_up, dtype=np.float32),
        wd=np.ascontiguousarray(L0_ffn_w_down, dtype=np.float32),
        **consts,
    )
    xs = np.ascontiguousarray(L0_attn_a_out, dtype=np.float32)
    in_maps = [dict(x=xs[i], **shared) for i in range(8)]
    res = run_bass_kernel_spmd(nc, in_maps, core_ids=list(range(8)))
    return np.stack([res.results[i]["y"] for i in range(8)], axis=0)


# version 18
# speedup vs baseline: 1.1879x; 1.1879x over previous
import numpy as np
import ml_dtypes
import concourse.bass as bass
import concourse.mybir as mybir
from concourse.bass_utils import run_bass_kernel_spmd
from contextlib import ExitStack

F32 = mybir.dt.float32
BF16 = mybir.dt.bfloat16
I16 = mybir.dt.int16
U16 = mybir.dt.uint16
AF = mybir.ActivationFunctionType
ALU = mybir.AluOpType
AX = mybir.AxisListType

T = 2048
D = 1024
NT = 16
H = 8
HB = 16
DHB = 64
II = 2816
NI = 22
LN_EPS = 1e-5
ENGINES = ("tensor", "vector", "scalar", "gpsimd", "sync")
BFNP = np.dtype(ml_dtypes.bfloat16)


def host_constants():
    c = {}
    i_idx = np.arange(128, dtype=np.float32)[:, None]
    j_idx = np.arange(128, dtype=np.float32)[None, :]
    c["eye128"] = np.eye(128, dtype=np.float32).astype(BFNP)
    c["eye8"] = np.tile(np.eye(128, dtype=np.float32), (1, 8)).astype(BFNP)
    c["iota_dif"] = np.ascontiguousarray(np.broadcast_to(j_idx - i_idx, (128, 128)), dtype=np.float32)
    c["iota_colp1"] = np.ascontiguousarray(i_idx + 1.0, dtype=np.float32)
    c["iota_colrev"] = np.ascontiguousarray(127.0 - i_idx, dtype=np.float32)
    c["m_up_s_neg"] = np.ascontiguousarray(-(j_idx > i_idx).astype(np.float32))
    c["m_low_s_neg"] = np.ascontiguousarray(-(j_idx < i_idx).astype(np.float32))
    c["m_up_i"] = np.ascontiguousarray((j_idx >= i_idx).astype(np.float32))
    c["ones_col"] = np.ones((1, 64), dtype=np.float32).astype(BFNP)
    inv_freq = 1.0 / 10000.0 ** (np.arange(0, DHB, 2, dtype=np.float32) / DHB)
    t_pos = np.arange(T, dtype=np.float32)
    freqs = np.outer(t_pos, inv_freq)
    cosw = np.cos(freqs)
    sinw = np.sin(freqs)
    HD2 = DHB // 2
    c["ropec"] = np.ascontiguousarray(
        cosw.reshape(NT, 128, HD2).transpose(1, 0, 2).reshape(128, NT * HD2)).astype(BFNP)
    c["ropes"] = np.ascontiguousarray(
        sinw.reshape(NT, 128, HD2).transpose(1, 0, 2).reshape(128, NT * HD2)).astype(BFNP)
    return c


CONST_SPECS = [
    ("eye128", [128, 128], BF16), ("eye8", [128, 1024], BF16),
    ("iota_dif", [128, 128], F32),
    ("iota_colp1", [128, 1], F32), ("iota_colrev", [128, 1], F32),
    ("m_up_s_neg", [128, 128], F32), ("m_low_s_neg", [128, 128], F32),
    ("m_up_i", [128, 128], F32), ("ones_col", [1, 64], BF16),
    ("ropec", [128, NT * DHB // 2], BF16), ("ropes", [128, NT * DHB // 2], BF16),
]


class DmaQ:
    def __init__(self, em, eng, name):
        self.em = em
        self.eng = eng
        self.h = em.ctx.enter_context(em.nc.semaphore(name))
        self.cnt = 0
        self.waited = {e: 0 for e in ENGINES}

    def start(self, out, in_, wait=()):
        for tk in wait:
            self.em.wait(self.eng, tk)
        if self.cnt > 0 and self.waited[self.eng] < self.cnt:
            self.wait(self.eng, (self, self.cnt))
        self.cnt += 16
        h = self.h

        def fn(e, out=out, in_=in_):
            e.dma_start(out=out, in_=in_).then_inc(h, 16)
        self.em.q[self.eng].append(fn)
        self.em.ninstr += 1
        return (self, self.cnt)

    def start_t(self, out, in_, wait=()):
        for tk in wait:
            self.em.wait(self.eng, tk)
        if self.cnt > 0 and self.waited[self.eng] < self.cnt:
            self.wait(self.eng, (self, self.cnt))
        self.cnt += 16
        h = self.h

        def fn(e, out=out, in_=in_):
            e.dma_start_transpose(out=out, in_=in_).then_inc(h, 16)
        self.em.q[self.eng].append(fn)
        self.em.ninstr += 1
        return (self, self.cnt)

    def start_t_batch(self, pairs, wait=()):
        for tk in wait:
            self.em.wait(self.eng, tk)
        if self.cnt > 0 and self.waited[self.eng] < self.cnt:
            self.wait(self.eng, (self, self.cnt))
        h = self.h
        for out, in_ in pairs:
            self.cnt += 16

            def fn(e, out=out, in_=in_):
                e.dma_start_transpose(out=out, in_=in_).then_inc(h, 16)
            self.em.q[self.eng].append(fn)
            self.em.ninstr += 1
        return (self, self.cnt)

    def wait(self, eng, token):
        q, val = token
        assert q is self
        if val <= self.waited[eng]:
            return
        self.waited[eng] = val
        self.em.q[eng].append(lambda e, h=self.h, val=val: e.wait_ge(h, val))


class Em:

    def __init__(self, nc, ctx):
        self.nc = nc
        self.ctx = ctx
        self.q = {e: [] for e in ENGINES}
        self.sem = {e: ctx.enter_context(nc.semaphore(f"c_{e}")) for e in ENGINES}
        self.cnt = {e: 0 for e in ENGINES}
        self.waited = {e: {} for e in ENGINES}
        self.ninstr = 0

    AUTOSYNC = ("vector", "scalar", "gpsimd")

    def emit(self, eng, fn, inc=False):
        self.ninstr += 1
        if eng in self.AUTOSYNC:
            prev = self.cnt[eng]
            sem = self.sem[eng]
            if prev > self.waited[eng].get(eng, 0):
                self.waited[eng][eng] = prev
                self.q[eng].append(lambda e, sem=sem, prev=prev: e.wait_ge(sem, prev))
                self.ninstr += 1
            inc = True
        if inc:
            self.cnt[eng] += 1
            sem = self.sem[eng]

            def wrapped(e, fn=fn, sem=sem):
                fn(e).then_inc(sem, 1)
            self.q[eng].append(wrapped)
        else:
            self.q[eng].append(fn)
        return (eng, self.cnt[eng])

    def tok(self, eng):
        return (eng, self.cnt[eng])

    def wait(self, eng, token):
        if token is None:
            return
        src, val = token
        if isinstance(src, DmaQ):
            src.wait(eng, token)
            return
        if val <= 0:
            return
        if val <= self.waited[eng].get(src, 0):
            return
        self.waited[eng][src] = val
        sem = self.sem[src]
        self.q[eng].append(lambda e, sem=sem, val=val: e.wait_ge(sem, val))

    def bump(self, eng):
        self.cnt[eng] += 1
        sem = self.sem[eng]
        self.q[eng].append(lambda e, sem=sem: e.drain().then_inc(sem, 1))
        self.ninstr += 1
        return (eng, self.cnt[eng])

    def checkpoint(self):
        toks = [self.bump(e) for e in ENGINES]
        for e in ENGINES:
            for tk in toks:
                self.wait(e, tk)

    def play(self):
        with self.nc.Block() as blk:
            for eng in ENGINES:
                ops = self.q[eng]

                def body(e, ops=ops):
                    for fn in ops:
                        fn(e)
                getattr(blk, eng)(body)


def build_program(parts=4):
    nc = bass.Bass("TRN2", target_bir_lowering=False)
    ctx = ExitStack()

    x_in = nc.dram_tensor("x", [T, D], F32, kind="ExternalInput").ap()
    w = {
        "Ww": nc.dram_tensor("Ww", [D, D], F32, kind="ExternalInput").ap(),
        "Wr": nc.dram_tensor("Wr", [D, D], F32, kind="ExternalInput").ap(),
        "Wb": nc.dram_tensor("Wb", [H, D], F32, kind="ExternalInput").ap(),
        "decay_t": nc.dram_tensor("decay_t", [128, H], F32, kind="ExternalInput").ap(),
        "Wqkv": nc.dram_tensor("Wqkv", [D, 3 * D], F32, kind="ExternalInput").ap(),
        "Wo": nc.dram_tensor("Wo", [D, D], F32, kind="ExternalInput").ap(),
        "wg": nc.dram_tensor("wg", [II, D], F32, kind="ExternalInput").ap(),
        "wu": nc.dram_tensor("wu", [II, D], F32, kind="ExternalInput").ap(),
        "wd": nc.dram_tensor("wd", [D, II], F32, kind="ExternalInput").ap(),
    }
    consts_d = {}
    for name, shape, dt_ in CONST_SPECS:
        consts_d[name] = nc.dram_tensor(name, shape, dt_, kind="ExternalInput").ap()
    y_out = nc.dram_tensor("y", [T, D], F32, kind="ExternalOutput").ap()
    scr = {
        "wgT": nc.dram_tensor("wgT_scr", [D, II], BF16, kind="Internal").ap(),
        "wuT": nc.dram_tensor("wuT_scr", [D, II], BF16, kind="Internal").ap(),
        "wdT": nc.dram_tensor("wdT_scr", [II, D], BF16, kind="Internal").ap(),
        "wg16": nc.dram_tensor("wg16_scr", [II, D], BF16, kind="Internal").ap(),
        "wu16": nc.dram_tensor("wu16_scr", [II, D], BF16, kind="Internal").ap(),
        "wd16": nc.dram_tensor("wd16_scr", [D, II], BF16, kind="Internal").ap(),
        "qkv16": nc.dram_tensor("qkv16_scr", [D, 3 * D], BF16, kind="Internal").ap(),
        "wo16": nc.dram_tensor("wo16_scr", [D, D], BF16, kind="Internal").ap(),
        "aob": nc.dram_tensor("aob_scr", [T, D], BF16, kind="Internal").ap(),
    }

    def sbuf(name, shape, dt_):
        return ctx.enter_context(nc.sbuf_tensor(name, shape, dt_))

    env = dict(nc=nc, x_in=x_in, w=w, consts_d=consts_d, y_out=y_out, scr=scr)
    env["X"] = sbuf("xres", [128, NT * D], F32)
    env["BA"] = sbuf("bigA", [128, 16384], BF16)
    env["BB"] = sbuf("bigB", [128, 16384], BF16)
    env["BC"] = sbuf("bigC", [128, 17408], BF16)
    env["MED"] = sbuf("med", [128, 8192], BF16)
    env["MED2"] = sbuf("med2", [128, 4096], BF16)
    env["STAT"] = sbuf("stat", [128, 96], F32)
    env["COLS"] = sbuf("cols", [128, 128], F32)
    env["QCOL"] = sbuf("qcol", [128, 528], F32)
    env["HC"] = sbuf("hcols", [128, 64], F32)
    env["MASKU"] = sbuf("masku", [128, 1024], BF16)
    env["MASKL"] = sbuf("maskl", [128, 1024], BF16)
    env["MASKA"] = sbuf("maska", [128, 1024], BF16)
    env["DEALL"] = sbuf("deall", [128, 1024], BF16)
    env["CTDE"] = sbuf("ctde", [128, 1024], BF16)
    env["SST"] = sbuf("Sstate", [128, 1024], BF16)
    env["CST"] = {name: sbuf("c_" + name, shape, dt_) for name, shape, dt_ in CONST_SPECS}
    env["PS"] = ctx.enter_context(nc.psum_tensor("PS", [128, 4096], F32))

    em = Em(nc, ctx)
    env["em"] = em

    b = Builder(env)
    b.emit_all(parts)
    em.play()
    ctx.close()
    nc._n_instr = em.ninstr
    return nc


class Builder:
    def __init__(self, env):
        self.__dict__.update(env)
        self.bank_w = {}
        self.bank_r = {}
        self._dqs = {}

    def dq(self, name, eng="sync"):
        if name not in self._dqs:
            self._dqs[name] = DmaQ(self.em, eng, "dq_" + name)
        return self._dqs[name]

    def t(self, fn, inc=False):
        return self.em.emit("tensor", fn, inc)

    def v(self, fn, inc=False):
        ivq = getattr(self, "_iv_queue", None)
        if ivq and self._iv_on:
            self._iv_on = False
            for _ in range(min(self._iv_k, len(ivq))):
                ivq.pop(0)()
            self._iv_on = True
        return self.em.emit("vector", fn, inc)

    def v_crit(self, fn, inc=False):
        return self.em.emit("vector", fn, inc)

    def s(self, fn, inc=False):
        return self.em.emit("scalar", fn, inc)

    def w8(self, eng, tok):
        self.em.wait(eng, tok)

    def bankf(self, i, ncols=512):
        return self.PS[:, 512 * i:512 * i + ncols]

    def bankb(self, i, ncols=1024):
        return self.PS[:].bitcast(BF16)[:, 1024 * i:1024 * i + ncols]

    def use_bank(self, i):
        self.w8("tensor", self.bank_r.get(i))

    def wrote_bank(self, i):
        self.bank_w[i] = self.em.tok("tensor")

    def read_bank(self, i, eng):
        self.w8(eng, self.bank_w.get(i))

    def bank_read_done(self, i, eng):
        self.bank_r[i] = self.em.tok(eng)

    def transpose8(self, srcs, bank, dst, evac_eng="vector", prow=128):
        n = len(srcs)
        self.use_bank(bank)
        for i, sap in enumerate(srcs):
            self.t(lambda t, sap=sap, i=i: t.transpose(
                self.bankb(bank)[0:prow, i * 128:(i + 1) * 128], sap,
                self.CST["eye128"][:]), inc=(i == n - 1))
        self.wrote_bank(bank)
        self.read_bank(bank, evac_eng)
        src_ap = self.bankb(bank)[0:prow, 0:n * 128]
        if evac_eng == "vector":
            self.v(lambda v, dst=dst, src_ap=src_ap: v.tensor_copy(out=dst, in_=src_ap),
                   inc=True)
        else:
            self.s(lambda s, dst=dst, src_ap=src_ap: s.copy(out=dst, in_=src_ap),
                   inc=True)
        self.bank_read_done(bank, evac_eng)
        return self.em.tok(evac_eng)

    def emit_all(self, parts):
        em = self.em
        self.load_x_and_consts()
        if parts >= 0.1:
            self.precast_weights()
        if parts >= 0.2:
            self.attn_a_weight_prep()
        if parts >= 0.3:
            self.decay_consts()
        if parts >= 0.1:
            em.checkpoint()
        if parts >= 0.6:
            self.ln_prologue(first=True)
        if parts >= 1:
            if parts >= 1.5:
                self.prep_overlay_init()
            self.attn_a()
            em.checkpoint()
        if parts == 1.5:
            self.prep_drain()
            em.checkpoint()
        if parts >= 2:
            self.ln_prologue()
            self.ffn(first=True)
            em.checkpoint()
        if parts >= 3:
            self.ln_prologue()
            self.attn_b()
            em.checkpoint()
        if parts >= 4:
            self.ln_prologue()
            self.ffn(first=False)
            em.checkpoint()
        em.checkpoint()
        yr = self.y_out.rearrange("(n p) d -> p n d", p=128)
        qy = self.dq("y")
        tok = qy.start(yr, self.X[:].rearrange("p (n d) -> p n d", d=D),
                       wait=[self.tok_xload])
        qy.wait("sync", tok)

    def load_x_and_consts(self):
        xr = self.x_in.rearrange("(n p) d -> p n d", p=128)
        self.tok_xload = self.dq("x").start(
            self.X[:].rearrange("p (n d) -> p n d", d=D), xr)
        for name, shape, dt_ in CONST_SPECS:
            self.tok_consts = self.dq("const").start(self.CST[name][:],
                                                     self.consts_d[name])

    def precast_weights(self):
        self.tok_cast = {}
        for src, dst in [("Wqkv", "qkv16"), ("Wo", "wo16")]:
            tok = self.dq("pre", "gpsimd").start(self.scr[dst], self.w[src])
        for dst in ("qkv16", "wo16"):
            self.tok_cast[dst] = tok

    def attn_a_weight_prep(self):
        self.tok_WwT = self._transpose_1024sq(self.w["Ww"], self.MED[:, 0:8192])
        self.tok_WrT = self._transpose_1024sq(self.w["Wr"], self.BC[:, 2048:10240])
        tok = self.dq("wb", "gpsimd").start(self.MED2[0:H, 0:D], self.w["Wb"],
                                            wait=[self.em.tok("tensor")])
        self.w8("tensor", tok)
        self.w8("tensor", self.tok_consts)
        self.use_bank(0)
        for cidx in range(8):
            self.t(lambda t, cidx=cidx: t.transpose(
                self.bankb(0)[:, cidx * 8:cidx * 8 + 8],
                self.MED2[0:H, cidx * 128:(cidx + 1) * 128],
                self.CST["eye128"][0:H, 0:H]), inc=(cidx == 7))
        self.wrote_bank(0)
        self.read_bank(0, "vector")
        self.v(lambda v: v.tensor_copy(out=self.MED2[:, 3584:3648],
                                       in_=self.bankb(0)[:, 0:64]), inc=True)
        self.bank_read_done(0, "vector")
        self.tok_wbT = self.em.tok("vector")

    def _transpose_1024sq(self, src_f32, dst):
        src_r = src_f32.rearrange("(c p) d -> c p d", p=128)
        entry = self.em.tok("tensor")
        tr = {-2: entry, -1: entry}
        for r in range(8):
            slot = self.MED2[:, (r % 2) * 1024:(r % 2) * 1024 + 1024]
            ltok = self.dq(f"stg{r % 2}", "gpsimd").start(slot, src_r[r],
                                                          wait=[tr.get(r - 2)])
            bank = r % 2
            self.w8("tensor", ltok)
            self.w8("tensor", self.tok_consts)
            self.use_bank(bank)
            for cdx in range(8):
                self.t(lambda t, slot=slot, bank=bank, cdx=cdx: t.transpose(
                    self.bankb(bank)[:, cdx * 128:(cdx + 1) * 128],
                    slot[:, cdx * 128:(cdx + 1) * 128],
                    self.CST["eye128"][:]), inc=(cdx == 7))
            tr[r] = self.em.tok("tensor")
            self.wrote_bank(bank)
            self.read_bank(bank, "vector")
            out = dst.rearrange("p (c q) -> p c q", q=1024)[:, :, r * 128:(r + 1) * 128]
            self.v(lambda v, bank=bank, out=out: v.tensor_copy(
                out=out, in_=self.bankb(bank)[:].rearrange("p (c q) -> p c q", q=128)),
                inc=True)
            self.bank_read_done(bank, "vector")
        return self.em.tok("vector")

    def decay_consts(self):
        CST = self.CST

        def hc(a, n=1):
            return self.HC[:, a:a + n]
        tokd = self.dq("decay", "gpsimd").start(hc(0, 8), self.w["decay_t"])
        self.w8("scalar", tokd)
        self.w8("scalar", self.tok_consts)
        stok = self.s(lambda s: s.activation(out=hc(8, 8), in_=hc(0, 8),
                                             func=AF.Sigmoid), inc=True)
        self.w8("scalar", stok)
        self.s(lambda s: s.activation(out=hc(16, 8), in_=hc(8, 8), func=AF.Ln), inc=True)
        self.w8("vector", self.em.tok("scalar"))
        self.v(lambda v: v.tensor_scalar(out=hc(24, 8), in0=hc(16, 8), scalar1=-1.0,
                                         scalar2=None, op0=ALU.mult), inc=True)
        self.w8("scalar", self.em.tok("vector"))
        for h in range(H):
            self.s(lambda s, h=h: s.activation(out=hc(32 + h), in_=CST["iota_colp1"][:],
                                               func=AF.Exp, scale=hc(16 + h)))
            self.s(lambda s, h=h: s.activation(out=hc(48 + h), in_=CST["iota_colrev"][:],
                                               func=AF.Exp, scale=hc(16 + h)))
            self.s(lambda s, h=h: s.activation(out=hc(56 + h), in_=hc(16 + h),
                                               func=AF.Exp, scale=128.0), inc=(h == H - 1))
        self.w8("vector", self.em.tok("scalar"))
        self.v(lambda v: v.tensor_scalar(out=hc(40, 8), in0=hc(32, 8), scalar1=-1.0,
                                         scalar2=None, op0=ALU.mult), inc=True)
        e1 = self.BA[:, 0:256].bitcast(F32)
        e2 = self.BA[:, 256:512].bitcast(F32)
        for h in range(H):
            sl = slice(h * 128, (h + 1) * 128)
            self.w8("scalar", self.em.tok("vector"))
            self.s(lambda s, h=h, e1=e1: s.activation(out=e1, in_=CST["iota_dif"][:],
                                                      func=AF.Exp, scale=hc(16 + h)))
            self.s(lambda s, h=h, e2=e2: s.activation(out=e2, in_=CST["iota_dif"][:],
                                                      func=AF.Exp, scale=hc(24 + h)))
            self.w8("vector", self.em.tok("scalar"))
            self.v(lambda v, h=h: v.tensor_tensor(
                out=self.COLS[:, 120:121], in0=hc(16 + h),
                in1=CST["iota_colp1"][:], op=ALU.mult), inc=True)
            self.w8("scalar", self.em.tok("vector"))
            self.s(lambda s, h=h, sl=sl: s.activation(out=self.DEALL[:, sl],
                                                      in_=CST["iota_dif"][:],
                                                      func=AF.Exp, scale=hc(16 + h),
                                                      bias=self.COLS[:, 120:121]),
                   inc=True)
            self.w8("vector", self.em.tok("scalar"))
            self.v(lambda v, sl=sl, e1=e1: v.tensor_tensor(
                out=self.MASKU[:, sl], in0=e1, in1=CST["m_up_s_neg"][:], op=ALU.mult))
            self.v(lambda v, sl=sl, e1=e1: v.tensor_tensor(
                out=self.MASKA[:, sl], in0=e1, in1=CST["m_up_i"][:], op=ALU.mult))
            self.v(lambda v, sl=sl, e2=e2: v.tensor_tensor(
                out=self.MASKL[:, sl], in0=e2, in1=CST["m_low_s_neg"][:], op=ALU.mult))
            self.v(lambda v, sl=sl, h=h: v.tensor_scalar(
                out=self.CTDE[:, sl], in0=CST["eye128"][:],
                scalar1=hc(56 + h), scalar2=None, op0=ALU.mult), inc=True)
        self.v(lambda v: v.memset(self.SST[:], 0.0))
        self.v(lambda v: v.memset(self.STAT[:, 64:80], 0.0), inc=True)
        self.tok_dk = self.em.tok("vector")

    def ln_prologue(self, first=False):
        X, STAT = self.X, self.STAT
        scrf = self.BA[:, 0:2048].bitcast(F32)
        if first:
            self.w8("vector", self.tok_xload)
            for n in range(NT):
                self.v(lambda v, n=n: v.tensor_reduce(
                    out=STAT[:, n:n + 1], in_=X[:, n * D:(n + 1) * D],
                    axis=AX.X, op=ALU.add), inc=(n == NT - 1))
        else:
            self.v(lambda v: v.tensor_tensor(out=STAT[:, 0:16], in0=STAT[:, 0:16],
                                             in1=STAT[:, 64:80], op=ALU.add), inc=True)
        self.w8("scalar", self.em.tok("vector"))
        self.w8("scalar", self.tok_xload)
        for n in range(NT):
            stok = self.s(lambda s, n=n: s.activation(
                out=scrf, in_=X[:, n * D:(n + 1) * D], func=AF.Square,
                accum_out=STAT[:, 16 + n:17 + n]), inc=True)
            self.w8("scalar", stok)
        self.w8("vector", self.em.tok("scalar"))
        self.v(lambda v: v.tensor_scalar(out=STAT[:, 32:48], in0=STAT[:, 0:16],
                                         scalar1=-1.0 / D, scalar2=None, op0=ALU.mult))
        self.v(lambda v: v.tensor_scalar(out=STAT[:, 64:80], in0=STAT[:, 16:32],
                                         scalar1=1.0 / D, scalar2=None, op0=ALU.mult))
        self.v(lambda v: v.tensor_tensor(out=STAT[:, 80:96], in0=STAT[:, 32:48],
                                         in1=STAT[:, 32:48], op=ALU.mult))
        self.v(lambda v: v.scalar_tensor_tensor(out=STAT[:, 80:96], in0=STAT[:, 80:96],
                                                scalar=-1.0, in1=STAT[:, 64:80],
                                                op0=ALU.mult, op1=ALU.add))
        self.v(lambda v: v.tensor_scalar(out=STAT[:, 80:96], in0=STAT[:, 80:96],
                                         scalar1=LN_EPS, scalar2=None, op0=ALU.add),
               inc=True)
        self.w8("scalar", self.em.tok("vector"))
        stok = self.s(lambda s: s.activation(out=STAT[:, 64:80], in_=STAT[:, 80:96],
                                             func=AF.Ln), inc=True)
        self.w8("scalar", stok)
        self.s(lambda s: s.activation(out=STAT[:, 48:64], in_=STAT[:, 64:80],
                                      func=AF.Exp, scale=-0.5), inc=True)
        self.tok_ln = self.em.tok("scalar")
        self.w8("vector", self.tok_ln)
        self.v(lambda v: v.memset(self.STAT[:, 0:16], 0.0))
        self.v(lambda v: v.memset(self.STAT[:, 64:80], 0.0), inc=True)

    def ln_apply(self, n, out_ap):
        self.v(lambda v, n=n, out_ap=out_ap: v.tensor_scalar(
            out=out_ap, in0=self.X[:, n * D:(n + 1) * D],
            scalar1=self.STAT[:, 32 + n:33 + n], scalar2=self.STAT[:, 48 + n:49 + n],
            op0=ALU.add, op1=ALU.mult), inc=True)
        return self.em.tok("vector")

    def resid_update(self, n, half, in0_ap, extra=None):
        xs = self.X[:, n * D + half * 512:n * D + half * 512 + 512]
        acc_col = (n) if half == 0 else (64 + n)
        acc = self.STAT[:, acc_col:acc_col + 1]
        if extra is not None:
            scx = self.BB[:, 14336:15360].bitcast(F32)
            self.v(lambda v, in0_ap=in0_ap, extra=extra, scx=scx: v.tensor_tensor(
                out=scx, in0=in0_ap, in1=extra, op=ALU.add), inc=True)
            in0_ap = scx
        self.v(lambda v, in0_ap=in0_ap, xs=xs, acc=acc: v.scalar_tensor_tensor(
            out=xs, in0=in0_ap, scalar=1.0, in1=xs, op0=ALU.mult, op1=ALU.add,
            accum_out=acc), inc=True)

    def attn_a(self):
        em, CST = self.em, self.CST
        BA, BB, MED2, COLS, HC = self.BA, self.BB, self.MED2, self.COLS, self.HC
        X, SST = self.X, self.SST

        def slot(idx, p):
            return BA[:, (idx * 2 + p) * 1024:(idx * 2 + p) * 1024 + 1024]

        def bbs(a, n=1024):
            return BB[:, a:a + n]

        def kT(p):
            return bbs(p * 1024)

        def kbT(p):
            return bbs(2048 + p * 1024)

        def kdeT(p):
            return bbs(4096 + p * 1024)
        U8 = bbs(6144)
        UT8 = bbs(7168)
        AT8 = bbs(8192)
        P8 = bbs(9216)
        PT8 = bbs(10240)
        Bm = bbs(11264)
        kcdT8 = bbs(12288)
        vnew8 = bbs(13312)

        def ccol(p, a, n=8):
            return COLS[:, p * 64 + a:p * 64 + a + n]
        scrq = self.BC[:, 0:2048].bitcast(F32)

        self.w8("vector", self.tok_dk)
        aa_end = {}
        eTok = {}

        def make_E(n):
            p = n % 2
            xn_t = slot(0, p)
            xnT_t = MED2[:, p * 1024:(p + 1) * 1024]
            ge = aa_end.get(p)
            rn_prev = eTok.get(("rn", n - 1))

            def s1():
                if ge:
                    self.w8("vector", ge["t"])
                    self.w8("vector", ge["s"])
                tok_xn = self.ln_apply(n, xn_t)
                eTok[("ln", n)] = tok_xn
                self.w8("tensor", tok_xn)
                self.transpose8([xn_t[:, c * 128:(c + 1) * 128] for c in range(8)],
                                p, xnT_t, "vector")
                eTok[("xnT", n)] = self.em.tok("vector")

            def s2():
                self.use_bank(p)
                self.w8("tensor", self.tok_wbT)
                self.w8("tensor", eTok[("xnT", n)])
                for kc in range(8):
                    self.t(lambda t, kc=kc: t.matmul(
                        self.bankf(p, 8), xnT_t[:, kc * 128:(kc + 1) * 128],
                        self.MED2[:, 3584 + kc * 8:3592 + kc * 8],
                        start=(kc == 0), stop=(kc == 7)), inc=(kc == 7))
                self.wrote_bank(p)
                self.read_bank(p, "scalar")
                if ge:
                    self.w8("scalar", ge["t"])
                    self.w8("scalar", ge["v"])
                self.s(lambda s_: s_.activation(out=ccol(p, 0), in_=self.bankf(p, 8),
                                                func=AF.Sigmoid), inc=True)
                self.bank_read_done(p, "scalar")

            def s3():
                self.w8("scalar", eTok[("ln", n)])
                if rn_prev:
                    self.w8("scalar", rn_prev)
                self.s(lambda s_: s_.activation(out=scrq, in_=xn_t,
                                                func=AF.Square), inc=True)
                self.w8("vector", self.em.tok("scalar"))
                self.v(lambda v: v.tensor_reduce(
                    out=ccol(p, 40).rearrange("p (h o) -> p h o", o=1),
                    in_=scrq.rearrange("p (h q) -> p h q", q=128), axis=AX.X,
                    op=ALU.add))
                self.v(lambda v: v.tensor_scalar(
                    out=ccol(p, 40), in0=ccol(p, 40), scalar1=1e-24, scalar2=None,
                    op0=ALU.max), inc=True)
                eTok[("rn", n)] = self.em.tok("vector")

            def s4():
                self.w8("scalar", eTok[("rn", n)])
                stok = self.s(lambda s_: s_.activation(out=ccol(p, 48),
                                                       in_=ccol(p, 40),
                                                       func=AF.Ln), inc=True)
                self.w8("scalar", stok)
                self.s(lambda s_: s_.activation(out=ccol(p, 8), in_=ccol(p, 48),
                                                func=AF.Exp, scale=-0.5), inc=True)
                eTok[("exp", n)] = self.em.tok("scalar")

            def s5():
                self.w8("vector", eTok[("exp", n)])
                self.v(lambda v: v.tensor_tensor(out=ccol(p, 16), in0=ccol(p, 8),
                                                 in1=ccol(p, 0), op=ALU.mult))
                self.v(lambda v: v.tensor_tensor(out=ccol(p, 24), in0=ccol(p, 16),
                                                 in1=HC[:, 40:48], op=ALU.mult))
                self.v(lambda v: v.tensor_tensor(out=ccol(p, 32), in0=ccol(p, 8),
                                                 in1=HC[:, 48:56], op=ALU.mult),
                       inc=True)

                def kbuild(dst, colap):
                    self.v(lambda v, dst=dst, colap=colap: v.tensor_tensor(
                        out=dst.rearrange("p (h q) -> p h q", q=128),
                        in0=xn_t.rearrange("p (h q) -> p h q", q=128),
                        in1=colap.rearrange("p (h o) -> p h o", o=1)
                            .broadcast_to((128, 8, 128)),
                        op=ALU.mult), inc=True)
                kbuild(slot(1, p), ccol(p, 8))
                kbuild(slot(2, p), ccol(p, 16))
                kbuild(slot(3, p), ccol(p, 24))
                kbuild(slot(4, p), ccol(p, 32))
                eTok[("kb", n)] = self.em.tok("vector")

            def s6():
                self.w8("tensor", eTok[("kb", n)])
                eTok[("kTe", n)] = self.transpose8(
                    [slot(1, p)[:, c * 128:(c + 1) * 128] for c in range(8)], p,
                    kT(p), "vector")
                self.v(lambda v: v.tensor_tensor(out=kdeT(p), in0=kT(p),
                                                 in1=self.DEALL[:], op=ALU.mult),
                       inc=True)
                eTok[("kdeT", n)] = self.em.tok("vector")
                self.bg_fill(4)

            def s7():
                eTok[("kbTe", n)] = self.transpose8(
                    [slot(2, p)[:, c * 128:(c + 1) * 128] for c in range(8)], p,
                    kbT(p), "scalar")

            def s8():
                self.w8("tensor", self.tok_WwT)
                for half in range(2):
                    self.use_bank(p)
                    for kc in range(8):
                        self.t(lambda t, kc=kc, half=half: t.matmul(
                            self.bankf(p), xnT_t[:, kc * 128:(kc + 1) * 128],
                            self.MED[:, kc * 1024 + half * 512:
                                      kc * 1024 + half * 512 + 512],
                            start=(kc == 0), stop=(kc == 7)), inc=(kc == 7))
                    self.wrote_bank(p)
                    self.read_bank(p, "vector")
                    self.v(lambda v, half=half: v.tensor_tensor(
                        out=slot(5, p)[:, half * 512:(half + 1) * 512]
                            .rearrange("p (h q) -> p h q", q=128),
                        in0=self.bankf(p).rearrange("p (h q) -> p h q", q=128),
                        in1=ccol(p, 4 * half, 4).rearrange("p (h o) -> p h o", o=1)
                            .broadcast_to((128, 4, 128)),
                        op=ALU.mult), inc=True)
                    self.bank_read_done(p, "vector")
                eTok[("vb", n)] = self.em.tok("vector")
                self.bg_fill(4)
            return [s1, s2, s3, s4, s5, s6, s7, s8]

        for seg in make_E(0):
            seg()
        for n in range(NT):
            p = n % 2
            eq = make_E(n + 1) if n + 1 < NT else []

            def ef(k=1):
                for _ in range(k):
                    if eq:
                        eq.pop(0)()

            def pf(i, h):
                return self.PS[:, 512 * i + h * 128:512 * i + h * 128 + 128]

            def pairf(i):
                return self.PS[:, 512 * i:512 * i + 1024]
            prs = [(2, kT, kbT), (4, kbT, kT), (6, kT, kT)]
            self.w8("tensor", eTok[("kTe", n)])
            self.w8("tensor", eTok[("kbTe", n)])
            for bk, lhs, rhs in prs:
                self.use_bank(bk)
                self.use_bank(bk + 1)
                for h in range(H):
                    self.t(lambda t, bk=bk, lhs=lhs, rhs=rhs, h=h, p=p: t.matmul(
                        pf(bk, h),
                        lhs(p)[:, h * 128:(h + 1) * 128],
                        rhs(p)[:, h * 128:(h + 1) * 128],
                        start=True, stop=True), inc=(h == H - 1))
                self.wrote_bank(bk)
                self.wrote_bank(bk + 1)
            for bk, dst, msk in [(2, U8, self.MASKU), (4, UT8, self.MASKL),
                                 (6, AT8, self.MASKA)]:
                self.read_bank(bk, "vector")
                self.read_bank(bk + 1, "vector")
                self.v(lambda v, bk=bk, dst=dst, msk=msk: v.tensor_tensor(
                    out=dst, in0=pairf(bk), in1=msk[:], op=ALU.mult), inc=True)
                self.bank_read_done(bk, "vector")
                self.bank_read_done(bk + 1, "vector")
            self.v(lambda v: v.tensor_tensor(out=Bm, in0=U8, in1=CST["eye8"][:],
                                             op=ALU.add), inc=True)
            tok_B0 = self.em.tok("vector")
            ef()
            curP, curPT = U8, UT8
            for j in range(1, 7):
                ef()
                self.bg_fill(6)
                self.use_bank(2)
                self.use_bank(3)
                self.w8("tensor", tok_B0)
                for h in range(H):
                    self.t(lambda t, h=h, curP=curP, curPT=curPT: t.matmul(
                        pf(2, h),
                        curP[:, h * 128:(h + 1) * 128],
                        curPT[:, h * 128:(h + 1) * 128],
                        start=True, stop=True), inc=(h == H - 1))
                self.wrote_bank(2)
                self.wrote_bank(3)
                if j < 6:
                    self.use_bank(4)
                    self.use_bank(5)
                    for h in range(H):
                        self.t(lambda t, h=h, curP=curP, curPT=curPT: t.matmul(
                            pf(4, h),
                            curPT[:, h * 128:(h + 1) * 128],
                            curP[:, h * 128:(h + 1) * 128],
                            start=True, stop=True), inc=(h == H - 1))
                    self.wrote_bank(4)
                    self.wrote_bank(5)
                self.read_bank(2, "scalar")
                self.read_bank(3, "scalar")
                self.w8("scalar", self.em.tok("tensor"))
                self.s(lambda s: s.copy(out=PT8, in_=pairf(2)), inc=True)
                self.bank_read_done(2, "scalar")
                self.bank_read_done(3, "scalar")
                if j < 6:
                    self.read_bank(4, "scalar")
                    self.read_bank(5, "scalar")
                    self.s(lambda s: s.copy(out=P8, in_=pairf(4)), inc=True)
                    self.bank_read_done(4, "scalar")
                    self.bank_read_done(5, "scalar")
                tok_P = self.em.tok("scalar")
                self.use_bank(6)
                self.use_bank(7)
                self.w8("tensor", tok_P)
                for h in range(H):
                    self.t(lambda t, h=h: t.matmul(
                        pf(6, h),
                        PT8[:, h * 128:(h + 1) * 128],
                        Bm[:, h * 128:(h + 1) * 128],
                        start=True, stop=True), inc=(h == H - 1))
                self.wrote_bank(6)
                self.wrote_bank(7)
                self.read_bank(6, "vector")
                self.read_bank(7, "vector")
                self.v(lambda v: v.tensor_tensor(out=Bm, in0=pairf(6), in1=Bm,
                                                 op=ALU.add), inc=True)
                self.bank_read_done(6, "vector")
                self.bank_read_done(7, "vector")
                tok_B0 = self.em.tok("vector")
                curP, curPT = P8, PT8
            self.use_bank(6)
            self.use_bank(7)
            self.w8("tensor", tok_B0)
            for h in range(H):
                self.t(lambda t, h=h, p=p: t.matmul(
                    pf(6, h),
                    slot(3, p)[:, h * 128:(h + 1) * 128],
                    Bm[:, h * 128:(h + 1) * 128],
                    start=True, stop=True), inc=(h == H - 1))
            self.wrote_bank(6)
            self.wrote_bank(7)
            self.read_bank(6, "scalar")
            self.read_bank(7, "scalar")
            self.s(lambda s: s.copy(out=kcdT8, in_=pairf(6)), inc=True)
            self.bank_read_done(6, "scalar")
            self.bank_read_done(7, "scalar")
            tok_kcd = self.em.tok("scalar")
            ef()
            for half in range(2):
                hs = list(range(4 * half, 4 * half + 4))
                self.use_bank(2)
                self.w8("tensor", tok_kcd)
                self.w8("tensor", eTok[("vb", n)])
                for h in hs:
                    q = (h % 4) * 128
                    self.t(lambda t, h=h, q=q, p=p: t.matmul(
                        self.bankf(2)[:, q:q + 128],
                        Bm[:, h * 128:(h + 1) * 128],
                        slot(5, p)[:, h * 128:(h + 1) * 128],
                        start=True, stop=False))
                    self.t(lambda t, h=h, q=q: t.matmul(
                        self.bankf(2)[:, q:q + 128],
                        kcdT8[:, h * 128:(h + 1) * 128],
                        SST[:, h * 128:(h + 1) * 128],
                        start=False, stop=True), inc=(h == hs[-1]))
                self.wrote_bank(2)
                self.read_bank(2, "vector")
                self.v(lambda v, half=half: v.tensor_copy(
                    out=vnew8[:, half * 512:(half + 1) * 512], in_=self.bankf(2)),
                    inc=True)
                self.bank_read_done(2, "vector")
                tok_vn = self.em.tok("vector")
                self.use_bank(3)
                self.w8("tensor", tok_vn)
                self.w8("tensor", eTok[("kdeT", n)])
                for h in hs:
                    q = (h % 4) * 128
                    self.t(lambda t, h=h, q=q, p=p: t.matmul(
                        self.bankf(3)[:, q:q + 128],
                        kdeT(p)[:, h * 128:(h + 1) * 128],
                        SST[:, h * 128:(h + 1) * 128],
                        start=True, stop=False))
                    self.t(lambda t, h=h, q=q: t.matmul(
                        self.bankf(3)[:, q:q + 128],
                        AT8[:, h * 128:(h + 1) * 128],
                        vnew8[:, h * 128:(h + 1) * 128],
                        start=False, stop=True), inc=(h == hs[-1]))
                self.wrote_bank(3)
                self.use_bank(4)
                for h in hs:
                    q = (h % 4) * 128
                    self.t(lambda t, h=h, q=q, p=p: t.matmul(
                        self.bankf(4)[:, q:q + 128],
                        slot(4, p)[:, h * 128:(h + 1) * 128],
                        vnew8[:, h * 128:(h + 1) * 128],
                        start=True, stop=False))
                    self.t(lambda t, h=h, q=q: t.matmul(
                        self.bankf(4)[:, q:q + 128],
                        self.CTDE[:, h * 128:(h + 1) * 128],
                        SST[:, h * 128:(h + 1) * 128],
                        start=False, stop=True), inc=(h == hs[-1]))
                self.wrote_bank(4)
                self.read_bank(3, "scalar")
                self.s(lambda s, half=half, p=p: s.copy(
                    out=slot(6, p)[:, half * 512:(half + 1) * 512],
                    in_=self.bankf(3)), inc=True)
                self.bank_read_done(3, "scalar")
                self.read_bank(4, "vector")
                self.v(lambda v, half=half: v.tensor_copy(
                    out=SST[:, half * 512:(half + 1) * 512], in_=self.bankf(4)),
                    inc=True)
                self.bank_read_done(4, "vector")
                ef()
                self.bg_fill(6)
            tok_o = self.em.tok("scalar")
            self.w8("tensor", tok_o)
            tok_oT = self.transpose8(
                [slot(6, p)[:, c * 128:(c + 1) * 128] for c in range(8)], 1 - p,
                slot(7, p), "vector")
            self.w8("tensor", self.tok_WrT)
            for half in range(2):
                bk = 5 + half
                self.use_bank(bk)
                self.w8("tensor", tok_oT)
                for kc in range(8):
                    self.t(lambda t, bk=bk, kc=kc, half=half, p=p: t.matmul(
                        self.bankf(bk), slot(7, p)[:, kc * 128:(kc + 1) * 128],
                        self.BC[:, 2048 + kc * 1024 + half * 512:
                                2048 + kc * 1024 + half * 512 + 512],
                        start=(kc == 0), stop=(kc == 7)), inc=(kc == 7))
                self.wrote_bank(bk)
                self.read_bank(bk, "vector")
                self.resid_update(n, half, self.bankf(bk),
                                  extra=slot(0, p)[:, half * 512:(half + 1) * 512])
                self.bank_read_done(bk, "vector")
            while eq:
                eq.pop(0)()
            aa_end[p] = dict(t=self.em.tok("tensor"), v=self.em.tok("vector"),
                             s=self.em.tok("scalar"))
            self.bg_fill(10)

    def ffn(self, first):
        if first:
            self.prep_drain()
        em, X, STAT = self.em, self.X, self.STAT
        BA, BB, BC, MED, MED2 = self.BA, self.BB, self.BC, self.MED, self.MED2

        def hgrp(i):
            return BC[:, i * II:(i + 1) * II]

        def xn_t(p):
            return BC[:, 11264 + p * 2048:11264 + p * 2048 + 2048].bitcast(F32)

        def xq_t(p):
            return BC[:, 15360 + p * 1024:15360 + p * 1024 + 1024]

        def xqT(c, i):
            return MED2[:, c * 512 + i * 128:c * 512 + i * 128 + 128]

        def hq_t(p):
            return BA[:, 0:II] if p == 0 else BB[:, 11264:11264 + II]

        def hqT_g(i):
            return BB[:, i * 2816:(i + 1) * 2816]

        def wslice(b):
            return MED[:, 0:8192] if b == 0 else BB[:, 0:8192]
        sil = BB[:, 14336:14848]
        wg_r = self.scr["wgT"].rearrange("(c p) i -> p c i", p=128)
        wu_r = self.scr["wuT"].rearrange("(c p) i -> p c i", p=128)
        wd_r = self.scr["wdT"].rearrange("(c p) d -> c p d", p=128)

        wdslot = [BA[:, 14080:15104], BA[:, 15104:16128], BB[:, 15360:16384]]
        g0t = em.tok("tensor")
        g0v = em.tok("vector")
        self._tok_gu = g0t
        self._tok_hqp = {0: g0t, 1: g0t}
        dtoks = {}
        wdq = {"q": 0}
        xn_pair = BC[:, 11264:15360].bitcast(F32)
        xq_pair = BC[:, 15360:17408]

        def emit_xq(grp, guard):
            tiles = list(range(grp * 4, grp * 4 + 4))
            for pi in range(2):
                self.w8("vector", guard)
                self.ln_apply(tiles[2 * pi], xn_t(0))
                self.ln_apply(tiles[2 * pi + 1], xn_t(1))
                self.quant_tile(xn_pair, xq_pair, 2048, src_f32=True)
                tok_xq = self.em.tok("vector")
                self.w8("tensor", tok_xq)
                for half in range(2):
                    i = 2 * pi + half
                    dst3 = MED2[:].rearrange("p (c q) -> p c q", q=512) \
                        [:, :, i * 128:(i + 1) * 128]
                    self.transpose8([xq_t(half)[:, c * 128:(c + 1) * 128]
                                     for c in range(8)],
                                    4 + (i % 2), dst3, "vector")
            self._tok_xqT = self.em.tok("vector")
            self._ffn_gtok = self.em.tok("tensor")

        def hquant(i, n):
            p = n % 2
            self.w8("vector", self._tok_hqp[p])
            self.quant_tile(hgrp(i), hq_t(p), II)
            self.w8("tensor", self.em.tok("vector"))
            for bi, lo in enumerate((0, 8, 16)):
                cnt = 8 if bi < 2 else 6
                self.transpose8(
                    [hq_t(p)[:, (lo + c) * 128:(lo + c + 1) * 128]
                     for c in range(cnt)],
                    4 + (bi % 2), hqT_g(i)[:, lo * 128:(lo + cnt) * 128], "vector")
            self._tok_hqp[p] = self.em.tok("tensor")

        def down_pass(pair):
            for li, i, n in pair:
                self.use_bank(2 * li)
                self.use_bank(2 * li + 1)
            self.w8("tensor", self.em.tok("vector"))
            for kc in range(NI):
                q = wdq["q"]
                slot = wdslot[q % 3]
                guard = [dtoks.get(q - 3), *self.tok_ffnprep]
                if q < 3:
                    guard += [g0t, g0v]
                dtok = self.dq(f"wd{q % 3}").start(slot, wd_r[kc], wait=guard)
                self.w8("tensor", dtok)
                for li, i, n in pair:
                    for half in range(2):
                        self.t(lambda t, li=li, i=i, kc=kc, half=half, slot=slot:
                               t.matmul(
                                   self.bankf(2 * li + half),
                                   hqT_g(i)[:, kc * 128:(kc + 1) * 128],
                                   slot[:, half * 512:(half + 1) * 512],
                                   start=(kc == 0), stop=(kc == NI - 1)),
                               inc=(li == pair[-1][0] and half == 1))
                dtoks[q] = self.em.tok("tensor")
                wdq["q"] = q + 1
            for li, i, n in pair:
                self.wrote_bank(2 * li)
                self.wrote_bank(2 * li + 1)

        def resid_pass(pair):
            for li, i, n in pair:
                for half in range(2):
                    bk = 2 * li + half
                    self.read_bank(bk, "vector")
                    self.resid_update(n, half, self.bankf(bk))
                    self.bank_read_done(bk, "vector")

        emit_xq(0, em.tok("tensor"))
        for grp in range(4):
            tiles = list(range(grp * 4, grp * 4 + 4))
            for nch in range(6):
                ncols = 512 if nch < 5 else 256
                wb = nch % 2
                wsl = wslice(wb)
                if wb == 0:
                    guard = [*self.tok_ffnprep, self._tok_gu]
                else:
                    guard = [*self.tok_ffnprep, self.em.tok("tensor")]
                tok_wg = self.dq(f"wsl{wb}").start(
                    wsl[:, 0:8 * ncols].rearrange("p (c q) -> p c q", q=ncols),
                    wg_r[:, :, nch * 512:nch * 512 + ncols], wait=guard)
                tok_wu = self.dq(f"wslu{wb}").start(
                    wsl[:, 4096:4096 + 8 * ncols].rearrange("p (c q) -> p c q", q=ncols),
                    wu_r[:, :, nch * 512:nch * 512 + ncols], wait=guard)
                self.w8("tensor", tok_wg)
                self.w8("tensor", tok_wu)
                self.w8("tensor", self._tok_xqT)
                for i, n in enumerate(tiles):
                    bkg, bku = 2 * i, 2 * i + 1
                    self.use_bank(bkg)
                    self.use_bank(bku)
                    for kc in range(8):
                        self.t(lambda t, bkg=bkg, kc=kc, i=i, wsl=wsl, ncols=ncols:
                               t.matmul(self.bankf(bkg, ncols), xqT(kc, i),
                                        wsl[:, kc * ncols:(kc + 1) * ncols],
                                        start=(kc == 0), stop=(kc == 7)))
                        self.t(lambda t, bku=bku, kc=kc, i=i, wsl=wsl, ncols=ncols:
                               t.matmul(self.bankf(bku, ncols), xqT(kc, i),
                                        wsl[:, 4096 + kc * ncols:4096 + (kc + 1) * ncols],
                                        start=(kc == 0), stop=(kc == 7)),
                               inc=(kc == 7))
                    self.wrote_bank(bkg)
                    self.wrote_bank(bku)
                    self.read_bank(bkg, "scalar")
                    self.w8("scalar", self.em.tok("vector"))
                    self.s(lambda s, bkg=bkg, ncols=ncols: s.activation(
                        out=sil[:, 0:ncols], in_=self.bankf(bkg, ncols),
                        func=AF.Silu), inc=True)
                    self.bank_read_done(bkg, "scalar")
                    self.read_bank(bku, "vector")
                    self.w8("vector", self.em.tok("scalar"))
                    self.v(lambda v, bku=bku, i=i, nch=nch, ncols=ncols:
                           v.tensor_tensor(
                               out=hgrp(i)[:, nch * 512:nch * 512 + ncols],
                               in0=self.bankf(bku, ncols), in1=sil[:, 0:ncols],
                               op=ALU.mult), inc=True)
                    self.bank_read_done(bku, "vector")
            self._tok_gu = self.em.tok("tensor")
            pair0 = [(0, 0, tiles[0]), (1, 1, tiles[1])]
            pair1 = [(0, 2, tiles[2]), (1, 3, tiles[3])]
            hquant(0, tiles[0])
            hquant(1, tiles[1])
            down_pass(pair0)
            hquant(2, tiles[2])
            hquant(3, tiles[3])
            resid_pass(pair0)
            down_pass(pair1)
            if grp < 3:
                emit_xq(grp + 1, self._ffn_gtok)
            resid_pass(pair1)

    def quant_tile(self, src, dst, F, src_f32=False, sbase=None, eng="vector",
                   qc=None):
        nb = F // 16
        ST, c0 = sbase if sbase is not None else (self.BA, 2816)
        f32A = ST[:, c0:c0 + 2 * F].bitcast(F32)
        f32B = ST[:, c0 + 2 * F:c0 + 4 * F].bitcast(F32)
        rB = ST[:, c0 + 2 * F:c0 + 3 * F]
        d1B = ST[:, c0 + 3 * F:c0 + 4 * F]
        d2B = ST[:, c0:c0 + F]
        magB = ST[:, c0 + F:c0 + 2 * F]
        habs = ST[:, c0 + 3 * F:c0 + 4 * F]
        QC = qc if qc is not None else self.QCOL

        def v(fn, inc=False):
            return self.em.emit(eng, fn, inc)
        src3 = src.rearrange("p (b k) -> p b k", k=16)
        self.v(lambda v_: v_.tensor_reduce(
            out=QC[:, 0:nb].rearrange("p (b o) -> p b o", o=1), in_=src3,
            axis=AX.X, op=ALU.max, apply_absolute_value=True))
        self.v(lambda v_: v_.tensor_scalar(out=QC[:, 0:nb], in0=QC[:, 0:nb],
                                           scalar1=6e-12, scalar2=None, op0=ALU.max))
        self.v(lambda v_: v_.reciprocal(out=QC[:, 176:176 + nb], in_=QC[:, 0:nb]))
        self.v(lambda v_: v_.tensor_scalar(out=QC[:, 176:176 + nb],
                                           in0=QC[:, 176:176 + nb],
                                           scalar1=12.0, scalar2=None, op0=ALU.mult))
        self.v(lambda v_: v_.tensor_scalar(out=QC[:, 352:352 + nb], in0=QC[:, 0:nb],
                                           scalar1=1.0 / 12.0, scalar2=None,
                                           op0=ALU.mult), inc=True)
        if eng != "vector":
            self.em.wait(eng, self.em.tok("vector"))
        if eng != "vector":
            assert src_f32
            self._quant_pool_body(src, dst, F, QC, ST, c0, nb, v)
            return
        inv3 = QC[:, 176:176 + nb].rearrange("p (b o) -> p b o", o=1) \
            .broadcast_to((128, nb, 16))
        sc3 = QC[:, 352:352 + nb].rearrange("p (b o) -> p b o", o=1) \
            .broadcast_to((128, nb, 16))
        if src_f32 and eng != "vector":
            v(lambda v_: v_.tensor_scalar(out=f32A, in0=src, scalar1=0.0,
                                          scalar2=2.0, op0=ALU.is_ge, op1=ALU.mult))
            v(lambda v_: v_.scalar_tensor_tensor(out=f32A, in0=f32A, scalar=-1.0,
                                                 in1=src, op0=ALU.add, op1=ALU.mult))
            v(lambda v_: v_.scalar_tensor_tensor(
                out=f32A.rearrange("p (b k) -> p b k", k=16),
                in0=f32A.rearrange("p (b k) -> p b k", k=16), scalar=12.0, in1=inv3,
                op0=ALU.mult, op1=ALU.mult))
        elif src_f32:
            v(lambda v_: v_.tensor_scalar(
                out=f32A.bitcast(mybir.dt.uint32), in0=src.bitcast(mybir.dt.uint32),
                scalar1=0x7FFFFFFF, scalar2=None, op0=ALU.bitwise_and))
            v(lambda v_: v_.tensor_tensor(
                out=f32A.rearrange("p (b k) -> p b k", k=16),
                in0=f32A.rearrange("p (b k) -> p b k", k=16), in1=inv3,
                op=ALU.mult))
        else:
            v(lambda v_: v_.tensor_scalar(out=habs.bitcast(U16), in0=src.bitcast(U16),
                                          scalar1=0x7FFF, scalar2=None,
                                          op0=ALU.bitwise_and))
            v(lambda v_: v_.scalar_tensor_tensor(
                out=f32A.rearrange("p (b k) -> p b k", k=16),
                in0=habs.rearrange("p (b k) -> p b k", k=16), scalar=1.0, in1=inv3,
                op0=ALU.mult, op1=ALU.mult))
        M23 = 8388608.0
        v(lambda v_: v_.tensor_scalar(out=f32B, in0=f32A, scalar1=8.0, scalar2=12.0,
                                      op0=ALU.max, op1=ALU.min))
        v(lambda v_: v_.scalar_tensor_tensor(out=f32B, in0=f32B, scalar=-0.75,
                                             in1=f32A, op0=ALU.mult, op1=ALU.add))
        v(lambda v_: v_.tensor_scalar(out=f32A, in0=f32A, scalar1=4.0, scalar2=8.0,
                                      op0=ALU.max, op1=ALU.min))
        v(lambda v_: v_.scalar_tensor_tensor(out=f32B, in0=f32A, scalar=-0.5,
                                             in1=f32B, op0=ALU.mult, op1=ALU.add))
        v(lambda v_: v_.tensor_scalar(out=f32A, in0=f32B, scalar1=16.0 + M23,
                                      scalar2=None, op0=ALU.add))
        v(lambda v_: v_.tensor_scalar(out=rB, in0=f32A, scalar1=-M23, scalar2=None,
                                      op0=ALU.add))
        v(lambda v_: v_.tensor_scalar(out=d1B, in0=rB, scalar1=12.0, scalar2=14.0,
                                      op0=ALU.max, op1=ALU.min))
        v(lambda v_: v_.tensor_scalar(out=d2B, in0=rB, scalar1=14.0, scalar2=15.0,
                                      op0=ALU.max, op1=ALU.min))
        v(lambda v_: v_.tensor_tensor(out=d1B, in0=d1B, in1=rB, op=ALU.add))
        v(lambda v_: v_.scalar_tensor_tensor(out=d1B, in0=d2B, scalar=3.0, in1=d1B,
                                             op0=ALU.mult, op1=ALU.add))
        v(lambda v_: v_.scalar_tensor_tensor(
            out=magB.rearrange("p (b k) -> p b k", k=16),
            in0=d1B.rearrange("p (b k) -> p b k", k=16), scalar=-62.0, in1=sc3,
            op0=ALU.add, op1=ALU.mult))
        if src_f32:
            shi = src.bitcast(U16).rearrange("p (k two) -> p k two", two=2)[:, :, 1:2] \
                .rearrange("p k o -> p (k o)")
        else:
            shi = src.bitcast(U16)
        if eng != "vector":
            self.w8("vector", self.em.tok(eng))
        self.v(lambda v_: v_.tensor_scalar(out=dst.bitcast(U16), in0=shi,
                                           scalar1=0x8000, scalar2=None,
                                           op0=ALU.bitwise_and))
        self.v(lambda v_: v_.tensor_tensor(out=dst.bitcast(U16),
                                           in0=dst.bitcast(U16),
                                           in1=magB.bitcast(U16),
                                           op=ALU.bitwise_or), inc=True)
        if eng != "vector":
            self.em.wait(eng, self.em.tok("vector"))

    def _quant_pool_body(self, src, dst, F, QC, ST, c0, nb, v, src_f32=True):
        f32A = ST[:, c0:c0 + 2 * F].bitcast(F32)
        f32B = ST[:, c0 + 2 * F:c0 + 4 * F].bitcast(F32)
        rB = ST[:, c0 + 2 * F:c0 + 3 * F]
        d1B = ST[:, c0 + 3 * F:c0 + 4 * F]
        d2B = ST[:, c0:c0 + F]
        magB = ST[:, c0 + F:c0 + 2 * F]
        inv3 = QC[:, 176:176 + nb].rearrange("p (b o) -> p b o", o=1) \
            .broadcast_to((128, nb, 16))
        sc3 = QC[:, 352:352 + nb].rearrange("p (b o) -> p b o", o=1) \
            .broadcast_to((128, nb, 16))
        M23 = 8388608.0
        v(lambda v_: v_.tensor_scalar(out=f32B, in0=src, scalar1=0.0, scalar2=2.0,
                                      op0=ALU.is_ge, op1=ALU.mult))
        v(lambda v_: v_.tensor_scalar(out=f32B, in0=f32B, scalar1=-1.0,
                                      scalar2=None, op0=ALU.add))
        v(lambda v_: v_.tensor_tensor(out=f32A, in0=f32B, in1=src, op=ALU.mult))
        v(lambda v_: v_.tensor_tensor(
            out=f32A.rearrange("p (b k) -> p b k", k=16),
            in0=f32A.rearrange("p (b k) -> p b k", k=16), in1=inv3, op=ALU.mult))
        v(lambda v_: v_.tensor_scalar(out=f32B, in0=f32A, scalar1=8.0, scalar2=12.0,
                                      op0=ALU.max, op1=ALU.min))
        v(lambda v_: v_.tensor_scalar(out=f32B, in0=f32B, scalar1=-0.75,
                                      scalar2=None, op0=ALU.mult))
        v(lambda v_: v_.tensor_tensor(out=f32B, in0=f32B, in1=f32A, op=ALU.add))
        v(lambda v_: v_.tensor_scalar(out=f32A, in0=f32A, scalar1=4.0, scalar2=8.0,
                                      op0=ALU.max, op1=ALU.min))
        v(lambda v_: v_.tensor_scalar(out=f32A, in0=f32A, scalar1=-0.5,
                                      scalar2=16.0, op0=ALU.mult, op1=ALU.add))
        v(lambda v_: v_.tensor_tensor(out=f32A, in0=f32A, in1=f32B, op=ALU.add),
          inc=True)
        self.w8("vector", self.em.tok("gpsimd"))
        self.v(lambda v_: v_.tensor_scalar(out=f32B, in0=f32A, scalar1=M23,
                                           scalar2=None, op0=ALU.add))
        self.v(lambda v_: v_.tensor_scalar(out=magB, in0=f32B, scalar1=-M23,
                                           scalar2=None, op0=ALU.add), inc=True)
        self.em.wait("gpsimd", self.em.tok("vector"))
        v(lambda v_: v_.tensor_scalar(out=d1B, in0=magB, scalar1=12.0, scalar2=14.0,
                                      op0=ALU.max, op1=ALU.min))
        v(lambda v_: v_.tensor_tensor(out=d1B, in0=d1B, in1=magB, op=ALU.add))
        v(lambda v_: v_.tensor_scalar(out=d2B, in0=magB, scalar1=14.0, scalar2=15.0,
                                      op0=ALU.max, op1=ALU.min))
        v(lambda v_: v_.tensor_scalar(out=d2B, in0=d2B, scalar1=3.0, scalar2=-62.0,
                                      op0=ALU.mult, op1=ALU.add))
        v(lambda v_: v_.tensor_tensor(out=d1B, in0=d1B, in1=d2B, op=ALU.add))
        v(lambda v_: v_.tensor_tensor(
            out=magB.rearrange("p (b k) -> p b k", k=16),
            in0=d1B.rearrange("p (b k) -> p b k", k=16), in1=sc3, op=ALU.mult),
          inc=True)
        if src_f32:
            shi = src.bitcast(U16).rearrange("p (k two) -> p k two", two=2) \
                [:, :, 1:2].rearrange("p k o -> p (k o)")
        else:
            shi = src.bitcast(U16)
        self.w8("vector", self.em.tok("gpsimd"))
        self.v(lambda v_: v_.tensor_scalar(out=dst.bitcast(U16), in0=shi,
                                           scalar1=0x8000, scalar2=None,
                                           op0=ALU.bitwise_and))
        self.v(lambda v_: v_.tensor_tensor(out=dst.bitcast(U16),
                                           in0=dst.bitcast(U16),
                                           in1=magB.bitcast(U16),
                                           op=ALU.bitwise_or), inc=True)
        self.em.wait("gpsimd", self.em.tok("vector"))

    def prep_overlay_init(self):
        tasks = []
        for nm_src, nm_dst in (("wg", "wgT"), ("wu", "wuT")):
            src = self.w[nm_src].rearrange("(c p) d -> c p d", p=128)
            dst = self.scr[nm_dst].rearrange("(c p) i -> p c i", p=128)
            for r in range(NI):
                tasks.append(dict(src=src[r], F=1024, cnt=8,
                                  dst=dst[:, :, r * 128:(r + 1) * 128]))
        srcd = self.w["wd"].rearrange("(c p) i -> c p i", p=128)
        dstd = self.scr["wdT"].rearrange("(c p) d -> p c d", p=128)
        for r in range(8):
            for lo, F in ((0, 1024), (1024, 1024), (2048, 768)):
                tasks.append(dict(
                    src=srcd[r][:, lo:lo + F], F=F, cnt=F // 128,
                    dst=dstd[:, lo // 128:(lo + F) // 128, r * 128:(r + 1) * 128]))
        self._ptasks = tasks
        self._ptok = {}
        self._pi = 0
        self._iv_queue = []
        self._iv_on = True
        self._iv_k = 1
        self._pring = [self.BC[:, 16384:17408], self.BB[:, 15360:16384],
                       self.MED2[:, 2048:3072]]
        self._prep_S1(0)
        self._prep_S1(1)
        self.bg_fill(0)

    def _pstage(self, i):
        return self.BC[:, 10240 + (i % 2) * 2048:10240 + (i % 2) * 2048 + 2048] \
            .bitcast(F32)

    def _pqdst(self, i, F):
        return self._pring[i % 3][:, 0:F]

    def _prep_S1(self, i):
        t = self._ptasks[i]
        F = t["F"]
        slot = self._pstage(i)[:, 0:F]
        prev = self._ptok.get(("o", i - 2), [])
        h = F // 2
        ta = self.dq(f"fstg{i % 2}a", "gpsimd").start(
            slot[:, 0:h], t["src"][:, 0:h], wait=prev)
        tb = self.dq(f"fstg{i % 2}b", "gpsimd").start(
            slot[:, h:F], t["src"][:, h:F], wait=prev)
        self._ptok[("l", i)] = [ta, tb]

    def _ptrp(self, i, F):
        stg0 = 10240 + (i % 2) * 2048
        return self.BC[:, stg0:stg0 + F]

    def _prep_batch(self, i):
        t = self._ptasks[i]
        F, cnt = t["F"], t["cnt"]
        qdst = self._pqdst(i, F)
        trp = self._ptrp(i, F)
        w = [self._ptok[("qv", i)], *self._ptok.get(("o", i - 2), [])]
        hc = (cnt + 1) // 2
        ta = self.dq("bgt0").start_t_batch(
            [(trp[:, c * 128:(c + 1) * 128], qdst[:, c * 128:(c + 1) * 128])
             for c in range(hc)], wait=w)
        tb = self.dq("bgt1").start_t_batch(
            [(trp[:, c * 128:(c + 1) * 128], qdst[:, c * 128:(c + 1) * 128])
             for c in range(hc, cnt)], wait=w)
        self._ptok[("t", i)] = [ta, tb]

    def _prep_store(self, i):
        t = self._ptasks[i]
        cnt = t["cnt"]
        trp3 = self._ptrp(i, t["F"]).rearrange("p (c q) -> p c q", q=128)
        tt = self._ptok[("t", i)]
        hc = (cnt + 1) // 2
        oa = self.dq("prepout0", "gpsimd").start(
            t["dst"][:, 0:hc, :], trp3[:, 0:hc], wait=[tt[0]])
        ob = self.dq("prepout1", "gpsimd").start(
            t["dst"][:, hc:cnt, :], trp3[:, hc:cnt], wait=[tt[1]])
        self._ptok[("o", i)] = [oa, ob]

    def _enqueue_task(self, i):
        q = self._iv_queue
        n = len(self._ptasks)
        q.extend(self._quant_ov_ops(i))
        q.append(lambda i=i: self._prep_batch(i))
        if i - 1 >= 0:
            q.append(lambda i=i: self._prep_store(i - 1))
        if 2 <= i + 1 < n:
            q.append(lambda i=i: self._prep_S1(i + 1))

    def bg_fill(self, k):
        if not hasattr(self, "_ptasks"):
            return
        n = len(self._ptasks)
        while len(self._iv_queue) < 48 and self._pi < n:
            self._enqueue_task(self._pi)
            self._pi += 1
        self._iv_on = False
        for _ in range(min(k, len(self._iv_queue))):
            self._iv_queue.pop(0)()
        self._iv_on = True

    def prep_drain(self):
        n = len(self._ptasks)
        self._iv_on = False
        while self._pi < n:
            self._enqueue_task(self._pi)
            self._pi += 1
        while self._iv_queue:
            self._iv_queue.pop(0)()
        self._prep_store(n - 1)
        for tk in self._ptok[("o", n - 1)]:
            self.w8("vector", tk)
        self.tok_ffnprep = [
            (self.dq("prepout0", "gpsimd"), self.dq("prepout0", "gpsimd").cnt),
            (self.dq("prepout1", "gpsimd"), self.dq("prepout1", "gpsimd").cnt)]

    def _quant_ov_ops(self, i):
        t = self._ptasks[i]
        F = t["F"]
        nb = F // 16
        QC = self.QCOL
        M23 = 8388608.0
        stg0 = 10240 + (i % 2) * 2048
        srcf = self._pstage(i)[:, 0:F]
        dst = self._pqdst(i, F)
        f32A = self.BC[:, 14336:14336 + 2 * F].bitcast(F32)
        f32B = srcf
        rB = self.BC[:, stg0:stg0 + F]
        d1B = self.BC[:, stg0 + F:stg0 + 2 * F]
        d2B = self.BC[:, 14336:14336 + F]
        magB = self.BC[:, 14336 + F:14336 + 2 * F]
        src3 = srcf.rearrange("p (b k) -> p b k", k=16)
        f32A3 = f32A.rearrange("p (b k) -> p b k", k=16)
        inv3 = QC[:, 176:176 + nb].rearrange("p (b o) -> p b o", o=1) \
            .broadcast_to((128, nb, 16))
        sc3 = QC[:, 352:352 + nb].rearrange("p (b o) -> p b o", o=1) \
            .broadcast_to((128, nb, 16))
        shi = srcf.bitcast(U16).rearrange("p (k two) -> p k two", two=2)[:, :, 1:2] \
            .rearrange("p k o -> p (k o)")
        U32 = mybir.dt.uint32
        ops = []

        def first():
            for tk in self._ptok.get(("l", i), []):
                self.w8("vector", tk)
            self.v(lambda v: v.tensor_reduce(
                out=QC[:, 0:nb].rearrange("p (b o) -> p b o", o=1), in_=src3,
                axis=AX.X, op=ALU.max, apply_absolute_value=True))
        ops.append(first)
        ops.append(lambda: self.v(lambda v: v.tensor_scalar(
            out=QC[:, 0:nb], in0=QC[:, 0:nb], scalar1=6e-12, scalar2=None,
            op0=ALU.max)))
        ops.append(lambda: self.v(lambda v: v.reciprocal(
            out=QC[:, 176:176 + nb], in_=QC[:, 0:nb])))
        ops.append(lambda: self.v(lambda v: v.tensor_scalar(
            out=QC[:, 176:176 + nb], in0=QC[:, 176:176 + nb], scalar1=12.0,
            scalar2=None, op0=ALU.mult)))
        ops.append(lambda: self.v(lambda v: v.tensor_scalar(
            out=QC[:, 352:352 + nb], in0=QC[:, 0:nb], scalar1=1.0 / 12.0,
            scalar2=None, op0=ALU.mult)))

        def sign():
            for tk in self._ptok.get(("t", i - 3), []):
                self.w8("vector", tk)
            self.v(lambda v: v.tensor_scalar(
                out=dst.bitcast(U16), in0=shi, scalar1=0x8000, scalar2=None,
                op0=ALU.bitwise_and))
        ops.append(sign)
        ops.append(lambda: self.v(lambda v: v.tensor_scalar(
            out=f32A.bitcast(U32), in0=srcf.bitcast(U32), scalar1=0x7FFFFFFF,
            scalar2=None, op0=ALU.bitwise_and)))
        ops.append(lambda: self.v(lambda v: v.tensor_tensor(
            out=f32A3, in0=f32A3, in1=inv3, op=ALU.mult)))
        ops.append(lambda: self.v(lambda v: v.tensor_scalar(
            out=f32B, in0=f32A, scalar1=8.0, scalar2=12.0, op0=ALU.max,
            op1=ALU.min)))
        ops.append(lambda: self.v(lambda v: v.scalar_tensor_tensor(
            out=f32B, in0=f32B, scalar=-0.75, in1=f32A, op0=ALU.mult,
            op1=ALU.add)))
        ops.append(lambda: self.v(lambda v: v.tensor_scalar(
            out=f32A, in0=f32A, scalar1=4.0, scalar2=8.0, op0=ALU.max,
            op1=ALU.min)))
        ops.append(lambda: self.v(lambda v: v.scalar_tensor_tensor(
            out=f32B, in0=f32A, scalar=-0.5, in1=f32B, op0=ALU.mult,
            op1=ALU.add)))
        ops.append(lambda: self.v(lambda v: v.tensor_scalar(
            out=f32A, in0=f32B, scalar1=16.0 + M23, scalar2=None, op0=ALU.add)))
        ops.append(lambda: self.v(lambda v: v.tensor_scalar(
            out=rB, in0=f32A, scalar1=-M23, scalar2=None, op0=ALU.add)))
        ops.append(lambda: self.v(lambda v: v.tensor_scalar(
            out=d1B, in0=rB, scalar1=12.0, scalar2=14.0, op0=ALU.max,
            op1=ALU.min)))
        ops.append(lambda: self.v(lambda v: v.tensor_scalar(
            out=d2B, in0=rB, scalar1=14.0, scalar2=15.0, op0=ALU.max,
            op1=ALU.min)))
        ops.append(lambda: self.v(lambda v: v.tensor_tensor(
            out=d1B, in0=d1B, in1=rB, op=ALU.add)))
        ops.append(lambda: self.v(lambda v: v.scalar_tensor_tensor(
            out=d1B, in0=d2B, scalar=3.0, in1=d1B, op0=ALU.mult, op1=ALU.add)))
        ops.append(lambda: self.v(lambda v: v.scalar_tensor_tensor(
            out=magB.rearrange("p (b k) -> p b k", k=16),
            in0=d1B.rearrange("p (b k) -> p b k", k=16), scalar=-62.0, in1=sc3,
            op0=ALU.add, op1=ALU.mult)))

        def last():
            self.v(lambda v: v.tensor_tensor(
                out=dst.bitcast(U16), in0=dst.bitcast(U16),
                in1=magB.bitcast(U16), op=ALU.bitwise_or), inc=True)
            self._ptok[("qv", i)] = self.em.tok("vector")
        ops.append(last)
        return ops

    def attn_b(self):
        em, CST, X, STAT = self.em, self.CST, self.X, self.STAT
        BA, BB, BC, MED, MED2 = self.BA, self.BB, self.BC, self.MED, self.MED2
        qkv_r = self.scr["qkv16"].rearrange("(c p) e -> p c e", p=128)
        wo_r64 = self.scr["wo16"].rearrange("(c p) d -> c p d", p=64)
        aob_r = self.scr["aob"].rearrange("(n p) d -> p n d", p=128)

        def qro(n):
            return BA[:, n * 512:(n + 1) * 512]

        def kro(n):
            return BB[:, n * 512:(n + 1) * 512]

        def vpk(kt, hh=None):
            base = 8192 + (kt // 2) * 520 + (hh * 65 if hh is not None else 0)
            buf = BB if kt % 2 == 0 else BA
            return buf[:, base:base + (65 if hh is not None else 520)]

        def aoT(hh):
            return BC[0:64, hh * 2048:(hh + 1) * 2048]
        qTh = MED2[:, 0:2048]
        kTh = MED2[:, 2048:4096]

        def pT(b):
            return MED[:, b * 1024:(b + 1) * 1024]
        xn_scr = BC[:, 16384:17408]
        ZB = BC[0:64, 16384:17408]
        ZR = BB[:, 14336:16384].bitcast(F32)[0:1, :]
        ZRB = BB[0:1, 13312:14336]
        self._aob_st = {}
        self._pend_epi = None
        self._pend_epi2 = None
        self._last_pv = None
        self._tok_repl = None
        self._tok_z = None
        for hp in range(2):
            self.w8("vector", self._aob_st.get(0))
            self.w8("vector", self._aob_st.get(1))
            _t0 = self.em.tok("tensor")
            _tt = {}
            for n in range(NT):
                xs = xn_scr if n % 2 == 0 else MED2[:, 0:1024]
                self.w8("vector", _tt.get(n - 2, _t0))
                tok_xn = self.ln_apply(n, xs)
                self.w8("tensor", tok_xn)
                dst = BC[:, 0:16384].rearrange("p (c q) -> p c q", q=2048)[:, :, n * 128:(n + 1) * 128]
                self.transpose8([xs[:, c * 128:(c + 1) * 128] for c in range(8)],
                                n % 4, dst, "vector")
                _tt[n] = self.em.tok("tensor")
            tok_xnT = self.em.tok("vector")

            def xnT(c):
                return BC[:, c * 2048:(c + 1) * 2048]
            for part in range(3):
                base_col = part * 1024 + hp * 512
                wsl = MED[:, 0:4096]
                _qg = [self.tok_cast["qkv16"], self.em.tok("tensor")]
                tok_w = self.dq("qkv").start(
                    wsl[:, 0:2048].rearrange("p (c q) -> p c q", q=512),
                    qkv_r[:, 0:4, base_col:base_col + 512], wait=_qg)
                tok_w2 = self.dq("qkv2").start(
                    wsl[:, 2048:4096].rearrange("p (c q) -> p c q", q=512),
                    qkv_r[:, 4:8, base_col:base_col + 512], wait=_qg)
                self.w8("tensor", tok_w)
                self.w8("tensor", tok_w2)
                self.w8("tensor", tok_xnT)
                for n in range(NT):
                    bk = n % 4
                    self.use_bank(bk)
                    for kc in range(8):
                        self.t(lambda t, bk=bk, kc=kc, n=n, wsl=wsl: t.matmul(
                            self.bankf(bk), xnT(kc)[:, n * 128:(n + 1) * 128],
                            wsl[:, kc * 512:(kc + 1) * 512],
                            start=(kc == 0), stop=(kc == 7)), inc=(kc == 7))
                    self.wrote_bank(bk)
                    self.read_bank(bk, "vector")
                    if part < 2:
                        dstt = qro(n) if part == 0 else kro(n)
                        ps4 = self.bankf(bk).rearrange("p (h s q) -> p h s q",
                                                       s=2, q=32)
                        cos4 = CST["ropec"][:, n * 32:(n + 1) * 32] \
                            .unsqueeze(1).unsqueeze(2).broadcast_to((128, 8, 2, 32))
                        sin3 = CST["ropes"][:, n * 32:(n + 1) * 32] \
                            .unsqueeze(1).broadcast_to((128, 8, 32))
                        t1 = BB[:, 14336:14848]
                        t2 = BB[:, 14848:15360]
                        t1v = t1.rearrange("p (h s q) -> p h s q", s=2, q=32)
                        t2v = t2.rearrange("p (h s q) -> p h s q", s=2, q=32)
                        dsv = dstt.rearrange("p (h s q) -> p h s q", s=2, q=32)
                        self.v(lambda v, t1v=t1v, ps4=ps4, cos4=cos4: v.tensor_tensor(
                            out=t1v, in0=ps4, in1=cos4, op=ALU.mult))
                        self.v(lambda v, t2v=t2v, ps4=ps4, sin3=sin3: v.tensor_tensor(
                            out=t2v[:, :, 0, :], in0=ps4[:, :, 1, :], in1=sin3,
                            op=ALU.mult))
                        self.v(lambda v, t2v=t2v, ps4=ps4, sin3=sin3: v.tensor_tensor(
                            out=t2v[:, :, 1, :], in0=ps4[:, :, 0, :], in1=sin3,
                            op=ALU.mult))
                        self.v(lambda v, dsv=dsv, t1v=t1v, t2v=t2v:
                               v.scalar_tensor_tensor(
                                   out=dsv[:, :, 0, :], in0=t2v[:, :, 0, :],
                                   scalar=-1.0, in1=t1v[:, :, 0, :],
                                   op0=ALU.mult, op1=ALU.add))
                        self.v(lambda v, dsv=dsv, t1v=t1v, t2v=t2v: v.tensor_tensor(
                            out=dsv[:, :, 1, :], in0=t1v[:, :, 1, :],
                            in1=t2v[:, :, 1, :], op=ALU.add), inc=True)
                    else:
                        self.v(lambda v, kt=n, bk=bk: v.tensor_copy(
                            out=vpk(kt).rearrange("p (h q) -> p h q", q=65)[:, :, 0:64],
                            in_=self.bankf(bk).rearrange("p (h q) -> p h q", q=64)))
                        self.v(lambda v, kt=n: v.memset(
                            vpk(kt).rearrange("p (h q) -> p h q", q=65)[:, :, 64:65],
                            1.0), inc=True)
                    self.bank_read_done(bk, "vector")
            tok_proj = self.em.tok("vector")
            for hh in range(8):
                for si, (srcf, dstf) in enumerate(((qro, qTh), (kro, kTh))):
                    self.w8("tensor", tok_proj)
                    for bi in range(2):
                        self.transpose8(
                            [srcf(nn)[:, hh * 64:(hh + 1) * 64]
                             for nn in range(bi * 8, bi * 8 + 8)],
                            2 + si * 2 + bi,
                            dstf[0:64, bi * 1024:(bi + 1) * 1024],
                            "vector", prow=64)
                tok_qkT = self.em.tok("vector")
                for qtc in range(2):
                    exp_toks = {}
                    pv_toks = {}
                    prev_pv = self._last_pv

                    def pv_mms(kt):
                        ptb = pT(kt % 3)
                        self.w8("tensor", exp_toks[kt])
                        for sub in range(2):
                            self.t(lambda t, kt=kt, sub=sub, ptb=ptb, hh=hh: t.matmul(
                                self.bankf(6 + sub)[0:65, :],
                                vpk(kt, hh),
                                ptb[:, sub * 512:(sub + 1) * 512],
                                start=(kt == 0), stop=(kt == NT - 1)),
                                inc=(sub == 1))
                        pv_toks[kt] = self.em.tok("tensor")
                    for mt in range(0, NT, 2):
                        for kt in (mt, mt + 1):
                            sb = 2 if kt % 2 == 0 else 4
                            self.use_bank(sb)
                            self.use_bank(sb + 1)
                            self.w8("tensor", tok_qkT)
                            for sub in range(2):
                                self.t(lambda t, sb=sb, sub=sub, kt=kt, qtc=qtc:
                                       t.matmul(
                                           self.bankf(sb + sub),
                                           kTh[0:64, kt * 128:(kt + 1) * 128],
                                           qTh[0:64, qtc * 1024 + sub * 512:
                                               qtc * 1024 + sub * 512 + 512],
                                           start=True, stop=True), inc=(sub == 1))
                            self.wrote_bank(sb)
                            self.wrote_bank(sb + 1)
                        if mt == 2:
                            if self._pend_epi is not None:
                                self._pend_epi()
                                self._pend_epi = None
                            self.use_bank(6)
                            self.use_bank(7)
                        if mt == 6 and self._pend_epi2 is not None:
                            self._pend_epi2()
                            self._pend_epi2 = None
                        if mt >= 2:
                            pv_mms(mt - 2)
                            pv_mms(mt - 1)
                        for kt in (mt, mt + 1):
                            sb = 2 if kt % 2 == 0 else 4
                            ptb = pT(kt % 3)
                            self.read_bank(sb, "scalar")
                            self.read_bank(sb + 1, "scalar")
                            if kt >= 3:
                                self.w8("scalar", pv_toks[kt - 3])
                            elif prev_pv is not None:
                                self.w8("scalar", prev_pv[{0: 15, 1: 13, 2: 14}[kt]])
                            self.s(lambda s, sb=sb, ptb=ptb: s.activation(
                                out=ptb, in_=self.PS[:, sb * 512:(sb + 2) * 512],
                                func=AF.Exp, scale=0.125), inc=True)
                            self.bank_read_done(sb, "scalar")
                            self.bank_read_done(sb + 1, "scalar")
                            exp_toks[kt] = self.em.tok("scalar")
                    pv_mms(NT - 2)
                    pv_mms(NT - 1)
                    self.wrote_bank(6)
                    self.wrote_bank(7)
                    self._last_pv = pv_toks

                    def epi(hh=hh, qtc=qtc):
                        pvb = BA[0:64, 12352:13376]
                        self.read_bank(6, "vector")
                        self.read_bank(7, "vector")
                        self.v(lambda v, pvb=pvb: v.tensor_copy(
                            out=pvb, in_=self.PS[0:64, 3072:4096]))
                        self.v(lambda v: v.tensor_copy(
                            out=ZR, in_=self.PS[64:65, 3072:4096]), inc=True)
                        self.bank_read_done(6, "vector")
                        self.bank_read_done(7, "vector")
                        self.w8("vector", self._tok_repl)
                        self.v(lambda v: v.reciprocal(out=ZR, in_=ZR))
                        self.v(lambda v: v.tensor_copy(out=ZRB, in_=ZR),
                               inc=True)
                        self._tok_z = self.em.tok("vector")

                        def epi2(hh=hh, qtc=qtc, pvb=pvb):
                            self.use_bank(0)
                            self.use_bank(1)
                            self.w8("tensor", self._tok_z)
                            for sub in range(2):
                                self.t(lambda t, sub=sub: t.matmul(
                                    self.bankf(0 + sub)[0:64, :],
                                    CST["ones_col"][:],
                                    ZRB[:, sub * 512:(sub + 1) * 512],
                                    start=True, stop=True), inc=(sub == 1))
                            self._tok_repl = self.em.tok("tensor")
                            self.wrote_bank(0)
                            self.wrote_bank(1)
                            self.read_bank(0, "vector")
                            self.read_bank(1, "vector")
                            self.v(lambda v: v.tensor_copy(
                                out=ZB, in_=self.PS[0:64, 0:1024]), inc=True)
                            self.bank_read_done(0, "vector")
                            self.bank_read_done(1, "vector")
                            self.v(lambda v, hh=hh, qtc=qtc, pvb=pvb:
                                   v.tensor_tensor(
                                       out=aoT(hh)[:, qtc * 1024:(qtc + 1) * 1024],
                                       in0=pvb, in1=ZB,
                                       op=ALU.mult), inc=True)
                        self._pend_epi2 = epi2
                    self._pend_epi = epi
            if self._pend_epi is not None:
                self._pend_epi()
                self._pend_epi = None
            if self._pend_epi2 is not None:
                self._pend_epi2()
                self._pend_epi2 = None
            wo_5d = self.scr["wo16"].rearrange(
                "(a c p) (hf q) -> a p c hf q", p=64, c=8, q=512)
            wosl = MED[:, 4096:8192]
            tok_ao = self.em.tok("vector")
            for half in range(2):
                tok_wo = self.dq("wo").start(
                    wosl[0:64, :].rearrange("p (c q) -> p c q", q=512),
                    wo_5d[hp, :, :, half],
                    wait=[self.tok_cast["wo16"], self.em.tok("tensor")])
                self.w8("tensor", tok_wo)
                self.w8("tensor", tok_ao)
                for n in range(NT):
                    bk = n % 2
                    self.use_bank(bk)
                    for hh in range(8):
                        self.t(lambda t, bk=bk, hh=hh, n=n: t.matmul(
                            self.bankf(bk),
                            aoT(hh)[:, n * 128:(n + 1) * 128],
                            wosl[0:64, hh * 512:(hh + 1) * 512],
                            start=(hh == 0), stop=(hh == 7)), inc=(hh == 7))
                    self.wrote_bank(bk)
                    self.read_bank(bk, "vector")
                    if hp == 0:
                        stage = BC[:, 16384 + bk * 512:16384 + bk * 512 + 512]
                        self.w8("vector", self._aob_st.get(bk))
                        self.v(lambda v, bk=bk, stage=stage: v.tensor_copy(
                            out=stage, in_=self.bankf(bk)), inc=True)
                        self.bank_read_done(bk, "vector")
                        self._aob_st[bk] = self.dq(f"aos{bk}").start(
                            aob_r[:, n, half * 512:half * 512 + 512], stage,
                            wait=[self.em.tok("vector")])
                    else:
                        lslot = BC[:, 16384 + bk * 512:16384 + bk * 512 + 512]
                        ltok = self.dq(f"aol{bk}").start(
                            lslot, aob_r[:, n, half * 512:half * 512 + 512],
                            wait=[self.em.tok("vector"), self._aob_st.get(0),
                                  self._aob_st.get(1)])
                        self.w8("vector", ltok)
                        self.resid_update(n, half, self.bankf(bk), extra=lslot)
                        self.bank_read_done(bk, "vector")


def kernel(L0_attn_a_out, L0_attn_a_W_write, L0_attn_a_W_read, L0_attn_a_W_beta,
           L0_attn_a_decay, L0_attn_b_Wqkv, L0_attn_b_Wo, L0_ffn_w_gate,
           L0_ffn_w_up, L0_ffn_w_down, _parts=4):
    consts = host_constants()
    nc = build_program(parts=_parts)
    decay_t = np.ascontiguousarray(
        np.tile(np.asarray(L0_attn_a_decay, dtype=np.float32)[None, :], (128, 1)))
    shared = dict(
        Ww=np.ascontiguousarray(L0_attn_a_W_write, dtype=np.float32),
        Wr=np.ascontiguousarray(L0_attn_a_W_read, dtype=np.float32),
        Wb=np.ascontiguousarray(L0_attn_a_W_beta, dtype=np.float32),
        decay_t=decay_t,
        Wqkv=np.ascontiguousarray(L0_attn_b_Wqkv, dtype=np.float32),
        Wo=np.ascontiguousarray(L0_attn_b_Wo, dtype=np.float32),
        wg=np.ascontiguousarray(L0_ffn_w_gate, dtype=np.float32),
        wu=np.ascontiguousarray(L0_ffn_w_up, dtype=np.float32),
        wd=np.ascontiguousarray(L0_ffn_w_down, dtype=np.float32),
        **consts,
    )
    xs = np.ascontiguousarray(L0_attn_a_out, dtype=np.float32)
    in_maps = [dict(x=xs[i], **shared) for i in range(8)]
    res = run_bass_kernel_spmd(nc, in_maps, core_ids=list(range(8)))
    return np.stack([res.results[i]["y"] for i in range(8)], axis=0)


# version 19
# speedup vs baseline: 1.2164x; 1.0239x over previous
import numpy as np
import ml_dtypes
import concourse.bass as bass
import concourse.mybir as mybir
from concourse.bass_utils import run_bass_kernel_spmd
from contextlib import ExitStack

F32 = mybir.dt.float32
BF16 = mybir.dt.bfloat16
I16 = mybir.dt.int16
U16 = mybir.dt.uint16
AF = mybir.ActivationFunctionType
ALU = mybir.AluOpType
AX = mybir.AxisListType

T = 2048
D = 1024
NT = 16
H = 8
HB = 16
DHB = 64
II = 2816
NI = 22
LN_EPS = 1e-5
ENGINES = ("tensor", "vector", "scalar", "gpsimd", "sync")
BFNP = np.dtype(ml_dtypes.bfloat16)


def host_constants():
    c = {}
    i_idx = np.arange(128, dtype=np.float32)[:, None]
    j_idx = np.arange(128, dtype=np.float32)[None, :]
    c["eye128"] = np.eye(128, dtype=np.float32).astype(BFNP)
    c["eye8"] = np.tile(np.eye(128, dtype=np.float32), (1, 8)).astype(BFNP)
    c["iota_dif"] = np.ascontiguousarray(np.broadcast_to(j_idx - i_idx, (128, 128)), dtype=np.float32)
    c["iota_colp1"] = np.ascontiguousarray(i_idx + 1.0, dtype=np.float32)
    c["iota_colrev"] = np.ascontiguousarray(127.0 - i_idx, dtype=np.float32)
    c["m_up_s_neg"] = np.ascontiguousarray(-(j_idx > i_idx).astype(np.float32))
    c["m_low_s_neg"] = np.ascontiguousarray(-(j_idx < i_idx).astype(np.float32))
    c["m_up_i"] = np.ascontiguousarray((j_idx >= i_idx).astype(np.float32))
    c["ones_col"] = np.ones((1, 64), dtype=np.float32).astype(BFNP)
    inv_freq = 1.0 / 10000.0 ** (np.arange(0, DHB, 2, dtype=np.float32) / DHB)
    t_pos = np.arange(T, dtype=np.float32)
    freqs = np.outer(t_pos, inv_freq)
    cosw = np.cos(freqs)
    sinw = np.sin(freqs)
    HD2 = DHB // 2
    c["ropec"] = np.ascontiguousarray(
        cosw.reshape(NT, 128, HD2).transpose(1, 0, 2).reshape(128, NT * HD2)).astype(BFNP)
    c["ropes"] = np.ascontiguousarray(
        sinw.reshape(NT, 128, HD2).transpose(1, 0, 2).reshape(128, NT * HD2)).astype(BFNP)
    return c


CONST_SPECS = [
    ("eye128", [128, 128], BF16), ("eye8", [128, 1024], BF16),
    ("iota_dif", [128, 128], F32),
    ("iota_colp1", [128, 1], F32), ("iota_colrev", [128, 1], F32),
    ("m_up_s_neg", [128, 128], F32), ("m_low_s_neg", [128, 128], F32),
    ("m_up_i", [128, 128], F32), ("ones_col", [1, 64], BF16),
    ("ropec", [128, NT * DHB // 2], BF16), ("ropes", [128, NT * DHB // 2], BF16),
]


class DmaQ:
    def __init__(self, em, eng, name):
        self.em = em
        self.eng = eng
        self.h = em.ctx.enter_context(em.nc.semaphore(name))
        self.cnt = 0
        self.waited = {e: 0 for e in ENGINES}

    def start(self, out, in_, wait=()):
        for tk in wait:
            self.em.wait(self.eng, tk)
        if self.cnt > 0 and self.waited[self.eng] < self.cnt:
            self.wait(self.eng, (self, self.cnt))
        self.cnt += 16
        h = self.h

        def fn(e, out=out, in_=in_):
            e.dma_start(out=out, in_=in_).then_inc(h, 16)
        self.em.q[self.eng].append(fn)
        self.em.ninstr += 1
        return (self, self.cnt)

    def start_t(self, out, in_, wait=()):
        for tk in wait:
            self.em.wait(self.eng, tk)
        if self.cnt > 0 and self.waited[self.eng] < self.cnt:
            self.wait(self.eng, (self, self.cnt))
        self.cnt += 16
        h = self.h

        def fn(e, out=out, in_=in_):
            e.dma_start_transpose(out=out, in_=in_).then_inc(h, 16)
        self.em.q[self.eng].append(fn)
        self.em.ninstr += 1
        return (self, self.cnt)

    def start_t_batch(self, pairs, wait=()):
        for tk in wait:
            self.em.wait(self.eng, tk)
        if self.cnt > 0 and self.waited[self.eng] < self.cnt:
            self.wait(self.eng, (self, self.cnt))
        h = self.h
        for out, in_ in pairs:
            self.cnt += 16

            def fn(e, out=out, in_=in_):
                e.dma_start_transpose(out=out, in_=in_).then_inc(h, 16)
            self.em.q[self.eng].append(fn)
            self.em.ninstr += 1
        return (self, self.cnt)

    def wait(self, eng, token):
        q, val = token
        assert q is self
        if val <= self.waited[eng]:
            return
        self.waited[eng] = val
        self.em.q[eng].append(lambda e, h=self.h, val=val: e.wait_ge(h, val))


class Em:

    def __init__(self, nc, ctx):
        self.nc = nc
        self.ctx = ctx
        self.q = {e: [] for e in ENGINES}
        self.sem = {e: ctx.enter_context(nc.semaphore(f"c_{e}")) for e in ENGINES}
        self.cnt = {e: 0 for e in ENGINES}
        self.waited = {e: {} for e in ENGINES}
        self.ninstr = 0

    AUTOSYNC = ("vector", "scalar", "gpsimd")

    def emit(self, eng, fn, inc=False):
        self.ninstr += 1
        if eng in self.AUTOSYNC:
            prev = self.cnt[eng]
            sem = self.sem[eng]
            if prev > self.waited[eng].get(eng, 0):
                self.waited[eng][eng] = prev
                self.q[eng].append(lambda e, sem=sem, prev=prev: e.wait_ge(sem, prev))
                self.ninstr += 1
            inc = True
        if inc:
            self.cnt[eng] += 1
            sem = self.sem[eng]

            def wrapped(e, fn=fn, sem=sem):
                fn(e).then_inc(sem, 1)
            self.q[eng].append(wrapped)
        else:
            self.q[eng].append(fn)
        return (eng, self.cnt[eng])

    def tok(self, eng):
        return (eng, self.cnt[eng])

    def wait(self, eng, token):
        if token is None:
            return
        src, val = token
        if isinstance(src, DmaQ):
            src.wait(eng, token)
            return
        if val <= 0:
            return
        if val <= self.waited[eng].get(src, 0):
            return
        self.waited[eng][src] = val
        sem = self.sem[src]
        self.q[eng].append(lambda e, sem=sem, val=val: e.wait_ge(sem, val))

    def bump(self, eng):
        self.cnt[eng] += 1
        sem = self.sem[eng]
        self.q[eng].append(lambda e, sem=sem: e.drain().then_inc(sem, 1))
        self.ninstr += 1
        return (eng, self.cnt[eng])

    def checkpoint(self):
        toks = [self.bump(e) for e in ENGINES]
        for e in ENGINES:
            for tk in toks:
                self.wait(e, tk)

    def play(self):
        with self.nc.Block() as blk:
            for eng in ENGINES:
                ops = self.q[eng]

                def body(e, ops=ops):
                    for fn in ops:
                        fn(e)
                getattr(blk, eng)(body)


def build_program(parts=4):
    nc = bass.Bass("TRN2", target_bir_lowering=False)
    ctx = ExitStack()

    x_in = nc.dram_tensor("x", [T, D], F32, kind="ExternalInput").ap()
    w = {
        "Ww": nc.dram_tensor("Ww", [D, D], F32, kind="ExternalInput").ap(),
        "Wr": nc.dram_tensor("Wr", [D, D], F32, kind="ExternalInput").ap(),
        "Wb": nc.dram_tensor("Wb", [H, D], F32, kind="ExternalInput").ap(),
        "decay_t": nc.dram_tensor("decay_t", [128, H], F32, kind="ExternalInput").ap(),
        "Wqkv": nc.dram_tensor("Wqkv", [D, 3 * D], F32, kind="ExternalInput").ap(),
        "Wo": nc.dram_tensor("Wo", [D, D], F32, kind="ExternalInput").ap(),
        "wg": nc.dram_tensor("wg", [II, D], F32, kind="ExternalInput").ap(),
        "wu": nc.dram_tensor("wu", [II, D], F32, kind="ExternalInput").ap(),
        "wd": nc.dram_tensor("wd", [D, II], F32, kind="ExternalInput").ap(),
    }
    consts_d = {}
    for name, shape, dt_ in CONST_SPECS:
        consts_d[name] = nc.dram_tensor(name, shape, dt_, kind="ExternalInput").ap()
    y_out = nc.dram_tensor("y", [T, D], F32, kind="ExternalOutput").ap()
    scr = {
        "wgT": nc.dram_tensor("wgT_scr", [D, II], BF16, kind="Internal").ap(),
        "wuT": nc.dram_tensor("wuT_scr", [D, II], BF16, kind="Internal").ap(),
        "wdT": nc.dram_tensor("wdT_scr", [II, D], BF16, kind="Internal").ap(),
        "wg16": nc.dram_tensor("wg16_scr", [II, D], BF16, kind="Internal").ap(),
        "wu16": nc.dram_tensor("wu16_scr", [II, D], BF16, kind="Internal").ap(),
        "wd16": nc.dram_tensor("wd16_scr", [D, II], BF16, kind="Internal").ap(),
        "qkv16": nc.dram_tensor("qkv16_scr", [D, 3 * D], BF16, kind="Internal").ap(),
        "wo16": nc.dram_tensor("wo16_scr", [D, D], BF16, kind="Internal").ap(),
        "aob": nc.dram_tensor("aob_scr", [T, D], BF16, kind="Internal").ap(),
    }

    def sbuf(name, shape, dt_):
        return ctx.enter_context(nc.sbuf_tensor(name, shape, dt_))

    env = dict(nc=nc, x_in=x_in, w=w, consts_d=consts_d, y_out=y_out, scr=scr)
    env["X"] = sbuf("xres", [128, NT * D], F32)
    env["BA"] = sbuf("bigA", [128, 16384], BF16)
    env["BB"] = sbuf("bigB", [128, 16384], BF16)
    env["BC"] = sbuf("bigC", [128, 17408], BF16)
    env["MED"] = sbuf("med", [128, 8192], BF16)
    env["MED2"] = sbuf("med2", [128, 4096], BF16)
    env["STAT"] = sbuf("stat", [128, 96], F32)
    env["COLS"] = sbuf("cols", [128, 128], F32)
    env["QCOL"] = sbuf("qcol", [128, 528], F32)
    env["HC"] = sbuf("hcols", [128, 64], F32)
    env["MASKU"] = sbuf("masku", [128, 1024], BF16)
    env["MASKL"] = sbuf("maskl", [128, 1024], BF16)
    env["MASKA"] = sbuf("maska", [128, 1024], BF16)
    env["DEALL"] = sbuf("deall", [128, 1024], BF16)
    env["CTDE"] = sbuf("ctde", [128, 1024], BF16)
    env["SST"] = sbuf("Sstate", [128, 1024], BF16)
    env["CST"] = {name: sbuf("c_" + name, shape, dt_) for name, shape, dt_ in CONST_SPECS}
    env["PS"] = ctx.enter_context(nc.psum_tensor("PS", [128, 4096], F32))

    em = Em(nc, ctx)
    env["em"] = em

    b = Builder(env)
    b.emit_all(parts)
    em.play()
    ctx.close()
    nc._n_instr = em.ninstr
    return nc


class Builder:
    def __init__(self, env):
        self.__dict__.update(env)
        self.bank_w = {}
        self.bank_r = {}
        self._dqs = {}

    def dq(self, name, eng="sync"):
        if name not in self._dqs:
            self._dqs[name] = DmaQ(self.em, eng, "dq_" + name)
        return self._dqs[name]

    def t(self, fn, inc=False):
        return self.em.emit("tensor", fn, inc)

    def v(self, fn, inc=False):
        ivq = getattr(self, "_iv_queue", None)
        if ivq and self._iv_on:
            self._iv_on = False
            for _ in range(min(self._iv_k, len(ivq))):
                ivq.pop(0)()
            self._iv_on = True
        return self.em.emit("vector", fn, inc)

    def v_crit(self, fn, inc=False):
        return self.em.emit("vector", fn, inc)

    def s(self, fn, inc=False):
        return self.em.emit("scalar", fn, inc)

    def w8(self, eng, tok):
        self.em.wait(eng, tok)

    def bankf(self, i, ncols=512):
        return self.PS[:, 512 * i:512 * i + ncols]

    def bankb(self, i, ncols=1024):
        return self.PS[:].bitcast(BF16)[:, 1024 * i:1024 * i + ncols]

    def use_bank(self, i):
        self.w8("tensor", self.bank_r.get(i))

    def wrote_bank(self, i):
        self.bank_w[i] = self.em.tok("tensor")

    def read_bank(self, i, eng):
        self.w8(eng, self.bank_w.get(i))

    def bank_read_done(self, i, eng):
        self.bank_r[i] = self.em.tok(eng)

    def transpose8(self, srcs, bank, dst, evac_eng="vector", prow=128):
        n = len(srcs)
        self.use_bank(bank)
        for i, sap in enumerate(srcs):
            self.t(lambda t, sap=sap, i=i: t.transpose(
                self.bankb(bank)[0:prow, i * 128:(i + 1) * 128], sap,
                self.CST["eye128"][:]), inc=(i == n - 1))
        self.wrote_bank(bank)
        self.read_bank(bank, evac_eng)
        src_ap = self.bankb(bank)[0:prow, 0:n * 128]
        if evac_eng == "vector":
            self.v(lambda v, dst=dst, src_ap=src_ap: v.tensor_copy(out=dst, in_=src_ap),
                   inc=True)
        else:
            self.s(lambda s, dst=dst, src_ap=src_ap: s.copy(out=dst, in_=src_ap),
                   inc=True)
        self.bank_read_done(bank, evac_eng)
        return self.em.tok(evac_eng)

    def emit_all(self, parts):
        em = self.em
        self.load_x_and_consts()
        if parts >= 0.1:
            self.precast_weights()
        if parts >= 0.2:
            self.attn_a_weight_prep()
        if parts >= 0.3:
            self.decay_consts()
        if parts >= 0.1:
            em.checkpoint()
        if parts >= 0.6:
            self.ln_prologue(first=True)
        if parts >= 1:
            if parts >= 1.5:
                self.prep_overlay_init()
            self.attn_a()
            em.checkpoint()
        if parts == 1.5:
            self.prep_drain()
            em.checkpoint()
        if parts >= 2:
            self.ln_prologue()
            self.ffn(first=True)
            em.checkpoint()
        if parts >= 3:
            self.ln_prologue()
            self.attn_b()
            em.checkpoint()
        if parts >= 4:
            self.ln_prologue()
            self.ffn(first=False)
            em.checkpoint()
        em.checkpoint()
        yr = self.y_out.rearrange("(n p) d -> p n d", p=128)
        qy = self.dq("y")
        tok = qy.start(yr, self.X[:].rearrange("p (n d) -> p n d", d=D),
                       wait=[self.tok_xload])
        qy.wait("sync", tok)

    def load_x_and_consts(self):
        xr = self.x_in.rearrange("(n p) d -> p n d", p=128)
        self.tok_xload = self.dq("x").start(
            self.X[:].rearrange("p (n d) -> p n d", d=D), xr)
        for name, shape, dt_ in CONST_SPECS:
            self.tok_consts = self.dq("const").start(self.CST[name][:],
                                                     self.consts_d[name])

    def precast_weights(self):
        self.tok_cast = {}
        for src, dst in [("Wqkv", "qkv16"), ("Wo", "wo16")]:
            tok = self.dq("pre", "gpsimd").start(self.scr[dst], self.w[src])
        for dst in ("qkv16", "wo16"):
            self.tok_cast[dst] = tok

    def attn_a_weight_prep(self):
        self.tok_WwT = self._transpose_1024sq(self.w["Ww"], self.MED[:, 0:8192])
        self.tok_WrT = self._transpose_1024sq(self.w["Wr"], self.BC[:, 2048:10240])
        tok = self.dq("wb", "gpsimd").start(self.MED2[0:H, 0:D], self.w["Wb"],
                                            wait=[self.em.tok("tensor")])
        self.w8("tensor", tok)
        self.w8("tensor", self.tok_consts)
        self.use_bank(0)
        for cidx in range(8):
            self.t(lambda t, cidx=cidx: t.transpose(
                self.bankb(0)[:, cidx * 8:cidx * 8 + 8],
                self.MED2[0:H, cidx * 128:(cidx + 1) * 128],
                self.CST["eye128"][0:H, 0:H]), inc=(cidx == 7))
        self.wrote_bank(0)
        self.read_bank(0, "vector")
        self.v(lambda v: v.tensor_copy(out=self.MED2[:, 3584:3648],
                                       in_=self.bankb(0)[:, 0:64]), inc=True)
        self.bank_read_done(0, "vector")
        self.tok_wbT = self.em.tok("vector")

    def _transpose_1024sq(self, src_f32, dst):
        src_r = src_f32.rearrange("(c p) d -> c p d", p=128)
        entry = self.em.tok("tensor")
        tr = {-2: entry, -1: entry}
        for r in range(8):
            slot = self.MED2[:, (r % 2) * 1024:(r % 2) * 1024 + 1024]
            ltok = self.dq(f"stg{r % 2}", "gpsimd").start(slot, src_r[r],
                                                          wait=[tr.get(r - 2)])
            bank = r % 2
            self.w8("tensor", ltok)
            self.w8("tensor", self.tok_consts)
            self.use_bank(bank)
            for cdx in range(8):
                self.t(lambda t, slot=slot, bank=bank, cdx=cdx: t.transpose(
                    self.bankb(bank)[:, cdx * 128:(cdx + 1) * 128],
                    slot[:, cdx * 128:(cdx + 1) * 128],
                    self.CST["eye128"][:]), inc=(cdx == 7))
            tr[r] = self.em.tok("tensor")
            self.wrote_bank(bank)
            self.read_bank(bank, "vector")
            out = dst.rearrange("p (c q) -> p c q", q=1024)[:, :, r * 128:(r + 1) * 128]
            self.v(lambda v, bank=bank, out=out: v.tensor_copy(
                out=out, in_=self.bankb(bank)[:].rearrange("p (c q) -> p c q", q=128)),
                inc=True)
            self.bank_read_done(bank, "vector")
        return self.em.tok("vector")

    def decay_consts(self):
        CST = self.CST

        def hc(a, n=1):
            return self.HC[:, a:a + n]
        tokd = self.dq("decay", "gpsimd").start(hc(0, 8), self.w["decay_t"])
        self.w8("scalar", tokd)
        self.w8("scalar", self.tok_consts)
        stok = self.s(lambda s: s.activation(out=hc(8, 8), in_=hc(0, 8),
                                             func=AF.Sigmoid), inc=True)
        self.w8("scalar", stok)
        self.s(lambda s: s.activation(out=hc(16, 8), in_=hc(8, 8), func=AF.Ln), inc=True)
        self.w8("vector", self.em.tok("scalar"))
        self.v(lambda v: v.tensor_scalar(out=hc(24, 8), in0=hc(16, 8), scalar1=-1.0,
                                         scalar2=None, op0=ALU.mult), inc=True)
        self.w8("scalar", self.em.tok("vector"))
        for h in range(H):
            self.s(lambda s, h=h: s.activation(out=hc(32 + h), in_=CST["iota_colp1"][:],
                                               func=AF.Exp, scale=hc(16 + h)))
            self.s(lambda s, h=h: s.activation(out=hc(48 + h), in_=CST["iota_colrev"][:],
                                               func=AF.Exp, scale=hc(16 + h)))
            self.s(lambda s, h=h: s.activation(out=hc(56 + h), in_=hc(16 + h),
                                               func=AF.Exp, scale=128.0), inc=(h == H - 1))
        self.w8("vector", self.em.tok("scalar"))
        self.v(lambda v: v.tensor_scalar(out=hc(40, 8), in0=hc(32, 8), scalar1=-1.0,
                                         scalar2=None, op0=ALU.mult), inc=True)
        e1 = self.BA[:, 0:256].bitcast(F32)
        e2 = self.BA[:, 256:512].bitcast(F32)
        for h in range(H):
            sl = slice(h * 128, (h + 1) * 128)
            self.w8("scalar", self.em.tok("vector"))
            self.s(lambda s, h=h, e1=e1: s.activation(out=e1, in_=CST["iota_dif"][:],
                                                      func=AF.Exp, scale=hc(16 + h)))
            self.s(lambda s, h=h, e2=e2: s.activation(out=e2, in_=CST["iota_dif"][:],
                                                      func=AF.Exp, scale=hc(24 + h)))
            self.w8("vector", self.em.tok("scalar"))
            self.v(lambda v, h=h: v.tensor_tensor(
                out=self.COLS[:, 120:121], in0=hc(16 + h),
                in1=CST["iota_colp1"][:], op=ALU.mult), inc=True)
            self.w8("scalar", self.em.tok("vector"))
            self.s(lambda s, h=h, sl=sl: s.activation(out=self.DEALL[:, sl],
                                                      in_=CST["iota_dif"][:],
                                                      func=AF.Exp, scale=hc(16 + h),
                                                      bias=self.COLS[:, 120:121]),
                   inc=True)
            self.w8("vector", self.em.tok("scalar"))
            self.v(lambda v, sl=sl, e1=e1: v.tensor_tensor(
                out=self.MASKU[:, sl], in0=e1, in1=CST["m_up_s_neg"][:], op=ALU.mult))
            self.v(lambda v, sl=sl, e1=e1: v.tensor_tensor(
                out=self.MASKA[:, sl], in0=e1, in1=CST["m_up_i"][:], op=ALU.mult))
            self.v(lambda v, sl=sl, e2=e2: v.tensor_tensor(
                out=self.MASKL[:, sl], in0=e2, in1=CST["m_low_s_neg"][:], op=ALU.mult))
            self.v(lambda v, sl=sl, h=h: v.tensor_scalar(
                out=self.CTDE[:, sl], in0=CST["eye128"][:],
                scalar1=hc(56 + h), scalar2=None, op0=ALU.mult), inc=True)
        self.v(lambda v: v.memset(self.SST[:], 0.0))
        self.v(lambda v: v.memset(self.STAT[:, 64:80], 0.0), inc=True)
        self.tok_dk = self.em.tok("vector")

    def ln_prologue(self, first=False):
        X, STAT = self.X, self.STAT
        scrf = self.BA[:, 0:2048].bitcast(F32)
        if first:
            self.w8("vector", self.tok_xload)
            for n in range(NT):
                self.v(lambda v, n=n: v.tensor_reduce(
                    out=STAT[:, n:n + 1], in_=X[:, n * D:(n + 1) * D],
                    axis=AX.X, op=ALU.add), inc=(n == NT - 1))
        else:
            self.v(lambda v: v.tensor_tensor(out=STAT[:, 0:16], in0=STAT[:, 0:16],
                                             in1=STAT[:, 64:80], op=ALU.add), inc=True)
        self.w8("scalar", self.em.tok("vector"))
        self.w8("scalar", self.tok_xload)
        for n in range(NT):
            stok = self.s(lambda s, n=n: s.activation(
                out=scrf, in_=X[:, n * D:(n + 1) * D], func=AF.Square,
                accum_out=STAT[:, 16 + n:17 + n]), inc=True)
            self.w8("scalar", stok)
        self.w8("vector", self.em.tok("scalar"))
        self.v(lambda v: v.tensor_scalar(out=STAT[:, 32:48], in0=STAT[:, 0:16],
                                         scalar1=-1.0 / D, scalar2=None, op0=ALU.mult))
        self.v(lambda v: v.tensor_scalar(out=STAT[:, 64:80], in0=STAT[:, 16:32],
                                         scalar1=1.0 / D, scalar2=None, op0=ALU.mult))
        self.v(lambda v: v.tensor_tensor(out=STAT[:, 80:96], in0=STAT[:, 32:48],
                                         in1=STAT[:, 32:48], op=ALU.mult))
        self.v(lambda v: v.scalar_tensor_tensor(out=STAT[:, 80:96], in0=STAT[:, 80:96],
                                                scalar=-1.0, in1=STAT[:, 64:80],
                                                op0=ALU.mult, op1=ALU.add))
        self.v(lambda v: v.tensor_scalar(out=STAT[:, 80:96], in0=STAT[:, 80:96],
                                         scalar1=LN_EPS, scalar2=None, op0=ALU.add),
               inc=True)
        self.w8("scalar", self.em.tok("vector"))
        stok = self.s(lambda s: s.activation(out=STAT[:, 64:80], in_=STAT[:, 80:96],
                                             func=AF.Ln), inc=True)
        self.w8("scalar", stok)
        self.s(lambda s: s.activation(out=STAT[:, 48:64], in_=STAT[:, 64:80],
                                      func=AF.Exp, scale=-0.5), inc=True)
        self.tok_ln = self.em.tok("scalar")
        self.w8("vector", self.tok_ln)
        self.v(lambda v: v.memset(self.STAT[:, 0:16], 0.0))
        self.v(lambda v: v.memset(self.STAT[:, 64:80], 0.0), inc=True)

    def ln_apply(self, n, out_ap):
        self.v(lambda v, n=n, out_ap=out_ap: v.tensor_scalar(
            out=out_ap, in0=self.X[:, n * D:(n + 1) * D],
            scalar1=self.STAT[:, 32 + n:33 + n], scalar2=self.STAT[:, 48 + n:49 + n],
            op0=ALU.add, op1=ALU.mult), inc=True)
        return self.em.tok("vector")

    def resid_update(self, n, half, in0_ap, extra=None):
        xs = self.X[:, n * D + half * 512:n * D + half * 512 + 512]
        acc_col = (n) if half == 0 else (64 + n)
        acc = self.STAT[:, acc_col:acc_col + 1]
        if extra is not None:
            scx = self.BB[:, 14336:15360].bitcast(F32)
            self.v(lambda v, in0_ap=in0_ap, extra=extra, scx=scx: v.tensor_tensor(
                out=scx, in0=in0_ap, in1=extra, op=ALU.add), inc=True)
            in0_ap = scx
        self.v(lambda v, in0_ap=in0_ap, xs=xs, acc=acc: v.scalar_tensor_tensor(
            out=xs, in0=in0_ap, scalar=1.0, in1=xs, op0=ALU.mult, op1=ALU.add,
            accum_out=acc), inc=True)

    def attn_a(self):
        em, CST = self.em, self.CST
        BA, BB, MED2, COLS, HC = self.BA, self.BB, self.MED2, self.COLS, self.HC
        X, SST = self.X, self.SST

        def slot(idx, p):
            return BA[:, (idx * 2 + p) * 1024:(idx * 2 + p) * 1024 + 1024]

        def bbs(a, n=1024):
            return BB[:, a:a + n]

        def kT(p):
            return bbs(p * 1024)

        def kbT(p):
            return bbs(2048 + p * 1024)

        def kdeT(p):
            return bbs(4096 + p * 1024)
        U8 = bbs(6144)
        UT8 = bbs(7168)
        AT8 = bbs(8192)
        P8 = bbs(9216)
        PT8 = bbs(10240)
        Bm = bbs(11264)
        kcdT8 = bbs(12288)
        vnew8 = bbs(13312)

        def ccol(p, a, n=8):
            return COLS[:, p * 64 + a:p * 64 + a + n]
        scrq = self.BC[:, 0:2048].bitcast(F32)

        self.w8("vector", self.tok_dk)
        aa_end = {}
        eTok = {}

        def make_E(n):
            p = n % 2
            xn_t = slot(0, p)
            xnT_t = MED2[:, p * 1024:(p + 1) * 1024]
            ge = aa_end.get(p)
            rn_prev = eTok.get(("rn", n - 1))

            def s1():
                if ge:
                    self.w8("vector", ge["t"])
                    self.w8("vector", ge["s"])
                tok_xn = self.ln_apply(n, xn_t)
                eTok[("ln", n)] = tok_xn
                self.w8("tensor", tok_xn)
                self.transpose8([xn_t[:, c * 128:(c + 1) * 128] for c in range(8)],
                                p, xnT_t, "vector")
                eTok[("xnT", n)] = self.em.tok("vector")

            def s2():
                self.use_bank(p)
                self.w8("tensor", self.tok_wbT)
                self.w8("tensor", eTok[("xnT", n)])
                for kc in range(8):
                    self.t(lambda t, kc=kc: t.matmul(
                        self.bankf(p, 8), xnT_t[:, kc * 128:(kc + 1) * 128],
                        self.MED2[:, 3584 + kc * 8:3592 + kc * 8],
                        start=(kc == 0), stop=(kc == 7)), inc=(kc == 7))
                self.wrote_bank(p)
                self.read_bank(p, "scalar")
                if ge:
                    self.w8("scalar", ge["t"])
                    self.w8("scalar", ge["v"])
                self.s(lambda s_: s_.activation(out=ccol(p, 0), in_=self.bankf(p, 8),
                                                func=AF.Sigmoid), inc=True)
                self.bank_read_done(p, "scalar")

            def s3():
                self.w8("scalar", eTok[("ln", n)])
                if rn_prev:
                    self.w8("scalar", rn_prev)
                self.s(lambda s_: s_.activation(out=scrq, in_=xn_t,
                                                func=AF.Square), inc=True)
                self.w8("vector", self.em.tok("scalar"))
                self.v(lambda v: v.tensor_reduce(
                    out=ccol(p, 40).rearrange("p (h o) -> p h o", o=1),
                    in_=scrq.rearrange("p (h q) -> p h q", q=128), axis=AX.X,
                    op=ALU.add))
                self.v(lambda v: v.tensor_scalar(
                    out=ccol(p, 40), in0=ccol(p, 40), scalar1=1e-24, scalar2=None,
                    op0=ALU.max), inc=True)
                eTok[("rn", n)] = self.em.tok("vector")

            def s4():
                self.w8("scalar", eTok[("rn", n)])
                stok = self.s(lambda s_: s_.activation(out=ccol(p, 48),
                                                       in_=ccol(p, 40),
                                                       func=AF.Ln), inc=True)
                self.w8("scalar", stok)
                self.s(lambda s_: s_.activation(out=ccol(p, 8), in_=ccol(p, 48),
                                                func=AF.Exp, scale=-0.5), inc=True)
                eTok[("exp", n)] = self.em.tok("scalar")

            def s5():
                self.w8("vector", eTok[("exp", n)])
                self.v(lambda v: v.tensor_tensor(out=ccol(p, 16), in0=ccol(p, 8),
                                                 in1=ccol(p, 0), op=ALU.mult))
                self.v(lambda v: v.tensor_tensor(out=ccol(p, 24), in0=ccol(p, 16),
                                                 in1=HC[:, 40:48], op=ALU.mult))
                self.v(lambda v: v.tensor_tensor(out=ccol(p, 32), in0=ccol(p, 8),
                                                 in1=HC[:, 48:56], op=ALU.mult),
                       inc=True)

                def kbuild(dst, colap):
                    self.v(lambda v, dst=dst, colap=colap: v.tensor_tensor(
                        out=dst.rearrange("p (h q) -> p h q", q=128),
                        in0=xn_t.rearrange("p (h q) -> p h q", q=128),
                        in1=colap.rearrange("p (h o) -> p h o", o=1)
                            .broadcast_to((128, 8, 128)),
                        op=ALU.mult), inc=True)
                kbuild(slot(1, p), ccol(p, 8))
                kbuild(slot(2, p), ccol(p, 16))
                kbuild(slot(3, p), ccol(p, 24))
                kbuild(slot(4, p), ccol(p, 32))
                eTok[("kb", n)] = self.em.tok("vector")

            def s6():
                self.w8("tensor", eTok[("kb", n)])
                eTok[("kTe", n)] = self.transpose8(
                    [slot(1, p)[:, c * 128:(c + 1) * 128] for c in range(8)], p,
                    kT(p), "vector")
                self.v(lambda v: v.tensor_tensor(out=kdeT(p), in0=kT(p),
                                                 in1=self.DEALL[:], op=ALU.mult),
                       inc=True)
                eTok[("kdeT", n)] = self.em.tok("vector")
                self.bg_fill(4)

            def s7():
                eTok[("kbTe", n)] = self.transpose8(
                    [slot(2, p)[:, c * 128:(c + 1) * 128] for c in range(8)], p,
                    kbT(p), "scalar")

            def s8():
                self.w8("tensor", self.tok_WwT)
                for half in range(2):
                    self.use_bank(p)
                    for kc in range(8):
                        self.t(lambda t, kc=kc, half=half: t.matmul(
                            self.bankf(p), xnT_t[:, kc * 128:(kc + 1) * 128],
                            self.MED[:, kc * 1024 + half * 512:
                                      kc * 1024 + half * 512 + 512],
                            start=(kc == 0), stop=(kc == 7)), inc=(kc == 7))
                    self.wrote_bank(p)
                    self.read_bank(p, "vector")
                    self.v(lambda v, half=half: v.tensor_tensor(
                        out=slot(5, p)[:, half * 512:(half + 1) * 512]
                            .rearrange("p (h q) -> p h q", q=128),
                        in0=self.bankf(p).rearrange("p (h q) -> p h q", q=128),
                        in1=ccol(p, 4 * half, 4).rearrange("p (h o) -> p h o", o=1)
                            .broadcast_to((128, 4, 128)),
                        op=ALU.mult), inc=True)
                    self.bank_read_done(p, "vector")
                eTok[("vb", n)] = self.em.tok("vector")
                self.bg_fill(4)
            return [s1, s2, s3, s4, s5, s6, s7, s8]

        for seg in make_E(0):
            seg()
        for n in range(NT):
            p = n % 2
            eq = make_E(n + 1) if n + 1 < NT else []

            def ef(k=1):
                for _ in range(k):
                    if eq:
                        eq.pop(0)()

            def pf(i, h):
                return self.PS[:, 512 * i + h * 128:512 * i + h * 128 + 128]

            def pairf(i):
                return self.PS[:, 512 * i:512 * i + 1024]
            prs = [(2, kT, kbT), (4, kbT, kT), (6, kT, kT)]
            self.w8("tensor", eTok[("kTe", n)])
            self.w8("tensor", eTok[("kbTe", n)])
            for bk, lhs, rhs in prs:
                self.use_bank(bk)
                self.use_bank(bk + 1)
                for h in range(H):
                    self.t(lambda t, bk=bk, lhs=lhs, rhs=rhs, h=h, p=p: t.matmul(
                        pf(bk, h),
                        lhs(p)[:, h * 128:(h + 1) * 128],
                        rhs(p)[:, h * 128:(h + 1) * 128],
                        start=True, stop=True), inc=(h == H - 1))
                self.wrote_bank(bk)
                self.wrote_bank(bk + 1)
            for bk, dst, msk in [(2, U8, self.MASKU), (4, UT8, self.MASKL),
                                 (6, AT8, self.MASKA)]:
                self.read_bank(bk, "vector")
                self.read_bank(bk + 1, "vector")
                self.v(lambda v, bk=bk, dst=dst, msk=msk: v.tensor_tensor(
                    out=dst, in0=pairf(bk), in1=msk[:], op=ALU.mult), inc=True)
                self.bank_read_done(bk, "vector")
                self.bank_read_done(bk + 1, "vector")
            self.v(lambda v: v.tensor_tensor(out=Bm, in0=U8, in1=CST["eye8"][:],
                                             op=ALU.add), inc=True)
            tok_B0 = self.em.tok("vector")
            ef()
            curP, curPT = U8, UT8
            for j in range(1, 7):
                ef()
                self.bg_fill(6)
                self.use_bank(2)
                self.use_bank(3)
                self.w8("tensor", tok_B0)
                for h in range(H):
                    self.t(lambda t, h=h, curP=curP, curPT=curPT: t.matmul(
                        pf(2, h),
                        curP[:, h * 128:(h + 1) * 128],
                        curPT[:, h * 128:(h + 1) * 128],
                        start=True, stop=True), inc=(h == H - 1))
                self.wrote_bank(2)
                self.wrote_bank(3)
                if j < 6:
                    self.use_bank(4)
                    self.use_bank(5)
                    for h in range(H):
                        self.t(lambda t, h=h, curP=curP, curPT=curPT: t.matmul(
                            pf(4, h),
                            curPT[:, h * 128:(h + 1) * 128],
                            curP[:, h * 128:(h + 1) * 128],
                            start=True, stop=True), inc=(h == H - 1))
                    self.wrote_bank(4)
                    self.wrote_bank(5)
                self.read_bank(2, "scalar")
                self.read_bank(3, "scalar")
                self.w8("scalar", self.em.tok("tensor"))
                self.s(lambda s: s.copy(out=PT8, in_=pairf(2)), inc=True)
                self.bank_read_done(2, "scalar")
                self.bank_read_done(3, "scalar")
                if j < 6:
                    self.read_bank(4, "scalar")
                    self.read_bank(5, "scalar")
                    self.s(lambda s: s.copy(out=P8, in_=pairf(4)), inc=True)
                    self.bank_read_done(4, "scalar")
                    self.bank_read_done(5, "scalar")
                tok_P = self.em.tok("scalar")
                self.use_bank(6)
                self.use_bank(7)
                self.w8("tensor", tok_P)
                for h in range(H):
                    self.t(lambda t, h=h: t.matmul(
                        pf(6, h),
                        PT8[:, h * 128:(h + 1) * 128],
                        Bm[:, h * 128:(h + 1) * 128],
                        start=True, stop=True), inc=(h == H - 1))
                self.wrote_bank(6)
                self.wrote_bank(7)
                self.read_bank(6, "vector")
                self.read_bank(7, "vector")
                self.v(lambda v: v.tensor_tensor(out=Bm, in0=pairf(6), in1=Bm,
                                                 op=ALU.add), inc=True)
                self.bank_read_done(6, "vector")
                self.bank_read_done(7, "vector")
                tok_B0 = self.em.tok("vector")
                curP, curPT = P8, PT8
            self.use_bank(6)
            self.use_bank(7)
            self.w8("tensor", tok_B0)
            for h in range(H):
                self.t(lambda t, h=h, p=p: t.matmul(
                    pf(6, h),
                    slot(3, p)[:, h * 128:(h + 1) * 128],
                    Bm[:, h * 128:(h + 1) * 128],
                    start=True, stop=True), inc=(h == H - 1))
            self.wrote_bank(6)
            self.wrote_bank(7)
            self.read_bank(6, "scalar")
            self.read_bank(7, "scalar")
            self.s(lambda s: s.copy(out=kcdT8, in_=pairf(6)), inc=True)
            self.bank_read_done(6, "scalar")
            self.bank_read_done(7, "scalar")
            tok_kcd = self.em.tok("scalar")
            ef()
            for half in range(2):
                hs = list(range(4 * half, 4 * half + 4))
                self.use_bank(2)
                self.w8("tensor", tok_kcd)
                self.w8("tensor", eTok[("vb", n)])
                for h in hs:
                    q = (h % 4) * 128
                    self.t(lambda t, h=h, q=q, p=p: t.matmul(
                        self.bankf(2)[:, q:q + 128],
                        Bm[:, h * 128:(h + 1) * 128],
                        slot(5, p)[:, h * 128:(h + 1) * 128],
                        start=True, stop=False))
                    self.t(lambda t, h=h, q=q: t.matmul(
                        self.bankf(2)[:, q:q + 128],
                        kcdT8[:, h * 128:(h + 1) * 128],
                        SST[:, h * 128:(h + 1) * 128],
                        start=False, stop=True), inc=(h == hs[-1]))
                self.wrote_bank(2)
                self.read_bank(2, "vector")
                self.v(lambda v, half=half: v.tensor_copy(
                    out=vnew8[:, half * 512:(half + 1) * 512], in_=self.bankf(2)),
                    inc=True)
                self.bank_read_done(2, "vector")
                tok_vn = self.em.tok("vector")
                self.use_bank(3)
                self.w8("tensor", tok_vn)
                self.w8("tensor", eTok[("kdeT", n)])
                for h in hs:
                    q = (h % 4) * 128
                    self.t(lambda t, h=h, q=q, p=p: t.matmul(
                        self.bankf(3)[:, q:q + 128],
                        kdeT(p)[:, h * 128:(h + 1) * 128],
                        SST[:, h * 128:(h + 1) * 128],
                        start=True, stop=False))
                    self.t(lambda t, h=h, q=q: t.matmul(
                        self.bankf(3)[:, q:q + 128],
                        AT8[:, h * 128:(h + 1) * 128],
                        vnew8[:, h * 128:(h + 1) * 128],
                        start=False, stop=True), inc=(h == hs[-1]))
                self.wrote_bank(3)
                self.use_bank(4)
                for h in hs:
                    q = (h % 4) * 128
                    self.t(lambda t, h=h, q=q, p=p: t.matmul(
                        self.bankf(4)[:, q:q + 128],
                        slot(4, p)[:, h * 128:(h + 1) * 128],
                        vnew8[:, h * 128:(h + 1) * 128],
                        start=True, stop=False))
                    self.t(lambda t, h=h, q=q: t.matmul(
                        self.bankf(4)[:, q:q + 128],
                        self.CTDE[:, h * 128:(h + 1) * 128],
                        SST[:, h * 128:(h + 1) * 128],
                        start=False, stop=True), inc=(h == hs[-1]))
                self.wrote_bank(4)
                self.read_bank(3, "scalar")
                self.s(lambda s, half=half, p=p: s.copy(
                    out=slot(6, p)[:, half * 512:(half + 1) * 512],
                    in_=self.bankf(3)), inc=True)
                self.bank_read_done(3, "scalar")
                self.read_bank(4, "vector")
                self.v(lambda v, half=half: v.tensor_copy(
                    out=SST[:, half * 512:(half + 1) * 512], in_=self.bankf(4)),
                    inc=True)
                self.bank_read_done(4, "vector")
                ef()
                self.bg_fill(6)
            tok_o = self.em.tok("scalar")
            self.w8("tensor", tok_o)
            tok_oT = self.transpose8(
                [slot(6, p)[:, c * 128:(c + 1) * 128] for c in range(8)], 1 - p,
                slot(7, p), "vector")
            self.w8("tensor", self.tok_WrT)
            for half in range(2):
                bk = 5 + half
                self.use_bank(bk)
                self.w8("tensor", tok_oT)
                for kc in range(8):
                    self.t(lambda t, bk=bk, kc=kc, half=half, p=p: t.matmul(
                        self.bankf(bk), slot(7, p)[:, kc * 128:(kc + 1) * 128],
                        self.BC[:, 2048 + kc * 1024 + half * 512:
                                2048 + kc * 1024 + half * 512 + 512],
                        start=(kc == 0), stop=(kc == 7)), inc=(kc == 7))
                self.wrote_bank(bk)
                self.read_bank(bk, "vector")
                self.resid_update(n, half, self.bankf(bk),
                                  extra=slot(0, p)[:, half * 512:(half + 1) * 512])
                self.bank_read_done(bk, "vector")
            while eq:
                eq.pop(0)()
            aa_end[p] = dict(t=self.em.tok("tensor"), v=self.em.tok("vector"),
                             s=self.em.tok("scalar"))
            self.bg_fill(10)

    def ffn(self, first):
        if first:
            self.prep_drain()
        em, X, STAT = self.em, self.X, self.STAT
        BA, BB, BC, MED, MED2 = self.BA, self.BB, self.BC, self.MED, self.MED2

        def hgrp(i):
            return BC[:, i * II:(i + 1) * II]

        def xn_t(p):
            return BC[:, 11264 + p * 2048:11264 + p * 2048 + 2048].bitcast(F32)

        def xq_t(p):
            return BC[:, 15360 + p * 1024:15360 + p * 1024 + 1024]

        def xqT(c, i):
            return MED2[:, c * 512 + i * 128:c * 512 + i * 128 + 128]

        def hq_t(p):
            return BA[:, 0:II] if p == 0 else BB[:, 11264:11264 + II]

        def hqT_g(i):
            return BB[:, i * 2816:(i + 1) * 2816]

        def wslice(b):
            return MED[:, 0:8192] if b == 0 else BB[:, 0:8192]
        sil = BB[:, 14336:14848]
        wg_r = self.scr["wgT"].rearrange("(c p) i -> p c i", p=128)
        wu_r = self.scr["wuT"].rearrange("(c p) i -> p c i", p=128)
        wd_r = self.scr["wdT"].rearrange("(c p) d -> c p d", p=128)

        wdslot = [BA[:, 14080:15104], BA[:, 15104:16128], BB[:, 15360:16384]]
        g0t = em.tok("tensor")
        g0v = em.tok("vector")
        self._tok_gu = g0t
        self._tok_hqp = {0: g0t, 1: g0t}
        dtoks = {}
        wdq = {"q": 0}
        xn_pair = BC[:, 11264:15360].bitcast(F32)
        xq_pair = BC[:, 15360:17408]

        def emit_xq(grp, guard):
            tiles = list(range(grp * 4, grp * 4 + 4))
            for pi in range(2):
                self.w8("vector", guard)
                self.ln_apply(tiles[2 * pi], xn_t(0))
                self.ln_apply(tiles[2 * pi + 1], xn_t(1))
                self.quant_tile(xn_pair, xq_pair, 2048, src_f32=True)
                tok_xq = self.em.tok("vector")
                self.w8("tensor", tok_xq)
                for half in range(2):
                    i = 2 * pi + half
                    dst3 = MED2[:].rearrange("p (c q) -> p c q", q=512) \
                        [:, :, i * 128:(i + 1) * 128]
                    self.transpose8([xq_t(half)[:, c * 128:(c + 1) * 128]
                                     for c in range(8)],
                                    4 + (i % 2), dst3, "vector")
            self._tok_xqT = self.em.tok("vector")
            self._ffn_gtok = self.em.tok("tensor")

        def hquant(i, n):
            p = n % 2
            self.w8("vector", self._tok_hqp[p])
            self.quant_tile(hgrp(i), hq_t(p), II)
            self.w8("tensor", self.em.tok("vector"))
            for bi, lo in enumerate((0, 8, 16)):
                cnt = 8 if bi < 2 else 6
                self.transpose8(
                    [hq_t(p)[:, (lo + c) * 128:(lo + c + 1) * 128]
                     for c in range(cnt)],
                    4 + (bi % 2), hqT_g(i)[:, lo * 128:(lo + cnt) * 128], "vector")
            self._tok_hqp[p] = self.em.tok("tensor")

        def down_pass(pair):
            for li, i, n in pair:
                self.use_bank(2 * li)
                self.use_bank(2 * li + 1)
            self.w8("tensor", self.em.tok("vector"))
            for kc in range(NI):
                q = wdq["q"]
                slot = wdslot[q % 3]
                guard = [dtoks.get(q - 3), *self.tok_ffnprep]
                if q < 3:
                    guard += [g0t, g0v]
                dtok = self.dq(f"wd{q % 3}").start(slot, wd_r[kc], wait=guard)
                self.w8("tensor", dtok)
                for li, i, n in pair:
                    for half in range(2):
                        self.t(lambda t, li=li, i=i, kc=kc, half=half, slot=slot:
                               t.matmul(
                                   self.bankf(2 * li + half),
                                   hqT_g(i)[:, kc * 128:(kc + 1) * 128],
                                   slot[:, half * 512:(half + 1) * 512],
                                   start=(kc == 0), stop=(kc == NI - 1)),
                               inc=(li == pair[-1][0] and half == 1))
                dtoks[q] = self.em.tok("tensor")
                wdq["q"] = q + 1
            for li, i, n in pair:
                self.wrote_bank(2 * li)
                self.wrote_bank(2 * li + 1)

        def resid_pass(pair):
            for li, i, n in pair:
                for half in range(2):
                    bk = 2 * li + half
                    self.read_bank(bk, "vector")
                    self.resid_update(n, half, self.bankf(bk))
                    self.bank_read_done(bk, "vector")

        emit_xq(0, em.tok("tensor"))
        for grp in range(4):
            tiles = list(range(grp * 4, grp * 4 + 4))
            for nch in range(6):
                ncols = 512 if nch < 5 else 256
                wb = nch % 2
                wsl = wslice(wb)
                if wb == 0:
                    guard = [*self.tok_ffnprep, self._tok_gu]
                else:
                    guard = [*self.tok_ffnprep, self.em.tok("tensor")]
                tok_wg = self.dq(f"wsl{wb}").start(
                    wsl[:, 0:8 * ncols].rearrange("p (c q) -> p c q", q=ncols),
                    wg_r[:, :, nch * 512:nch * 512 + ncols], wait=guard)
                tok_wu = self.dq(f"wslu{wb}").start(
                    wsl[:, 4096:4096 + 8 * ncols].rearrange("p (c q) -> p c q", q=ncols),
                    wu_r[:, :, nch * 512:nch * 512 + ncols], wait=guard)
                self.w8("tensor", tok_wg)
                self.w8("tensor", tok_wu)
                self.w8("tensor", self._tok_xqT)
                for i, n in enumerate(tiles):
                    bkg, bku = 2 * i, 2 * i + 1
                    self.use_bank(bkg)
                    self.use_bank(bku)
                    for kc in range(8):
                        self.t(lambda t, bkg=bkg, kc=kc, i=i, wsl=wsl, ncols=ncols:
                               t.matmul(self.bankf(bkg, ncols), xqT(kc, i),
                                        wsl[:, kc * ncols:(kc + 1) * ncols],
                                        start=(kc == 0), stop=(kc == 7)))
                        self.t(lambda t, bku=bku, kc=kc, i=i, wsl=wsl, ncols=ncols:
                               t.matmul(self.bankf(bku, ncols), xqT(kc, i),
                                        wsl[:, 4096 + kc * ncols:4096 + (kc + 1) * ncols],
                                        start=(kc == 0), stop=(kc == 7)),
                               inc=(kc == 7))
                    self.wrote_bank(bkg)
                    self.wrote_bank(bku)
                    self.read_bank(bkg, "scalar")
                    self.w8("scalar", self.em.tok("vector"))
                    self.s(lambda s, bkg=bkg, ncols=ncols: s.activation(
                        out=sil[:, 0:ncols], in_=self.bankf(bkg, ncols),
                        func=AF.Silu), inc=True)
                    self.bank_read_done(bkg, "scalar")
                    self.read_bank(bku, "vector")
                    self.w8("vector", self.em.tok("scalar"))
                    self.v(lambda v, bku=bku, i=i, nch=nch, ncols=ncols:
                           v.tensor_tensor(
                               out=hgrp(i)[:, nch * 512:nch * 512 + ncols],
                               in0=self.bankf(bku, ncols), in1=sil[:, 0:ncols],
                               op=ALU.mult), inc=True)
                    self.bank_read_done(bku, "vector")
            self._tok_gu = self.em.tok("tensor")
            pair0 = [(0, 0, tiles[0]), (1, 1, tiles[1])]
            pair1 = [(0, 2, tiles[2]), (1, 3, tiles[3])]
            hquant(0, tiles[0])
            hquant(1, tiles[1])
            down_pass(pair0)
            hquant(2, tiles[2])
            hquant(3, tiles[3])
            resid_pass(pair0)
            down_pass(pair1)
            if grp < 3:
                emit_xq(grp + 1, self._ffn_gtok)
            resid_pass(pair1)

    def quant_tile(self, src, dst, F, src_f32=False, sbase=None, eng="vector",
                   qc=None):
        nb = F // 16
        ST, c0 = sbase if sbase is not None else (self.BA, 2816)
        f32A = ST[:, c0:c0 + 2 * F].bitcast(F32)
        f32B = ST[:, c0 + 2 * F:c0 + 4 * F].bitcast(F32)
        rB = ST[:, c0 + 2 * F:c0 + 3 * F]
        d1B = ST[:, c0 + 3 * F:c0 + 4 * F]
        d2B = ST[:, c0:c0 + F]
        magB = ST[:, c0 + F:c0 + 2 * F]
        habs = ST[:, c0 + 3 * F:c0 + 4 * F]
        QC = qc if qc is not None else self.QCOL

        def v(fn, inc=False):
            return self.em.emit(eng, fn, inc)
        src3 = src.rearrange("p (b k) -> p b k", k=16)
        self.v(lambda v_: v_.tensor_reduce(
            out=QC[:, 0:nb].rearrange("p (b o) -> p b o", o=1), in_=src3,
            axis=AX.X, op=ALU.max, apply_absolute_value=True))
        self.v(lambda v_: v_.tensor_scalar(out=QC[:, 0:nb], in0=QC[:, 0:nb],
                                           scalar1=6e-12, scalar2=None, op0=ALU.max))
        self.v(lambda v_: v_.reciprocal(out=QC[:, 176:176 + nb], in_=QC[:, 0:nb]))
        self.v(lambda v_: v_.tensor_scalar(out=QC[:, 176:176 + nb],
                                           in0=QC[:, 176:176 + nb],
                                           scalar1=12.0, scalar2=None, op0=ALU.mult))
        self.v(lambda v_: v_.tensor_scalar(out=QC[:, 352:352 + nb], in0=QC[:, 0:nb],
                                           scalar1=1.0 / 12.0, scalar2=None,
                                           op0=ALU.mult), inc=True)
        if eng != "vector":
            self.em.wait(eng, self.em.tok("vector"))
        if eng != "vector":
            assert src_f32
            self._quant_pool_body(src, dst, F, QC, ST, c0, nb, v)
            return
        inv3 = QC[:, 176:176 + nb].rearrange("p (b o) -> p b o", o=1) \
            .broadcast_to((128, nb, 16))
        sc3 = QC[:, 352:352 + nb].rearrange("p (b o) -> p b o", o=1) \
            .broadcast_to((128, nb, 16))
        if src_f32 and eng != "vector":
            v(lambda v_: v_.tensor_scalar(out=f32A, in0=src, scalar1=0.0,
                                          scalar2=2.0, op0=ALU.is_ge, op1=ALU.mult))
            v(lambda v_: v_.scalar_tensor_tensor(out=f32A, in0=f32A, scalar=-1.0,
                                                 in1=src, op0=ALU.add, op1=ALU.mult))
            v(lambda v_: v_.scalar_tensor_tensor(
                out=f32A.rearrange("p (b k) -> p b k", k=16),
                in0=f32A.rearrange("p (b k) -> p b k", k=16), scalar=12.0, in1=inv3,
                op0=ALU.mult, op1=ALU.mult))
        elif src_f32:
            v(lambda v_: v_.tensor_scalar(
                out=f32A.bitcast(mybir.dt.uint32), in0=src.bitcast(mybir.dt.uint32),
                scalar1=0x7FFFFFFF, scalar2=None, op0=ALU.bitwise_and))
            v(lambda v_: v_.tensor_tensor(
                out=f32A.rearrange("p (b k) -> p b k", k=16),
                in0=f32A.rearrange("p (b k) -> p b k", k=16), in1=inv3,
                op=ALU.mult))
        else:
            v(lambda v_: v_.tensor_scalar(out=habs.bitcast(U16), in0=src.bitcast(U16),
                                          scalar1=0x7FFF, scalar2=None,
                                          op0=ALU.bitwise_and))
            v(lambda v_: v_.scalar_tensor_tensor(
                out=f32A.rearrange("p (b k) -> p b k", k=16),
                in0=habs.rearrange("p (b k) -> p b k", k=16), scalar=1.0, in1=inv3,
                op0=ALU.mult, op1=ALU.mult))
        M23 = 8388608.0
        v(lambda v_: v_.tensor_scalar(out=f32B, in0=f32A, scalar1=8.0, scalar2=12.0,
                                      op0=ALU.max, op1=ALU.min))
        v(lambda v_: v_.scalar_tensor_tensor(out=f32B, in0=f32B, scalar=-0.75,
                                             in1=f32A, op0=ALU.mult, op1=ALU.add))
        v(lambda v_: v_.tensor_scalar(out=f32A, in0=f32A, scalar1=4.0, scalar2=8.0,
                                      op0=ALU.max, op1=ALU.min))
        v(lambda v_: v_.scalar_tensor_tensor(out=f32B, in0=f32A, scalar=-0.5,
                                             in1=f32B, op0=ALU.mult, op1=ALU.add))
        v(lambda v_: v_.tensor_scalar(out=f32A, in0=f32B, scalar1=16.0 + M23,
                                      scalar2=None, op0=ALU.add))
        v(lambda v_: v_.tensor_scalar(out=rB, in0=f32A, scalar1=-M23, scalar2=None,
                                      op0=ALU.add))
        v(lambda v_: v_.tensor_scalar(out=d1B, in0=rB, scalar1=12.0, scalar2=14.0,
                                      op0=ALU.max, op1=ALU.min))
        v(lambda v_: v_.tensor_scalar(out=d2B, in0=rB, scalar1=14.0, scalar2=15.0,
                                      op0=ALU.max, op1=ALU.min))
        v(lambda v_: v_.tensor_tensor(out=d1B, in0=d1B, in1=rB, op=ALU.add))
        v(lambda v_: v_.scalar_tensor_tensor(out=d1B, in0=d2B, scalar=3.0, in1=d1B,
                                             op0=ALU.mult, op1=ALU.add))
        v(lambda v_: v_.scalar_tensor_tensor(
            out=magB.rearrange("p (b k) -> p b k", k=16),
            in0=d1B.rearrange("p (b k) -> p b k", k=16), scalar=-62.0, in1=sc3,
            op0=ALU.add, op1=ALU.mult))
        if src_f32:
            shi = src.bitcast(U16).rearrange("p (k two) -> p k two", two=2)[:, :, 1:2] \
                .rearrange("p k o -> p (k o)")
        else:
            shi = src.bitcast(U16)
        if eng != "vector":
            self.w8("vector", self.em.tok(eng))
        self.v(lambda v_: v_.tensor_scalar(out=dst.bitcast(U16), in0=shi,
                                           scalar1=0x8000, scalar2=None,
                                           op0=ALU.bitwise_and))
        self.v(lambda v_: v_.tensor_tensor(out=dst.bitcast(U16),
                                           in0=dst.bitcast(U16),
                                           in1=magB.bitcast(U16),
                                           op=ALU.bitwise_or), inc=True)
        if eng != "vector":
            self.em.wait(eng, self.em.tok("vector"))

    def _quant_pool_body(self, src, dst, F, QC, ST, c0, nb, v, src_f32=True):
        f32A = ST[:, c0:c0 + 2 * F].bitcast(F32)
        f32B = ST[:, c0 + 2 * F:c0 + 4 * F].bitcast(F32)
        rB = ST[:, c0 + 2 * F:c0 + 3 * F]
        d1B = ST[:, c0 + 3 * F:c0 + 4 * F]
        d2B = ST[:, c0:c0 + F]
        magB = ST[:, c0 + F:c0 + 2 * F]
        inv3 = QC[:, 176:176 + nb].rearrange("p (b o) -> p b o", o=1) \
            .broadcast_to((128, nb, 16))
        sc3 = QC[:, 352:352 + nb].rearrange("p (b o) -> p b o", o=1) \
            .broadcast_to((128, nb, 16))
        M23 = 8388608.0
        v(lambda v_: v_.tensor_scalar(out=f32B, in0=src, scalar1=0.0, scalar2=2.0,
                                      op0=ALU.is_ge, op1=ALU.mult))
        v(lambda v_: v_.tensor_scalar(out=f32B, in0=f32B, scalar1=-1.0,
                                      scalar2=None, op0=ALU.add))
        v(lambda v_: v_.tensor_tensor(out=f32A, in0=f32B, in1=src, op=ALU.mult))
        v(lambda v_: v_.tensor_tensor(
            out=f32A.rearrange("p (b k) -> p b k", k=16),
            in0=f32A.rearrange("p (b k) -> p b k", k=16), in1=inv3, op=ALU.mult))
        v(lambda v_: v_.tensor_scalar(out=f32B, in0=f32A, scalar1=8.0, scalar2=12.0,
                                      op0=ALU.max, op1=ALU.min))
        v(lambda v_: v_.tensor_scalar(out=f32B, in0=f32B, scalar1=-0.75,
                                      scalar2=None, op0=ALU.mult))
        v(lambda v_: v_.tensor_tensor(out=f32B, in0=f32B, in1=f32A, op=ALU.add))
        v(lambda v_: v_.tensor_scalar(out=f32A, in0=f32A, scalar1=4.0, scalar2=8.0,
                                      op0=ALU.max, op1=ALU.min))
        v(lambda v_: v_.tensor_scalar(out=f32A, in0=f32A, scalar1=-0.5,
                                      scalar2=16.0, op0=ALU.mult, op1=ALU.add))
        v(lambda v_: v_.tensor_tensor(out=f32A, in0=f32A, in1=f32B, op=ALU.add),
          inc=True)
        self.w8("vector", self.em.tok("gpsimd"))
        self.v(lambda v_: v_.tensor_scalar(out=f32B, in0=f32A, scalar1=M23,
                                           scalar2=None, op0=ALU.add))
        self.v(lambda v_: v_.tensor_scalar(out=magB, in0=f32B, scalar1=-M23,
                                           scalar2=None, op0=ALU.add), inc=True)
        self.em.wait("gpsimd", self.em.tok("vector"))
        v(lambda v_: v_.tensor_scalar(out=d1B, in0=magB, scalar1=12.0, scalar2=14.0,
                                      op0=ALU.max, op1=ALU.min))
        v(lambda v_: v_.tensor_tensor(out=d1B, in0=d1B, in1=magB, op=ALU.add))
        v(lambda v_: v_.tensor_scalar(out=d2B, in0=magB, scalar1=14.0, scalar2=15.0,
                                      op0=ALU.max, op1=ALU.min))
        v(lambda v_: v_.tensor_scalar(out=d2B, in0=d2B, scalar1=3.0, scalar2=-62.0,
                                      op0=ALU.mult, op1=ALU.add))
        v(lambda v_: v_.tensor_tensor(out=d1B, in0=d1B, in1=d2B, op=ALU.add))
        v(lambda v_: v_.tensor_tensor(
            out=magB.rearrange("p (b k) -> p b k", k=16),
            in0=d1B.rearrange("p (b k) -> p b k", k=16), in1=sc3, op=ALU.mult),
          inc=True)
        if src_f32:
            shi = src.bitcast(U16).rearrange("p (k two) -> p k two", two=2) \
                [:, :, 1:2].rearrange("p k o -> p (k o)")
        else:
            shi = src.bitcast(U16)
        self.w8("vector", self.em.tok("gpsimd"))
        self.v(lambda v_: v_.tensor_scalar(out=dst.bitcast(U16), in0=shi,
                                           scalar1=0x8000, scalar2=None,
                                           op0=ALU.bitwise_and))
        self.v(lambda v_: v_.tensor_tensor(out=dst.bitcast(U16),
                                           in0=dst.bitcast(U16),
                                           in1=magB.bitcast(U16),
                                           op=ALU.bitwise_or), inc=True)
        self.em.wait("gpsimd", self.em.tok("vector"))

    def prep_overlay_init(self):
        tasks = []
        for nm_src, nm_dst in (("wg", "wgT"), ("wu", "wuT")):
            src = self.w[nm_src].rearrange("(c p) d -> c p d", p=128)
            dst = self.scr[nm_dst].rearrange("(c p) i -> p c i", p=128)
            for r in range(NI):
                tasks.append(dict(src=src[r], F=1024, cnt=8,
                                  dst=dst[:, :, r * 128:(r + 1) * 128]))
        srcd = self.w["wd"].rearrange("(c p) i -> c p i", p=128)
        dstd = self.scr["wdT"].rearrange("(c p) d -> p c d", p=128)
        for r in range(8):
            for lo, F in ((0, 1024), (1024, 1024), (2048, 768)):
                tasks.append(dict(
                    src=srcd[r][:, lo:lo + F], F=F, cnt=F // 128,
                    dst=dstd[:, lo // 128:(lo + F) // 128, r * 128:(r + 1) * 128]))
        self._ptasks = tasks
        self._ptok = {}
        self._pi = 0
        self._iv_queue = []
        self._iv_on = True
        self._iv_k = 1
        self._pring = [self.BC[:, 16384:17408], self.BB[:, 15360:16384],
                       self.MED2[:, 2048:3072]]
        self._prep_S1(0)
        self._prep_S1(1)
        self.bg_fill(0)

    def _pstage(self, i):
        return self.BC[:, 10240 + (i % 2) * 2048:10240 + (i % 2) * 2048 + 2048] \
            .bitcast(F32)

    def _pqdst(self, i, F):
        return self._pring[i % 3][:, 0:F]

    def _prep_S1(self, i):
        t = self._ptasks[i]
        F = t["F"]
        slot = self._pstage(i)[:, 0:F]
        prev = self._ptok.get(("o", i - 2), [])
        h = F // 2
        ta = self.dq(f"fstg{i % 2}a", "gpsimd").start(
            slot[:, 0:h], t["src"][:, 0:h], wait=prev)
        tb = self.dq(f"fstg{i % 2}b", "gpsimd").start(
            slot[:, h:F], t["src"][:, h:F], wait=prev)
        self._ptok[("l", i)] = [ta, tb]

    def _ptrp(self, i, F):
        stg0 = 10240 + (i % 2) * 2048
        return self.BC[:, stg0:stg0 + F]

    def _prep_batch(self, i):
        t = self._ptasks[i]
        F, cnt = t["F"], t["cnt"]
        qdst = self._pqdst(i, F)
        trp = self._ptrp(i, F)
        w = [self._ptok[("qv", i)], *self._ptok.get(("o", i - 2), [])]
        hc = (cnt + 1) // 2
        ta = self.dq("bgt0").start_t_batch(
            [(trp[:, c * 128:(c + 1) * 128], qdst[:, c * 128:(c + 1) * 128])
             for c in range(hc)], wait=w)
        tb = self.dq("bgt1").start_t_batch(
            [(trp[:, c * 128:(c + 1) * 128], qdst[:, c * 128:(c + 1) * 128])
             for c in range(hc, cnt)], wait=w)
        self._ptok[("t", i)] = [ta, tb]

    def _prep_store(self, i):
        t = self._ptasks[i]
        cnt = t["cnt"]
        trp3 = self._ptrp(i, t["F"]).rearrange("p (c q) -> p c q", q=128)
        tt = self._ptok[("t", i)]
        hc = (cnt + 1) // 2
        oa = self.dq("prepout0", "gpsimd").start(
            t["dst"][:, 0:hc, :], trp3[:, 0:hc], wait=[tt[0]])
        ob = self.dq("prepout1", "gpsimd").start(
            t["dst"][:, hc:cnt, :], trp3[:, hc:cnt], wait=[tt[1]])
        self._ptok[("o", i)] = [oa, ob]

    def _enqueue_task(self, i):
        q = self._iv_queue
        n = len(self._ptasks)
        q.extend(self._quant_ov_ops(i))
        q.append(lambda i=i: self._prep_batch(i))
        if i - 1 >= 0:
            q.append(lambda i=i: self._prep_store(i - 1))
        if 2 <= i + 1 < n:
            q.append(lambda i=i: self._prep_S1(i + 1))

    def bg_fill(self, k):
        if not hasattr(self, "_ptasks"):
            return
        n = len(self._ptasks)
        while len(self._iv_queue) < 48 and self._pi < n:
            self._enqueue_task(self._pi)
            self._pi += 1
        self._iv_on = False
        for _ in range(min(k, len(self._iv_queue))):
            self._iv_queue.pop(0)()
        self._iv_on = True

    def prep_drain(self):
        n = len(self._ptasks)
        self._iv_on = False
        while self._pi < n:
            self._enqueue_task(self._pi)
            self._pi += 1
        while self._iv_queue:
            self._iv_queue.pop(0)()
        self._prep_store(n - 1)
        for tk in self._ptok[("o", n - 1)]:
            self.w8("vector", tk)
        self.tok_ffnprep = [
            (self.dq("prepout0", "gpsimd"), self.dq("prepout0", "gpsimd").cnt),
            (self.dq("prepout1", "gpsimd"), self.dq("prepout1", "gpsimd").cnt)]

    def _quant_ov_ops(self, i):
        t = self._ptasks[i]
        F = t["F"]
        nb = F // 16
        QC = self.QCOL
        M23 = 8388608.0
        stg0 = 10240 + (i % 2) * 2048
        srcf = self._pstage(i)[:, 0:F]
        dst = self._pqdst(i, F)
        f32A = self.BC[:, 14336:14336 + 2 * F].bitcast(F32)
        f32B = srcf
        rB = self.BC[:, stg0:stg0 + F]
        d1B = self.BC[:, stg0 + F:stg0 + 2 * F]
        d2B = self.BC[:, 14336:14336 + F]
        magB = self.BC[:, 14336 + F:14336 + 2 * F]
        src3 = srcf.rearrange("p (b k) -> p b k", k=16)
        f32A3 = f32A.rearrange("p (b k) -> p b k", k=16)
        inv3 = QC[:, 176:176 + nb].rearrange("p (b o) -> p b o", o=1) \
            .broadcast_to((128, nb, 16))
        sc3 = QC[:, 352:352 + nb].rearrange("p (b o) -> p b o", o=1) \
            .broadcast_to((128, nb, 16))
        shi = srcf.bitcast(U16).rearrange("p (k two) -> p k two", two=2)[:, :, 1:2] \
            .rearrange("p k o -> p (k o)")
        U32 = mybir.dt.uint32
        ops = []

        def first():
            for tk in self._ptok.get(("l", i), []):
                self.w8("vector", tk)
            self.v(lambda v: v.tensor_reduce(
                out=QC[:, 0:nb].rearrange("p (b o) -> p b o", o=1), in_=src3,
                axis=AX.X, op=ALU.max, apply_absolute_value=True))
        ops.append(first)
        ops.append(lambda: self.v(lambda v: v.tensor_scalar(
            out=QC[:, 0:nb], in0=QC[:, 0:nb], scalar1=6e-12, scalar2=None,
            op0=ALU.max)))
        ops.append(lambda: self.v(lambda v: v.reciprocal(
            out=QC[:, 176:176 + nb], in_=QC[:, 0:nb])))
        ops.append(lambda: self.v(lambda v: v.tensor_scalar(
            out=QC[:, 176:176 + nb], in0=QC[:, 176:176 + nb], scalar1=12.0,
            scalar2=None, op0=ALU.mult)))
        ops.append(lambda: self.v(lambda v: v.tensor_scalar(
            out=QC[:, 352:352 + nb], in0=QC[:, 0:nb], scalar1=1.0 / 12.0,
            scalar2=None, op0=ALU.mult)))

        def sign():
            for tk in self._ptok.get(("t", i - 3), []):
                self.w8("vector", tk)
            self.v(lambda v: v.tensor_scalar(
                out=dst.bitcast(U16), in0=shi, scalar1=0x8000, scalar2=None,
                op0=ALU.bitwise_and))
        ops.append(sign)
        ops.append(lambda: self.v(lambda v: v.tensor_scalar(
            out=f32A.bitcast(U32), in0=srcf.bitcast(U32), scalar1=0x7FFFFFFF,
            scalar2=None, op0=ALU.bitwise_and)))
        ops.append(lambda: self.v(lambda v: v.tensor_tensor(
            out=f32A3, in0=f32A3, in1=inv3, op=ALU.mult)))
        ops.append(lambda: self.v(lambda v: v.tensor_scalar(
            out=f32B, in0=f32A, scalar1=8.0, scalar2=12.0, op0=ALU.max,
            op1=ALU.min)))
        ops.append(lambda: self.v(lambda v: v.scalar_tensor_tensor(
            out=f32B, in0=f32B, scalar=-0.75, in1=f32A, op0=ALU.mult,
            op1=ALU.add)))
        ops.append(lambda: self.v(lambda v: v.tensor_scalar(
            out=f32A, in0=f32A, scalar1=4.0, scalar2=8.0, op0=ALU.max,
            op1=ALU.min)))
        ops.append(lambda: self.v(lambda v: v.scalar_tensor_tensor(
            out=f32B, in0=f32A, scalar=-0.5, in1=f32B, op0=ALU.mult,
            op1=ALU.add)))
        ops.append(lambda: self.v(lambda v: v.tensor_scalar(
            out=f32A, in0=f32B, scalar1=16.0 + M23, scalar2=None, op0=ALU.add)))
        ops.append(lambda: self.v(lambda v: v.tensor_scalar(
            out=rB, in0=f32A, scalar1=-M23, scalar2=None, op0=ALU.add)))
        ops.append(lambda: self.v(lambda v: v.tensor_scalar(
            out=d1B, in0=rB, scalar1=12.0, scalar2=14.0, op0=ALU.max,
            op1=ALU.min)))
        ops.append(lambda: self.v(lambda v: v.tensor_scalar(
            out=d2B, in0=rB, scalar1=14.0, scalar2=15.0, op0=ALU.max,
            op1=ALU.min)))
        ops.append(lambda: self.v(lambda v: v.tensor_tensor(
            out=d1B, in0=d1B, in1=rB, op=ALU.add)))
        ops.append(lambda: self.v(lambda v: v.scalar_tensor_tensor(
            out=d1B, in0=d2B, scalar=3.0, in1=d1B, op0=ALU.mult, op1=ALU.add)))
        ops.append(lambda: self.v(lambda v: v.scalar_tensor_tensor(
            out=magB.rearrange("p (b k) -> p b k", k=16),
            in0=d1B.rearrange("p (b k) -> p b k", k=16), scalar=-62.0, in1=sc3,
            op0=ALU.add, op1=ALU.mult)))

        def last():
            self.v(lambda v: v.tensor_tensor(
                out=dst.bitcast(U16), in0=dst.bitcast(U16),
                in1=magB.bitcast(U16), op=ALU.bitwise_or), inc=True)
            self._ptok[("qv", i)] = self.em.tok("vector")
        ops.append(last)
        return ops

    def attn_b(self):
        em, CST, X, STAT = self.em, self.CST, self.X, self.STAT
        BA, BB, BC, MED, MED2 = self.BA, self.BB, self.BC, self.MED, self.MED2
        qkv_r = self.scr["qkv16"].rearrange("(c p) e -> p c e", p=128)
        wo_r64 = self.scr["wo16"].rearrange("(c p) d -> c p d", p=64)
        aob_r = self.scr["aob"].rearrange("(n p) d -> p n d", p=128)

        def qro(n):
            return BA[:, n * 512:(n + 1) * 512]

        def kro(n):
            return BB[:, n * 512:(n + 1) * 512]

        def vpk(kt, hh=None):
            base = 8192 + (kt // 2) * 520 + (hh * 65 if hh is not None else 0)
            buf = BB if kt % 2 == 0 else BA
            return buf[:, base:base + (65 if hh is not None else 520)]

        def aoT(hh):
            return BC[0:64, hh * 2048:(hh + 1) * 2048]
        qTh = MED2[:, 0:2048]
        kTh = MED2[:, 2048:4096]

        def pT(b):
            return MED[:, b * 1024:(b + 1) * 1024]
        xn_scr = BC[:, 16384:17408]
        ZB = BC[0:64, 16384:17408]
        ZR = BB[:, 14336:16384].bitcast(F32)[0:1, :]
        ZRB = BB[0:1, 13312:14336]
        self._aob_st = {}
        self.v(lambda v: v.memset(MED2[64:128, 0:4096], 0.0), inc=True)
        self._pend_epi = None
        self._pend_epi2 = None
        self._last_pv = None
        self._tok_repl = None
        self._tok_z = None
        for hp in range(2):
            self.w8("vector", self._aob_st.get(0))
            self.w8("vector", self._aob_st.get(1))
            _t0 = self.em.tok("tensor")
            _tt = {}
            for n in range(NT):
                xs = xn_scr if n % 2 == 0 else MED2[:, 0:1024]
                self.w8("vector", _tt.get(n - 2, _t0))
                tok_xn = self.ln_apply(n, xs)
                self.w8("tensor", tok_xn)
                dst = BC[:, 0:16384].rearrange("p (c q) -> p c q", q=2048)[:, :, n * 128:(n + 1) * 128]
                self.transpose8([xs[:, c * 128:(c + 1) * 128] for c in range(8)],
                                n % 4, dst, "vector")
                _tt[n] = self.em.tok("tensor")
            self.w8("vector", self.em.tok("tensor"))
            self.v(lambda v: v.memset(MED2[64:128, 0:1024], 0.0), inc=True)
            tok_xnT = self.em.tok("vector")

            def xnT(c):
                return BC[:, c * 2048:(c + 1) * 2048]
            for part in range(3):
                base_col = part * 1024 + hp * 512
                wsl = MED[:, 0:4096]
                _qg = [self.tok_cast["qkv16"], self.em.tok("tensor")]
                tok_w = self.dq("qkv").start(
                    wsl[:, 0:2048].rearrange("p (c q) -> p c q", q=512),
                    qkv_r[:, 0:4, base_col:base_col + 512], wait=_qg)
                tok_w2 = self.dq("qkv2").start(
                    wsl[:, 2048:4096].rearrange("p (c q) -> p c q", q=512),
                    qkv_r[:, 4:8, base_col:base_col + 512], wait=_qg)
                self.w8("tensor", tok_w)
                self.w8("tensor", tok_w2)
                self.w8("tensor", tok_xnT)
                for n in range(NT):
                    bk = n % 4
                    self.use_bank(bk)
                    for kc in range(8):
                        self.t(lambda t, bk=bk, kc=kc, n=n, wsl=wsl: t.matmul(
                            self.bankf(bk), xnT(kc)[:, n * 128:(n + 1) * 128],
                            wsl[:, kc * 512:(kc + 1) * 512],
                            start=(kc == 0), stop=(kc == 7)), inc=(kc == 7))
                    self.wrote_bank(bk)
                    self.read_bank(bk, "vector")
                    if part < 2:
                        dstt = qro(n) if part == 0 else kro(n)
                        ps4 = self.bankf(bk).rearrange("p (h s q) -> p h s q",
                                                       s=2, q=32)
                        cos4 = CST["ropec"][:, n * 32:(n + 1) * 32] \
                            .unsqueeze(1).unsqueeze(2).broadcast_to((128, 8, 2, 32))
                        sin3 = CST["ropes"][:, n * 32:(n + 1) * 32] \
                            .unsqueeze(1).broadcast_to((128, 8, 32))
                        t1 = BB[:, 14336:14848]
                        t2 = BB[:, 14848:15360]
                        t1v = t1.rearrange("p (h s q) -> p h s q", s=2, q=32)
                        t2v = t2.rearrange("p (h s q) -> p h s q", s=2, q=32)
                        dsv = dstt.rearrange("p (h s q) -> p h s q", s=2, q=32)
                        self.v(lambda v, t1v=t1v, ps4=ps4, cos4=cos4: v.tensor_tensor(
                            out=t1v, in0=ps4, in1=cos4, op=ALU.mult))
                        self.v(lambda v, t2v=t2v, ps4=ps4, sin3=sin3: v.tensor_tensor(
                            out=t2v[:, :, 0, :], in0=ps4[:, :, 1, :], in1=sin3,
                            op=ALU.mult))
                        self.v(lambda v, t2v=t2v, ps4=ps4, sin3=sin3: v.tensor_tensor(
                            out=t2v[:, :, 1, :], in0=ps4[:, :, 0, :], in1=sin3,
                            op=ALU.mult))
                        self.v(lambda v, dsv=dsv, t1v=t1v, t2v=t2v:
                               v.scalar_tensor_tensor(
                                   out=dsv[:, :, 0, :], in0=t2v[:, :, 0, :],
                                   scalar=-1.0, in1=t1v[:, :, 0, :],
                                   op0=ALU.mult, op1=ALU.add))
                        self.v(lambda v, dsv=dsv, t1v=t1v, t2v=t2v: v.tensor_tensor(
                            out=dsv[:, :, 1, :], in0=t1v[:, :, 1, :],
                            in1=t2v[:, :, 1, :], op=ALU.add), inc=True)
                    else:
                        self.v(lambda v, kt=n, bk=bk: v.tensor_copy(
                            out=vpk(kt).rearrange("p (h q) -> p h q", q=65)[:, :, 0:64],
                            in_=self.bankf(bk).rearrange("p (h q) -> p h q", q=64)))
                        self.v(lambda v, kt=n: v.memset(
                            vpk(kt).rearrange("p (h q) -> p h q", q=65)[:, :, 64:65],
                            1.0), inc=True)
                    self.bank_read_done(bk, "vector")
            tok_proj = self.em.tok("vector")
            for hh in range(8):
                for si, (srcf, dstf) in enumerate(((qro, qTh), (kro, kTh))):
                    self.w8("tensor", tok_proj)
                    for bi in range(2):
                        self.transpose8(
                            [srcf(nn)[:, hh * 64:(hh + 1) * 64]
                             for nn in range(bi * 8, bi * 8 + 8)],
                            2 + si * 2 + bi,
                            dstf[0:64, bi * 1024:(bi + 1) * 1024],
                            "vector", prow=64)
                tok_qkT = self.em.tok("vector")
                for qtc in range(2):
                    exp_toks = {}
                    pv_toks = {}
                    prev_pv = self._last_pv

                    def pv_mms(kt):
                        ptb = pT(kt % 3)
                        self.w8("tensor", exp_toks[kt])
                        for sub in range(2):
                            self.t(lambda t, kt=kt, sub=sub, ptb=ptb, hh=hh: t.matmul(
                                self.bankf(6 + sub)[0:65, :],
                                vpk(kt, hh),
                                ptb[:, sub * 512:(sub + 1) * 512],
                                start=(kt == 0), stop=(kt == NT - 1)),
                                inc=(sub == 1))
                        pv_toks[kt] = self.em.tok("tensor")
                    for mt in range(0, NT, 2):
                        for kt in (mt, mt + 1):
                            sb = 2 if kt % 2 == 0 else 4
                            self.use_bank(sb)
                            self.use_bank(sb + 1)
                            self.w8("tensor", tok_qkT)
                            for sub in range(2):
                                self.t(lambda t, sb=sb, sub=sub, kt=kt, qtc=qtc:
                                       t.matmul(
                                           self.bankf(sb + sub),
                                           kTh[:, kt * 128:(kt + 1) * 128],
                                           qTh[:, qtc * 1024 + sub * 512:
                                               qtc * 1024 + sub * 512 + 512],
                                           start=True, stop=True), inc=(sub == 1))
                            self.wrote_bank(sb)
                            self.wrote_bank(sb + 1)
                        if mt == 2:
                            if self._pend_epi is not None:
                                self._pend_epi()
                                self._pend_epi = None
                            self.use_bank(6)
                            self.use_bank(7)
                        if mt == 6 and self._pend_epi2 is not None:
                            self._pend_epi2()
                            self._pend_epi2 = None
                        if mt >= 2:
                            pv_mms(mt - 2)
                            pv_mms(mt - 1)
                        for kt in (mt, mt + 1):
                            sb = 2 if kt % 2 == 0 else 4
                            ptb = pT(kt % 3)
                            self.read_bank(sb, "scalar")
                            self.read_bank(sb + 1, "scalar")
                            if kt >= 3:
                                self.w8("scalar", pv_toks[kt - 3])
                            elif prev_pv is not None:
                                self.w8("scalar", prev_pv[{0: 15, 1: 13, 2: 14}[kt]])
                            self.s(lambda s, sb=sb, ptb=ptb: s.activation(
                                out=ptb, in_=self.PS[:, sb * 512:(sb + 2) * 512],
                                func=AF.Exp, scale=0.125), inc=True)
                            self.bank_read_done(sb, "scalar")
                            self.bank_read_done(sb + 1, "scalar")
                            exp_toks[kt] = self.em.tok("scalar")
                    pv_mms(NT - 2)
                    pv_mms(NT - 1)
                    self.wrote_bank(6)
                    self.wrote_bank(7)
                    self._last_pv = pv_toks

                    def epi(hh=hh, qtc=qtc):
                        pvb = BA[0:64, 12352:13376]
                        self.read_bank(6, "vector")
                        self.read_bank(7, "vector")
                        self.v(lambda v, pvb=pvb: v.tensor_copy(
                            out=pvb, in_=self.PS[0:64, 3072:4096]))
                        self.v(lambda v: v.tensor_copy(
                            out=ZR, in_=self.PS[64:65, 3072:4096]), inc=True)
                        self.bank_read_done(6, "vector")
                        self.bank_read_done(7, "vector")
                        self.w8("vector", self._tok_repl)
                        self.v(lambda v: v.reciprocal(out=ZR, in_=ZR))
                        self.v(lambda v: v.tensor_copy(out=ZRB, in_=ZR),
                               inc=True)
                        self._tok_z = self.em.tok("vector")

                        def epi2(hh=hh, qtc=qtc, pvb=pvb):
                            self.use_bank(0)
                            self.use_bank(1)
                            self.w8("tensor", self._tok_z)
                            for sub in range(2):
                                self.t(lambda t, sub=sub: t.matmul(
                                    self.bankf(0 + sub)[0:64, :],
                                    CST["ones_col"][:],
                                    ZRB[:, sub * 512:(sub + 1) * 512],
                                    start=True, stop=True), inc=(sub == 1))
                            self._tok_repl = self.em.tok("tensor")
                            self.wrote_bank(0)
                            self.wrote_bank(1)
                            self.read_bank(0, "vector")
                            self.read_bank(1, "vector")
                            self.v(lambda v: v.tensor_copy(
                                out=ZB, in_=self.PS[0:64, 0:1024]), inc=True)
                            self.bank_read_done(0, "vector")
                            self.bank_read_done(1, "vector")
                            self.v(lambda v, hh=hh, qtc=qtc, pvb=pvb:
                                   v.tensor_tensor(
                                       out=aoT(hh)[:, qtc * 1024:(qtc + 1) * 1024],
                                       in0=pvb, in1=ZB,
                                       op=ALU.mult), inc=True)
                        self._pend_epi2 = epi2
                    self._pend_epi = epi
            if self._pend_epi is not None:
                self._pend_epi()
                self._pend_epi = None
            if self._pend_epi2 is not None:
                self._pend_epi2()
                self._pend_epi2 = None
            wo_5d = self.scr["wo16"].rearrange(
                "(a c p) (hf q) -> a p c hf q", p=64, c=8, q=512)
            wosl = MED[:, 4096:8192]
            tok_ao = self.em.tok("vector")
            for half in range(2):
                tok_wo = self.dq("wo").start(
                    wosl[0:64, :].rearrange("p (c q) -> p c q", q=512),
                    wo_5d[hp, :, :, half],
                    wait=[self.tok_cast["wo16"], self.em.tok("tensor")])
                self.w8("tensor", tok_wo)
                self.w8("tensor", tok_ao)
                for n in range(NT):
                    bk = n % 2
                    self.use_bank(bk)
                    for hh in range(8):
                        self.t(lambda t, bk=bk, hh=hh, n=n: t.matmul(
                            self.bankf(bk),
                            aoT(hh)[:, n * 128:(n + 1) * 128],
                            wosl[0:64, hh * 512:(hh + 1) * 512],
                            start=(hh == 0), stop=(hh == 7)), inc=(hh == 7))
                    self.wrote_bank(bk)
                    self.read_bank(bk, "vector")
                    if hp == 0:
                        stage = BC[:, 16384 + bk * 512:16384 + bk * 512 + 512]
                        self.w8("vector", self._aob_st.get(bk))
                        self.v(lambda v, bk=bk, stage=stage: v.tensor_copy(
                            out=stage, in_=self.bankf(bk)), inc=True)
                        self.bank_read_done(bk, "vector")
                        self._aob_st[bk] = self.dq(f"aos{bk}").start(
                            aob_r[:, n, half * 512:half * 512 + 512], stage,
                            wait=[self.em.tok("vector")])
                    else:
                        lslot = BC[:, 16384 + bk * 512:16384 + bk * 512 + 512]
                        ltok = self.dq(f"aol{bk}").start(
                            lslot, aob_r[:, n, half * 512:half * 512 + 512],
                            wait=[self.em.tok("vector"), self._aob_st.get(0),
                                  self._aob_st.get(1)])
                        self.w8("vector", ltok)
                        self.resid_update(n, half, self.bankf(bk), extra=lslot)
                        self.bank_read_done(bk, "vector")


def kernel(L0_attn_a_out, L0_attn_a_W_write, L0_attn_a_W_read, L0_attn_a_W_beta,
           L0_attn_a_decay, L0_attn_b_Wqkv, L0_attn_b_Wo, L0_ffn_w_gate,
           L0_ffn_w_up, L0_ffn_w_down, _parts=4):
    consts = host_constants()
    nc = build_program(parts=_parts)
    decay_t = np.ascontiguousarray(
        np.tile(np.asarray(L0_attn_a_decay, dtype=np.float32)[None, :], (128, 1)))
    shared = dict(
        Ww=np.ascontiguousarray(L0_attn_a_W_write, dtype=np.float32),
        Wr=np.ascontiguousarray(L0_attn_a_W_read, dtype=np.float32),
        Wb=np.ascontiguousarray(L0_attn_a_W_beta, dtype=np.float32),
        decay_t=decay_t,
        Wqkv=np.ascontiguousarray(L0_attn_b_Wqkv, dtype=np.float32),
        Wo=np.ascontiguousarray(L0_attn_b_Wo, dtype=np.float32),
        wg=np.ascontiguousarray(L0_ffn_w_gate, dtype=np.float32),
        wu=np.ascontiguousarray(L0_ffn_w_up, dtype=np.float32),
        wd=np.ascontiguousarray(L0_ffn_w_down, dtype=np.float32),
        **consts,
    )
    xs = np.ascontiguousarray(L0_attn_a_out, dtype=np.float32)
    in_maps = [dict(x=xs[i], **shared) for i in range(8)]
    res = run_bass_kernel_spmd(nc, in_maps, core_ids=list(range(8)))
    return np.stack([res.results[i]["y"] for i in range(8)], axis=0)


# version 20
# speedup vs baseline: 1.2249x; 1.0071x over previous
import numpy as np
import ml_dtypes
import concourse.bass as bass
import concourse.mybir as mybir
from concourse.bass_utils import run_bass_kernel_spmd
from contextlib import ExitStack

F32 = mybir.dt.float32
BF16 = mybir.dt.bfloat16
I16 = mybir.dt.int16
U16 = mybir.dt.uint16
AF = mybir.ActivationFunctionType
ALU = mybir.AluOpType
AX = mybir.AxisListType

T = 2048
D = 1024
NT = 16
H = 8
HB = 16
DHB = 64
II = 2816
NI = 22
LN_EPS = 1e-5
ENGINES = ("tensor", "vector", "scalar", "gpsimd", "sync")
BFNP = np.dtype(ml_dtypes.bfloat16)


def host_constants():
    c = {}
    i_idx = np.arange(128, dtype=np.float32)[:, None]
    j_idx = np.arange(128, dtype=np.float32)[None, :]
    c["eye128"] = np.eye(128, dtype=np.float32).astype(BFNP)
    c["eye8"] = np.tile(np.eye(128, dtype=np.float32), (1, 8)).astype(BFNP)
    c["iota_dif"] = np.ascontiguousarray(np.broadcast_to(j_idx - i_idx, (128, 128)), dtype=np.float32)
    c["iota_colp1"] = np.ascontiguousarray(i_idx + 1.0, dtype=np.float32)
    c["iota_colrev"] = np.ascontiguousarray(127.0 - i_idx, dtype=np.float32)
    c["m_up_s_neg"] = np.ascontiguousarray(-(j_idx > i_idx).astype(np.float32))
    c["m_low_s_neg"] = np.ascontiguousarray(-(j_idx < i_idx).astype(np.float32))
    c["m_up_i"] = np.ascontiguousarray((j_idx >= i_idx).astype(np.float32))
    c["ones_col"] = np.ones((1, 64), dtype=np.float32).astype(BFNP)
    inv_freq = 1.0 / 10000.0 ** (np.arange(0, DHB, 2, dtype=np.float32) / DHB)
    t_pos = np.arange(T, dtype=np.float32)
    freqs = np.outer(t_pos, inv_freq)
    cosw = np.cos(freqs)
    sinw = np.sin(freqs)
    HD2 = DHB // 2
    c["ropec"] = np.ascontiguousarray(
        cosw.reshape(NT, 128, HD2).transpose(1, 0, 2).reshape(128, NT * HD2)).astype(BFNP)
    c["ropes"] = np.ascontiguousarray(
        sinw.reshape(NT, 128, HD2).transpose(1, 0, 2).reshape(128, NT * HD2)).astype(BFNP)
    return c


CONST_SPECS = [
    ("eye128", [128, 128], BF16), ("eye8", [128, 1024], BF16),
    ("iota_dif", [128, 128], F32),
    ("iota_colp1", [128, 1], F32), ("iota_colrev", [128, 1], F32),
    ("m_up_s_neg", [128, 128], F32), ("m_low_s_neg", [128, 128], F32),
    ("m_up_i", [128, 128], F32), ("ones_col", [1, 64], BF16),
    ("ropec", [128, NT * DHB // 2], BF16), ("ropes", [128, NT * DHB // 2], BF16),
]


class DmaQ:
    def __init__(self, em, eng, name):
        self.em = em
        self.eng = eng
        self.h = em.ctx.enter_context(em.nc.semaphore(name))
        self.cnt = 0
        self.waited = {e: 0 for e in ENGINES}

    def start(self, out, in_, wait=()):
        for tk in wait:
            self.em.wait(self.eng, tk)
        if self.cnt > 0 and self.waited[self.eng] < self.cnt:
            self.wait(self.eng, (self, self.cnt))
        self.cnt += 16
        h = self.h

        def fn(e, out=out, in_=in_):
            e.dma_start(out=out, in_=in_).then_inc(h, 16)
        self.em.q[self.eng].append(fn)
        self.em.ninstr += 1
        return (self, self.cnt)

    def start_t(self, out, in_, wait=()):
        for tk in wait:
            self.em.wait(self.eng, tk)
        if self.cnt > 0 and self.waited[self.eng] < self.cnt:
            self.wait(self.eng, (self, self.cnt))
        self.cnt += 16
        h = self.h

        def fn(e, out=out, in_=in_):
            e.dma_start_transpose(out=out, in_=in_).then_inc(h, 16)
        self.em.q[self.eng].append(fn)
        self.em.ninstr += 1
        return (self, self.cnt)

    def start_t_batch(self, pairs, wait=()):
        for tk in wait:
            self.em.wait(self.eng, tk)
        if self.cnt > 0 and self.waited[self.eng] < self.cnt:
            self.wait(self.eng, (self, self.cnt))
        h = self.h
        for out, in_ in pairs:
            self.cnt += 16

            def fn(e, out=out, in_=in_):
                e.dma_start_transpose(out=out, in_=in_).then_inc(h, 16)
            self.em.q[self.eng].append(fn)
            self.em.ninstr += 1
        return (self, self.cnt)

    def wait(self, eng, token):
        q, val = token
        assert q is self
        if val <= self.waited[eng]:
            return
        self.waited[eng] = val
        self.em.q[eng].append(lambda e, h=self.h, val=val: e.wait_ge(h, val))


class Em:

    def __init__(self, nc, ctx):
        self.nc = nc
        self.ctx = ctx
        self.q = {e: [] for e in ENGINES}
        self.sem = {e: ctx.enter_context(nc.semaphore(f"c_{e}")) for e in ENGINES}
        self.cnt = {e: 0 for e in ENGINES}
        self.waited = {e: {} for e in ENGINES}
        self.ninstr = 0

    AUTOSYNC = ("vector", "scalar", "gpsimd")

    def emit(self, eng, fn, inc=False):
        self.ninstr += 1
        if eng in self.AUTOSYNC:
            prev = self.cnt[eng]
            sem = self.sem[eng]
            if prev > self.waited[eng].get(eng, 0):
                self.waited[eng][eng] = prev
                self.q[eng].append(lambda e, sem=sem, prev=prev: e.wait_ge(sem, prev))
                self.ninstr += 1
            inc = True
        if inc:
            self.cnt[eng] += 1
            sem = self.sem[eng]

            def wrapped(e, fn=fn, sem=sem):
                fn(e).then_inc(sem, 1)
            self.q[eng].append(wrapped)
        else:
            self.q[eng].append(fn)
        return (eng, self.cnt[eng])

    def tok(self, eng):
        return (eng, self.cnt[eng])

    def wait(self, eng, token):
        if token is None:
            return
        src, val = token
        if isinstance(src, DmaQ):
            src.wait(eng, token)
            return
        if val <= 0:
            return
        if val <= self.waited[eng].get(src, 0):
            return
        self.waited[eng][src] = val
        sem = self.sem[src]
        self.q[eng].append(lambda e, sem=sem, val=val: e.wait_ge(sem, val))

    def bump(self, eng):
        self.cnt[eng] += 1
        sem = self.sem[eng]
        self.q[eng].append(lambda e, sem=sem: e.drain().then_inc(sem, 1))
        self.ninstr += 1
        return (eng, self.cnt[eng])

    def checkpoint(self):
        toks = [self.bump(e) for e in ENGINES]
        for e in ENGINES:
            for tk in toks:
                self.wait(e, tk)

    def play(self):
        with self.nc.Block() as blk:
            for eng in ENGINES:
                ops = self.q[eng]

                def body(e, ops=ops):
                    for fn in ops:
                        fn(e)
                getattr(blk, eng)(body)


def build_program(parts=4):
    nc = bass.Bass("TRN2", target_bir_lowering=False)
    ctx = ExitStack()

    x_in = nc.dram_tensor("x", [T, D], F32, kind="ExternalInput").ap()
    w = {
        "Ww": nc.dram_tensor("Ww", [D, D], F32, kind="ExternalInput").ap(),
        "Wr": nc.dram_tensor("Wr", [D, D], F32, kind="ExternalInput").ap(),
        "Wb": nc.dram_tensor("Wb", [H, D], F32, kind="ExternalInput").ap(),
        "decay_t": nc.dram_tensor("decay_t", [128, H], F32, kind="ExternalInput").ap(),
        "Wqkv": nc.dram_tensor("Wqkv", [D, 3 * D], F32, kind="ExternalInput").ap(),
        "Wo": nc.dram_tensor("Wo", [D, D], F32, kind="ExternalInput").ap(),
        "wg": nc.dram_tensor("wg", [II, D], F32, kind="ExternalInput").ap(),
        "wu": nc.dram_tensor("wu", [II, D], F32, kind="ExternalInput").ap(),
        "wd": nc.dram_tensor("wd", [D, II], F32, kind="ExternalInput").ap(),
    }
    consts_d = {}
    for name, shape, dt_ in CONST_SPECS:
        consts_d[name] = nc.dram_tensor(name, shape, dt_, kind="ExternalInput").ap()
    y_out = nc.dram_tensor("y", [T, D], F32, kind="ExternalOutput").ap()
    scr = {
        "wgT": nc.dram_tensor("wgT_scr", [D, II], BF16, kind="Internal").ap(),
        "wuT": nc.dram_tensor("wuT_scr", [D, II], BF16, kind="Internal").ap(),
        "wdT": nc.dram_tensor("wdT_scr", [II, D], BF16, kind="Internal").ap(),
        "wg16": nc.dram_tensor("wg16_scr", [II, D], BF16, kind="Internal").ap(),
        "wu16": nc.dram_tensor("wu16_scr", [II, D], BF16, kind="Internal").ap(),
        "wd16": nc.dram_tensor("wd16_scr", [D, II], BF16, kind="Internal").ap(),
        "qkv16": nc.dram_tensor("qkv16_scr", [D, 3 * D], BF16, kind="Internal").ap(),
        "wo16": nc.dram_tensor("wo16_scr", [D, D], BF16, kind="Internal").ap(),
        "aob": nc.dram_tensor("aob_scr", [T, D], BF16, kind="Internal").ap(),
    }

    def sbuf(name, shape, dt_):
        return ctx.enter_context(nc.sbuf_tensor(name, shape, dt_))

    env = dict(nc=nc, x_in=x_in, w=w, consts_d=consts_d, y_out=y_out, scr=scr)
    env["X"] = sbuf("xres", [128, NT * D], F32)
    env["BA"] = sbuf("bigA", [128, 16384], BF16)
    env["BB"] = sbuf("bigB", [128, 16384], BF16)
    env["BC"] = sbuf("bigC", [128, 17408], BF16)
    env["MED"] = sbuf("med", [128, 8192], BF16)
    env["MED2"] = sbuf("med2", [128, 4096], BF16)
    env["STAT"] = sbuf("stat", [128, 96], F32)
    env["COLS"] = sbuf("cols", [128, 128], F32)
    env["QCOL"] = sbuf("qcol", [128, 528], F32)
    env["HC"] = sbuf("hcols", [128, 64], F32)
    env["MASKU"] = sbuf("masku", [128, 1024], BF16)
    env["MASKL"] = sbuf("maskl", [128, 1024], BF16)
    env["MASKA"] = sbuf("maska", [128, 1024], BF16)
    env["DEALL"] = sbuf("deall", [128, 1024], BF16)
    env["CTDE"] = sbuf("ctde", [128, 1024], BF16)
    env["SST"] = sbuf("Sstate", [128, 1024], BF16)
    env["CST"] = {name: sbuf("c_" + name, shape, dt_) for name, shape, dt_ in CONST_SPECS}
    env["PS"] = ctx.enter_context(nc.psum_tensor("PS", [128, 4096], F32))

    em = Em(nc, ctx)
    env["em"] = em

    b = Builder(env)
    b.emit_all(parts)
    em.play()
    ctx.close()
    nc._n_instr = em.ninstr
    return nc


class Builder:
    def __init__(self, env):
        self.__dict__.update(env)
        self.bank_w = {}
        self.bank_r = {}
        self._dqs = {}

    def dq(self, name, eng="sync"):
        if name not in self._dqs:
            self._dqs[name] = DmaQ(self.em, eng, "dq_" + name)
        return self._dqs[name]

    def t(self, fn, inc=False):
        return self.em.emit("tensor", fn, inc)

    def v(self, fn, inc=False):
        ivq = getattr(self, "_iv_queue", None)
        if ivq and self._iv_on:
            self._iv_on = False
            for _ in range(min(self._iv_k, len(ivq))):
                ivq.pop(0)()
            self._iv_on = True
        return self.em.emit("vector", fn, inc)

    def v_crit(self, fn, inc=False):
        return self.em.emit("vector", fn, inc)

    def s(self, fn, inc=False):
        return self.em.emit("scalar", fn, inc)

    def w8(self, eng, tok):
        self.em.wait(eng, tok)

    def bankf(self, i, ncols=512):
        return self.PS[:, 512 * i:512 * i + ncols]

    def bankb(self, i, ncols=1024):
        return self.PS[:].bitcast(BF16)[:, 1024 * i:1024 * i + ncols]

    def use_bank(self, i):
        self.w8("tensor", self.bank_r.get(i))

    def wrote_bank(self, i):
        self.bank_w[i] = self.em.tok("tensor")

    def read_bank(self, i, eng):
        self.w8(eng, self.bank_w.get(i))

    def bank_read_done(self, i, eng):
        self.bank_r[i] = self.em.tok(eng)

    def transpose8(self, srcs, bank, dst, evac_eng="vector", prow=128):
        n = len(srcs)
        self.use_bank(bank)
        for i, sap in enumerate(srcs):
            self.t(lambda t, sap=sap, i=i: t.transpose(
                self.bankb(bank)[0:prow, i * 128:(i + 1) * 128], sap,
                self.CST["eye128"][:]), inc=(i == n - 1))
        self.wrote_bank(bank)
        self.read_bank(bank, evac_eng)
        src_ap = self.bankb(bank)[0:prow, 0:n * 128]
        if evac_eng == "vector":
            self.v(lambda v, dst=dst, src_ap=src_ap: v.tensor_copy(out=dst, in_=src_ap),
                   inc=True)
        else:
            self.s(lambda s, dst=dst, src_ap=src_ap: s.copy(out=dst, in_=src_ap),
                   inc=True)
        self.bank_read_done(bank, evac_eng)
        return self.em.tok(evac_eng)

    def emit_all(self, parts):
        em = self.em
        self.load_x_and_consts()
        if parts >= 0.1:
            self.precast_weights()
        if parts >= 0.2:
            self.attn_a_weight_prep()
        if parts >= 0.3:
            self.decay_consts()
        if parts >= 0.1:
            em.checkpoint()
        if parts >= 0.6:
            self.ln_prologue(first=True)
        if parts >= 1:
            if parts >= 1.5:
                self.prep_overlay_init()
            self.attn_a()
            em.checkpoint()
        if parts == 1.5:
            self.prep_drain()
            em.checkpoint()
        if parts >= 2:
            self.ln_prologue()
            self.ffn(first=True)
            em.checkpoint()
        if parts >= 3:
            self.ln_prologue()
            self.attn_b()
            em.checkpoint()
        if parts >= 4:
            self.ln_prologue()
            self.ffn(first=False)
            em.checkpoint()
        em.checkpoint()
        yr = self.y_out.rearrange("(n p) d -> p n d", p=128)
        qy = self.dq("y")
        tok = qy.start(yr, self.X[:].rearrange("p (n d) -> p n d", d=D),
                       wait=[self.tok_xload])
        qy.wait("sync", tok)

    def load_x_and_consts(self):
        xr = self.x_in.rearrange("(n p) d -> p n d", p=128)
        self.tok_xload = self.dq("x").start(
            self.X[:].rearrange("p (n d) -> p n d", d=D), xr)
        for name, shape, dt_ in CONST_SPECS:
            self.tok_consts = self.dq("const").start(self.CST[name][:],
                                                     self.consts_d[name])

    def precast_weights(self):
        self.tok_cast = {}
        for src, dst in [("Wqkv", "qkv16"), ("Wo", "wo16")]:
            tok = self.dq("pre", "gpsimd").start(self.scr[dst], self.w[src])
        for dst in ("qkv16", "wo16"):
            self.tok_cast[dst] = tok

    def attn_a_weight_prep(self):
        self.tok_WwT = self._transpose_1024sq(self.w["Ww"], self.MED[:, 0:8192])
        self.tok_WrT = self._transpose_1024sq(self.w["Wr"], self.BC[:, 2048:10240])
        tok = self.dq("wb", "gpsimd").start(self.MED2[0:H, 0:D], self.w["Wb"],
                                            wait=[self.em.tok("tensor")])
        self.w8("tensor", tok)
        self.w8("tensor", self.tok_consts)
        self.use_bank(0)
        for cidx in range(8):
            self.t(lambda t, cidx=cidx: t.transpose(
                self.bankb(0)[:, cidx * 8:cidx * 8 + 8],
                self.MED2[0:H, cidx * 128:(cidx + 1) * 128],
                self.CST["eye128"][0:H, 0:H]), inc=(cidx == 7))
        self.wrote_bank(0)
        self.read_bank(0, "vector")
        self.v(lambda v: v.tensor_copy(out=self.MED2[:, 3584:3648],
                                       in_=self.bankb(0)[:, 0:64]), inc=True)
        self.bank_read_done(0, "vector")
        self.tok_wbT = self.em.tok("vector")

    def _transpose_1024sq(self, src_f32, dst):
        src_r = src_f32.rearrange("(c p) d -> c p d", p=128)
        entry = self.em.tok("tensor")
        tr = {-2: entry, -1: entry}
        for r in range(8):
            slot = self.MED2[:, (r % 2) * 1024:(r % 2) * 1024 + 1024]
            ltok = self.dq(f"stg{r % 2}", "gpsimd").start(slot, src_r[r],
                                                          wait=[tr.get(r - 2)])
            bank = r % 2
            self.w8("tensor", ltok)
            self.w8("tensor", self.tok_consts)
            self.use_bank(bank)
            for cdx in range(8):
                self.t(lambda t, slot=slot, bank=bank, cdx=cdx: t.transpose(
                    self.bankb(bank)[:, cdx * 128:(cdx + 1) * 128],
                    slot[:, cdx * 128:(cdx + 1) * 128],
                    self.CST["eye128"][:]), inc=(cdx == 7))
            tr[r] = self.em.tok("tensor")
            self.wrote_bank(bank)
            self.read_bank(bank, "vector")
            out = dst.rearrange("p (c q) -> p c q", q=1024)[:, :, r * 128:(r + 1) * 128]
            self.v(lambda v, bank=bank, out=out: v.tensor_copy(
                out=out, in_=self.bankb(bank)[:].rearrange("p (c q) -> p c q", q=128)),
                inc=True)
            self.bank_read_done(bank, "vector")
        return self.em.tok("vector")

    def decay_consts(self):
        CST = self.CST

        def hc(a, n=1):
            return self.HC[:, a:a + n]
        tokd = self.dq("decay", "gpsimd").start(hc(0, 8), self.w["decay_t"])
        self.w8("scalar", tokd)
        self.w8("scalar", self.tok_consts)
        stok = self.s(lambda s: s.activation(out=hc(8, 8), in_=hc(0, 8),
                                             func=AF.Sigmoid), inc=True)
        self.w8("scalar", stok)
        self.s(lambda s: s.activation(out=hc(16, 8), in_=hc(8, 8), func=AF.Ln), inc=True)
        self.w8("vector", self.em.tok("scalar"))
        self.v(lambda v: v.tensor_scalar(out=hc(24, 8), in0=hc(16, 8), scalar1=-1.0,
                                         scalar2=None, op0=ALU.mult), inc=True)
        self.w8("scalar", self.em.tok("vector"))
        for h in range(H):
            self.s(lambda s, h=h: s.activation(out=hc(32 + h), in_=CST["iota_colp1"][:],
                                               func=AF.Exp, scale=hc(16 + h)))
            self.s(lambda s, h=h: s.activation(out=hc(48 + h), in_=CST["iota_colrev"][:],
                                               func=AF.Exp, scale=hc(16 + h)))
            self.s(lambda s, h=h: s.activation(out=hc(56 + h), in_=hc(16 + h),
                                               func=AF.Exp, scale=128.0), inc=(h == H - 1))
        self.w8("vector", self.em.tok("scalar"))
        self.v(lambda v: v.tensor_scalar(out=hc(40, 8), in0=hc(32, 8), scalar1=-1.0,
                                         scalar2=None, op0=ALU.mult), inc=True)
        e1 = self.BA[:, 0:256].bitcast(F32)
        e2 = self.BA[:, 256:512].bitcast(F32)
        for h in range(H):
            sl = slice(h * 128, (h + 1) * 128)
            self.w8("scalar", self.em.tok("vector"))
            self.s(lambda s, h=h, e1=e1: s.activation(out=e1, in_=CST["iota_dif"][:],
                                                      func=AF.Exp, scale=hc(16 + h)))
            self.s(lambda s, h=h, e2=e2: s.activation(out=e2, in_=CST["iota_dif"][:],
                                                      func=AF.Exp, scale=hc(24 + h)))
            self.w8("vector", self.em.tok("scalar"))
            self.v(lambda v, h=h: v.tensor_tensor(
                out=self.COLS[:, 120:121], in0=hc(16 + h),
                in1=CST["iota_colp1"][:], op=ALU.mult), inc=True)
            self.w8("scalar", self.em.tok("vector"))
            self.s(lambda s, h=h, sl=sl: s.activation(out=self.DEALL[:, sl],
                                                      in_=CST["iota_dif"][:],
                                                      func=AF.Exp, scale=hc(16 + h),
                                                      bias=self.COLS[:, 120:121]),
                   inc=True)
            self.w8("vector", self.em.tok("scalar"))
            self.v(lambda v, sl=sl, e1=e1: v.tensor_tensor(
                out=self.MASKU[:, sl], in0=e1, in1=CST["m_up_s_neg"][:], op=ALU.mult))
            self.v(lambda v, sl=sl, e1=e1: v.tensor_tensor(
                out=self.MASKA[:, sl], in0=e1, in1=CST["m_up_i"][:], op=ALU.mult))
            self.v(lambda v, sl=sl, e2=e2: v.tensor_tensor(
                out=self.MASKL[:, sl], in0=e2, in1=CST["m_low_s_neg"][:], op=ALU.mult))
            self.v(lambda v, sl=sl, h=h: v.tensor_scalar(
                out=self.CTDE[:, sl], in0=CST["eye128"][:],
                scalar1=hc(56 + h), scalar2=None, op0=ALU.mult), inc=True)
        self.v(lambda v: v.memset(self.SST[:], 0.0))
        self.v(lambda v: v.memset(self.STAT[:, 64:80], 0.0), inc=True)
        self.tok_dk = self.em.tok("vector")

    def ln_prologue(self, first=False):
        X, STAT = self.X, self.STAT
        scrf = self.BA[:, 0:2048].bitcast(F32)
        if first:
            self.w8("vector", self.tok_xload)
            for n in range(NT):
                self.v(lambda v, n=n: v.tensor_reduce(
                    out=STAT[:, n:n + 1], in_=X[:, n * D:(n + 1) * D],
                    axis=AX.X, op=ALU.add), inc=(n == NT - 1))
        else:
            self.v(lambda v: v.tensor_tensor(out=STAT[:, 0:16], in0=STAT[:, 0:16],
                                             in1=STAT[:, 64:80], op=ALU.add), inc=True)
        self.w8("scalar", self.em.tok("vector"))
        self.w8("scalar", self.tok_xload)
        for n in range(NT):
            stok = self.s(lambda s, n=n: s.activation(
                out=scrf, in_=X[:, n * D:(n + 1) * D], func=AF.Square,
                accum_out=STAT[:, 16 + n:17 + n]), inc=True)
            self.w8("scalar", stok)
        self.w8("vector", self.em.tok("scalar"))
        self.v(lambda v: v.tensor_scalar(out=STAT[:, 32:48], in0=STAT[:, 0:16],
                                         scalar1=-1.0 / D, scalar2=None, op0=ALU.mult))
        self.v(lambda v: v.tensor_scalar(out=STAT[:, 64:80], in0=STAT[:, 16:32],
                                         scalar1=1.0 / D, scalar2=None, op0=ALU.mult))
        self.v(lambda v: v.tensor_tensor(out=STAT[:, 80:96], in0=STAT[:, 32:48],
                                         in1=STAT[:, 32:48], op=ALU.mult))
        self.v(lambda v: v.scalar_tensor_tensor(out=STAT[:, 80:96], in0=STAT[:, 80:96],
                                                scalar=-1.0, in1=STAT[:, 64:80],
                                                op0=ALU.mult, op1=ALU.add))
        self.v(lambda v: v.tensor_scalar(out=STAT[:, 80:96], in0=STAT[:, 80:96],
                                         scalar1=LN_EPS, scalar2=None, op0=ALU.add),
               inc=True)
        self.w8("scalar", self.em.tok("vector"))
        stok = self.s(lambda s: s.activation(out=STAT[:, 64:80], in_=STAT[:, 80:96],
                                             func=AF.Ln), inc=True)
        self.w8("scalar", stok)
        self.s(lambda s: s.activation(out=STAT[:, 48:64], in_=STAT[:, 64:80],
                                      func=AF.Exp, scale=-0.5), inc=True)
        self.tok_ln = self.em.tok("scalar")
        self.w8("vector", self.tok_ln)
        self.v(lambda v: v.memset(self.STAT[:, 0:16], 0.0))
        self.v(lambda v: v.memset(self.STAT[:, 64:80], 0.0), inc=True)

    def ln_apply(self, n, out_ap):
        self.v(lambda v, n=n, out_ap=out_ap: v.tensor_scalar(
            out=out_ap, in0=self.X[:, n * D:(n + 1) * D],
            scalar1=self.STAT[:, 32 + n:33 + n], scalar2=self.STAT[:, 48 + n:49 + n],
            op0=ALU.add, op1=ALU.mult), inc=True)
        return self.em.tok("vector")

    def resid_update(self, n, half, in0_ap, extra=None):
        xs = self.X[:, n * D + half * 512:n * D + half * 512 + 512]
        acc_col = (n) if half == 0 else (64 + n)
        acc = self.STAT[:, acc_col:acc_col + 1]
        if extra is not None:
            scx = self.BB[:, 14336:15360].bitcast(F32)
            self.v(lambda v, in0_ap=in0_ap, extra=extra, scx=scx: v.tensor_tensor(
                out=scx, in0=in0_ap, in1=extra, op=ALU.add), inc=True)
            in0_ap = scx
        self.v(lambda v, in0_ap=in0_ap, xs=xs, acc=acc: v.scalar_tensor_tensor(
            out=xs, in0=in0_ap, scalar=1.0, in1=xs, op0=ALU.mult, op1=ALU.add,
            accum_out=acc), inc=True)

    def attn_a(self):
        em, CST = self.em, self.CST
        BA, BB, MED2, COLS, HC = self.BA, self.BB, self.MED2, self.COLS, self.HC
        X, SST = self.X, self.SST

        def slot(idx, p):
            return BA[:, (idx * 2 + p) * 1024:(idx * 2 + p) * 1024 + 1024]

        def bbs(a, n=1024):
            return BB[:, a:a + n]

        def kT(p):
            return bbs(p * 1024)

        def kbT(p):
            return bbs(2048 + p * 1024)

        def kdeT(p):
            return bbs(4096 + p * 1024)
        U8 = bbs(6144)
        UT8 = bbs(7168)
        AT8 = bbs(8192)
        P8 = bbs(9216)
        PT8 = bbs(10240)
        Bm = bbs(11264)
        kcdT8 = bbs(12288)
        vnew8 = bbs(13312)

        def ccol(p, a, n=8):
            return COLS[:, p * 64 + a:p * 64 + a + n]
        scrq = self.BC[:, 0:2048].bitcast(F32)

        self.w8("vector", self.tok_dk)
        aa_end = {}
        eTok = {}

        def make_E(n):
            p = n % 2
            xn_t = slot(0, p)
            xnT_t = MED2[:, p * 1024:(p + 1) * 1024]
            ge = aa_end.get(p)
            rn_prev = eTok.get(("rn", n - 1))

            def s1():
                if ge:
                    self.w8("vector", ge["t"])
                    self.w8("vector", ge["s"])
                tok_xn = self.ln_apply(n, xn_t)
                eTok[("ln", n)] = tok_xn
                self.w8("tensor", tok_xn)
                self.transpose8([xn_t[:, c * 128:(c + 1) * 128] for c in range(8)],
                                p, xnT_t, "vector")
                eTok[("xnT", n)] = self.em.tok("vector")

            def s2():
                self.use_bank(p)
                self.w8("tensor", self.tok_wbT)
                self.w8("tensor", eTok[("xnT", n)])
                for kc in range(8):
                    self.t(lambda t, kc=kc: t.matmul(
                        self.bankf(p, 8), xnT_t[:, kc * 128:(kc + 1) * 128],
                        self.MED2[:, 3584 + kc * 8:3592 + kc * 8],
                        start=(kc == 0), stop=(kc == 7)), inc=(kc == 7))
                self.wrote_bank(p)
                self.read_bank(p, "scalar")
                if ge:
                    self.w8("scalar", ge["t"])
                    self.w8("scalar", ge["v"])
                self.s(lambda s_: s_.activation(out=ccol(p, 0), in_=self.bankf(p, 8),
                                                func=AF.Sigmoid), inc=True)
                self.bank_read_done(p, "scalar")

            def s3():
                self.w8("scalar", eTok[("ln", n)])
                if rn_prev:
                    self.w8("scalar", rn_prev)
                self.s(lambda s_: s_.activation(out=scrq, in_=xn_t,
                                                func=AF.Square), inc=True)
                self.w8("vector", self.em.tok("scalar"))
                self.v(lambda v: v.tensor_reduce(
                    out=ccol(p, 40).rearrange("p (h o) -> p h o", o=1),
                    in_=scrq.rearrange("p (h q) -> p h q", q=128), axis=AX.X,
                    op=ALU.add))
                self.v(lambda v: v.tensor_scalar(
                    out=ccol(p, 40), in0=ccol(p, 40), scalar1=1e-24, scalar2=None,
                    op0=ALU.max), inc=True)
                eTok[("rn", n)] = self.em.tok("vector")

            def s4():
                self.w8("scalar", eTok[("rn", n)])
                stok = self.s(lambda s_: s_.activation(out=ccol(p, 48),
                                                       in_=ccol(p, 40),
                                                       func=AF.Ln), inc=True)
                self.w8("scalar", stok)
                self.s(lambda s_: s_.activation(out=ccol(p, 8), in_=ccol(p, 48),
                                                func=AF.Exp, scale=-0.5), inc=True)
                eTok[("exp", n)] = self.em.tok("scalar")

            def s5():
                self.w8("vector", eTok[("exp", n)])
                self.v(lambda v: v.tensor_tensor(out=ccol(p, 16), in0=ccol(p, 8),
                                                 in1=ccol(p, 0), op=ALU.mult))
                self.v(lambda v: v.tensor_tensor(out=ccol(p, 24), in0=ccol(p, 16),
                                                 in1=HC[:, 40:48], op=ALU.mult))
                self.v(lambda v: v.tensor_tensor(out=ccol(p, 32), in0=ccol(p, 8),
                                                 in1=HC[:, 48:56], op=ALU.mult),
                       inc=True)

                def kbuild(dst, colap):
                    self.v(lambda v, dst=dst, colap=colap: v.tensor_tensor(
                        out=dst.rearrange("p (h q) -> p h q", q=128),
                        in0=xn_t.rearrange("p (h q) -> p h q", q=128),
                        in1=colap.rearrange("p (h o) -> p h o", o=1)
                            .broadcast_to((128, 8, 128)),
                        op=ALU.mult), inc=True)
                kbuild(slot(1, p), ccol(p, 8))
                kbuild(slot(2, p), ccol(p, 16))
                kbuild(slot(3, p), ccol(p, 24))
                kbuild(slot(4, p), ccol(p, 32))
                eTok[("kb", n)] = self.em.tok("vector")

            def s6():
                self.w8("tensor", eTok[("kb", n)])
                eTok[("kTe", n)] = self.transpose8(
                    [slot(1, p)[:, c * 128:(c + 1) * 128] for c in range(8)], p,
                    kT(p), "vector")
                self.v(lambda v: v.tensor_tensor(out=kdeT(p), in0=kT(p),
                                                 in1=self.DEALL[:], op=ALU.mult),
                       inc=True)
                eTok[("kdeT", n)] = self.em.tok("vector")
                self.bg_fill(4)

            def s7():
                eTok[("kbTe", n)] = self.transpose8(
                    [slot(2, p)[:, c * 128:(c + 1) * 128] for c in range(8)], p,
                    kbT(p), "scalar")

            def s8():
                self.w8("tensor", self.tok_WwT)
                for half in range(2):
                    self.use_bank(p)
                    for kc in range(8):
                        self.t(lambda t, kc=kc, half=half: t.matmul(
                            self.bankf(p), xnT_t[:, kc * 128:(kc + 1) * 128],
                            self.MED[:, kc * 1024 + half * 512:
                                      kc * 1024 + half * 512 + 512],
                            start=(kc == 0), stop=(kc == 7)), inc=(kc == 7))
                    self.wrote_bank(p)
                    self.read_bank(p, "vector")
                    self.v(lambda v, half=half: v.tensor_tensor(
                        out=slot(5, p)[:, half * 512:(half + 1) * 512]
                            .rearrange("p (h q) -> p h q", q=128),
                        in0=self.bankf(p).rearrange("p (h q) -> p h q", q=128),
                        in1=ccol(p, 4 * half, 4).rearrange("p (h o) -> p h o", o=1)
                            .broadcast_to((128, 4, 128)),
                        op=ALU.mult), inc=True)
                    self.bank_read_done(p, "vector")
                eTok[("vb", n)] = self.em.tok("vector")
                self.bg_fill(4)
            return [s1, s2, s3, s4, s5, s6, s7, s8]

        for seg in make_E(0):
            seg()
        for n in range(NT):
            p = n % 2
            eq = make_E(n + 1) if n + 1 < NT else []

            def ef(k=1):
                for _ in range(k):
                    if eq:
                        eq.pop(0)()

            def pf(i, h):
                return self.PS[:, 512 * i + h * 128:512 * i + h * 128 + 128]

            def pairf(i):
                return self.PS[:, 512 * i:512 * i + 1024]
            prs = [(2, kT, kbT), (4, kbT, kT), (6, kT, kT)]
            self.w8("tensor", eTok[("kTe", n)])
            self.w8("tensor", eTok[("kbTe", n)])
            for bk, lhs, rhs in prs:
                self.use_bank(bk)
                self.use_bank(bk + 1)
                for h in range(H):
                    self.t(lambda t, bk=bk, lhs=lhs, rhs=rhs, h=h, p=p: t.matmul(
                        pf(bk, h),
                        lhs(p)[:, h * 128:(h + 1) * 128],
                        rhs(p)[:, h * 128:(h + 1) * 128],
                        start=True, stop=True), inc=(h == H - 1))
                self.wrote_bank(bk)
                self.wrote_bank(bk + 1)
            for bk, dst, msk in [(2, U8, self.MASKU), (4, UT8, self.MASKL),
                                 (6, AT8, self.MASKA)]:
                self.read_bank(bk, "vector")
                self.read_bank(bk + 1, "vector")
                self.v(lambda v, bk=bk, dst=dst, msk=msk: v.tensor_tensor(
                    out=dst, in0=pairf(bk), in1=msk[:], op=ALU.mult), inc=True)
                self.bank_read_done(bk, "vector")
                self.bank_read_done(bk + 1, "vector")
            self.v(lambda v: v.tensor_tensor(out=Bm, in0=U8, in1=CST["eye8"][:],
                                             op=ALU.add), inc=True)
            tok_B0 = self.em.tok("vector")
            ef()
            curP, curPT = U8, UT8
            for j in range(1, 7):
                ef()
                self.bg_fill(6)
                self.use_bank(2)
                self.use_bank(3)
                self.w8("tensor", tok_B0)
                for h in range(H):
                    self.t(lambda t, h=h, curP=curP, curPT=curPT: t.matmul(
                        pf(2, h),
                        curP[:, h * 128:(h + 1) * 128],
                        curPT[:, h * 128:(h + 1) * 128],
                        start=True, stop=True), inc=(h == H - 1))
                self.wrote_bank(2)
                self.wrote_bank(3)
                if j < 6:
                    self.use_bank(4)
                    self.use_bank(5)
                    for h in range(H):
                        self.t(lambda t, h=h, curP=curP, curPT=curPT: t.matmul(
                            pf(4, h),
                            curPT[:, h * 128:(h + 1) * 128],
                            curP[:, h * 128:(h + 1) * 128],
                            start=True, stop=True), inc=(h == H - 1))
                    self.wrote_bank(4)
                    self.wrote_bank(5)
                self.read_bank(2, "scalar")
                self.read_bank(3, "scalar")
                self.w8("scalar", self.em.tok("tensor"))
                self.s(lambda s: s.copy(out=PT8, in_=pairf(2)), inc=True)
                self.bank_read_done(2, "scalar")
                self.bank_read_done(3, "scalar")
                if j < 6:
                    self.read_bank(4, "scalar")
                    self.read_bank(5, "scalar")
                    self.s(lambda s: s.copy(out=P8, in_=pairf(4)), inc=True)
                    self.bank_read_done(4, "scalar")
                    self.bank_read_done(5, "scalar")
                tok_P = self.em.tok("scalar")
                self.use_bank(6)
                self.use_bank(7)
                self.w8("tensor", tok_P)
                for h in range(H):
                    self.t(lambda t, h=h: t.matmul(
                        pf(6, h),
                        PT8[:, h * 128:(h + 1) * 128],
                        Bm[:, h * 128:(h + 1) * 128],
                        start=True, stop=True), inc=(h == H - 1))
                self.wrote_bank(6)
                self.wrote_bank(7)
                self.read_bank(6, "vector")
                self.read_bank(7, "vector")
                self.v(lambda v: v.tensor_tensor(out=Bm, in0=pairf(6), in1=Bm,
                                                 op=ALU.add), inc=True)
                self.bank_read_done(6, "vector")
                self.bank_read_done(7, "vector")
                tok_B0 = self.em.tok("vector")
                curP, curPT = P8, PT8
            self.use_bank(6)
            self.use_bank(7)
            self.w8("tensor", tok_B0)
            for h in range(H):
                self.t(lambda t, h=h, p=p: t.matmul(
                    pf(6, h),
                    slot(3, p)[:, h * 128:(h + 1) * 128],
                    Bm[:, h * 128:(h + 1) * 128],
                    start=True, stop=True), inc=(h == H - 1))
            self.wrote_bank(6)
            self.wrote_bank(7)
            self.read_bank(6, "scalar")
            self.read_bank(7, "scalar")
            self.s(lambda s: s.copy(out=kcdT8, in_=pairf(6)), inc=True)
            self.bank_read_done(6, "scalar")
            self.bank_read_done(7, "scalar")
            tok_kcd = self.em.tok("scalar")
            ef()
            for half in range(2):
                hs = list(range(4 * half, 4 * half + 4))
                self.use_bank(2)
                self.w8("tensor", tok_kcd)
                self.w8("tensor", eTok[("vb", n)])
                for h in hs:
                    q = (h % 4) * 128
                    self.t(lambda t, h=h, q=q, p=p: t.matmul(
                        self.bankf(2)[:, q:q + 128],
                        Bm[:, h * 128:(h + 1) * 128],
                        slot(5, p)[:, h * 128:(h + 1) * 128],
                        start=True, stop=False))
                    self.t(lambda t, h=h, q=q: t.matmul(
                        self.bankf(2)[:, q:q + 128],
                        kcdT8[:, h * 128:(h + 1) * 128],
                        SST[:, h * 128:(h + 1) * 128],
                        start=False, stop=True), inc=(h == hs[-1]))
                self.wrote_bank(2)
                self.read_bank(2, "vector")
                self.v(lambda v, half=half: v.tensor_copy(
                    out=vnew8[:, half * 512:(half + 1) * 512], in_=self.bankf(2)),
                    inc=True)
                self.bank_read_done(2, "vector")
                tok_vn = self.em.tok("vector")
                self.use_bank(3)
                self.w8("tensor", tok_vn)
                self.w8("tensor", eTok[("kdeT", n)])
                for h in hs:
                    q = (h % 4) * 128
                    self.t(lambda t, h=h, q=q, p=p: t.matmul(
                        self.bankf(3)[:, q:q + 128],
                        kdeT(p)[:, h * 128:(h + 1) * 128],
                        SST[:, h * 128:(h + 1) * 128],
                        start=True, stop=False))
                    self.t(lambda t, h=h, q=q: t.matmul(
                        self.bankf(3)[:, q:q + 128],
                        AT8[:, h * 128:(h + 1) * 128],
                        vnew8[:, h * 128:(h + 1) * 128],
                        start=False, stop=True), inc=(h == hs[-1]))
                self.wrote_bank(3)
                self.use_bank(4)
                for h in hs:
                    q = (h % 4) * 128
                    self.t(lambda t, h=h, q=q, p=p: t.matmul(
                        self.bankf(4)[:, q:q + 128],
                        slot(4, p)[:, h * 128:(h + 1) * 128],
                        vnew8[:, h * 128:(h + 1) * 128],
                        start=True, stop=False))
                    self.t(lambda t, h=h, q=q: t.matmul(
                        self.bankf(4)[:, q:q + 128],
                        self.CTDE[:, h * 128:(h + 1) * 128],
                        SST[:, h * 128:(h + 1) * 128],
                        start=False, stop=True), inc=(h == hs[-1]))
                self.wrote_bank(4)
                self.read_bank(3, "scalar")
                self.s(lambda s, half=half, p=p: s.copy(
                    out=slot(6, p)[:, half * 512:(half + 1) * 512],
                    in_=self.bankf(3)), inc=True)
                self.bank_read_done(3, "scalar")
                self.read_bank(4, "vector")
                self.v(lambda v, half=half: v.tensor_copy(
                    out=SST[:, half * 512:(half + 1) * 512], in_=self.bankf(4)),
                    inc=True)
                self.bank_read_done(4, "vector")
                ef()
                self.bg_fill(6)
            tok_o = self.em.tok("scalar")
            self.w8("tensor", tok_o)
            tok_oT = self.transpose8(
                [slot(6, p)[:, c * 128:(c + 1) * 128] for c in range(8)], 1 - p,
                slot(7, p), "vector")
            self.w8("tensor", self.tok_WrT)
            for half in range(2):
                bk = 5 + half
                self.use_bank(bk)
                self.w8("tensor", tok_oT)
                for kc in range(8):
                    self.t(lambda t, bk=bk, kc=kc, half=half, p=p: t.matmul(
                        self.bankf(bk), slot(7, p)[:, kc * 128:(kc + 1) * 128],
                        self.BC[:, 2048 + kc * 1024 + half * 512:
                                2048 + kc * 1024 + half * 512 + 512],
                        start=(kc == 0), stop=(kc == 7)), inc=(kc == 7))
                self.wrote_bank(bk)
                self.read_bank(bk, "vector")
                self.resid_update(n, half, self.bankf(bk),
                                  extra=slot(0, p)[:, half * 512:(half + 1) * 512])
                self.bank_read_done(bk, "vector")
            while eq:
                eq.pop(0)()
            aa_end[p] = dict(t=self.em.tok("tensor"), v=self.em.tok("vector"),
                             s=self.em.tok("scalar"))
            self.bg_fill(10)

    def ffn(self, first):
        if first:
            self.prep_drain()
        em, X, STAT = self.em, self.X, self.STAT
        BA, BB, BC, MED, MED2 = self.BA, self.BB, self.BC, self.MED, self.MED2

        def hgrp(i):
            return BC[:, i * II:(i + 1) * II]

        def xn_t(p):
            return BC[:, 11264 + p * 2048:11264 + p * 2048 + 2048].bitcast(F32)

        def xq_t(p):
            return BC[:, 15360 + p * 1024:15360 + p * 1024 + 1024]

        def xqT(c, i):
            return MED2[:, c * 512 + i * 128:c * 512 + i * 128 + 128]

        def hq_t(p):
            return BA[:, 0:II] if p == 0 else BB[:, 11264:11264 + II]

        def hqT_g(i):
            return BB[:, i * 2816:(i + 1) * 2816]

        def wslice(b):
            return MED[:, 0:8192] if b == 0 else BB[:, 0:8192]
        sil = BB[:, 14336:14848]
        wg_r = self.scr["wgT"].rearrange("(c p) i -> p c i", p=128)
        wu_r = self.scr["wuT"].rearrange("(c p) i -> p c i", p=128)
        wd_r = self.scr["wdT"].rearrange("(c p) d -> c p d", p=128)

        wdslot = [BA[:, 14080:15104], BA[:, 15104:16128], BB[:, 15360:16384]]
        g0t = em.tok("tensor")
        g0v = em.tok("vector")
        self._tok_gu = g0t
        self._tok_hqp = {0: g0t, 1: g0t}
        dtoks = {}
        wdq = {"q": 0}
        xn_pair = BC[:, 11264:15360].bitcast(F32)
        xq_pair = BC[:, 15360:17408]

        def emit_xq(grp, guard):
            tiles = list(range(grp * 4, grp * 4 + 4))
            for pi in range(2):
                self.w8("vector", guard)
                self.ln_apply(tiles[2 * pi], xn_t(0))
                self.ln_apply(tiles[2 * pi + 1], xn_t(1))
                self.quant_tile(xn_pair, xq_pair, 2048, src_f32=True)
                tok_xq = self.em.tok("vector")
                self.w8("tensor", tok_xq)
                for half in range(2):
                    i = 2 * pi + half
                    dst3 = MED2[:].rearrange("p (c q) -> p c q", q=512) \
                        [:, :, i * 128:(i + 1) * 128]
                    self.transpose8([xq_t(half)[:, c * 128:(c + 1) * 128]
                                     for c in range(8)],
                                    4 + (i % 2), dst3, "vector")
            self._tok_xqT = self.em.tok("vector")
            self._ffn_gtok = self.em.tok("tensor")

        def hquant(i, n):
            p = n % 2
            self.w8("vector", self._tok_hqp[p])
            self.quant_tile(hgrp(i), hq_t(p), II)
            self.w8("tensor", self.em.tok("vector"))
            for bi, lo in enumerate((0, 8, 16)):
                cnt = 8 if bi < 2 else 6
                self.transpose8(
                    [hq_t(p)[:, (lo + c) * 128:(lo + c + 1) * 128]
                     for c in range(cnt)],
                    4 + (bi % 2), hqT_g(i)[:, lo * 128:(lo + cnt) * 128], "vector")
            self._tok_hqp[p] = self.em.tok("tensor")

        def down_pass(pair):
            for li, i, n in pair:
                self.use_bank(2 * li)
                self.use_bank(2 * li + 1)
            self.w8("tensor", self.em.tok("vector"))
            for kc in range(NI):
                q = wdq["q"]
                slot = wdslot[q % 3]
                guard = [dtoks.get(q - 3), *self.tok_ffnprep]
                if q < 3:
                    guard += [g0t, g0v]
                dtok = self.dq(f"wd{q % 3}").start(slot, wd_r[kc], wait=guard)
                self.w8("tensor", dtok)
                for li, i, n in pair:
                    for half in range(2):
                        self.t(lambda t, li=li, i=i, kc=kc, half=half, slot=slot:
                               t.matmul(
                                   self.bankf(2 * li + half),
                                   hqT_g(i)[:, kc * 128:(kc + 1) * 128],
                                   slot[:, half * 512:(half + 1) * 512],
                                   start=(kc == 0), stop=(kc == NI - 1)),
                               inc=(li == pair[-1][0] and half == 1))
                dtoks[q] = self.em.tok("tensor")
                wdq["q"] = q + 1
            for li, i, n in pair:
                self.wrote_bank(2 * li)
                self.wrote_bank(2 * li + 1)

        def resid_pass(pair):
            for li, i, n in pair:
                for half in range(2):
                    bk = 2 * li + half
                    self.read_bank(bk, "vector")
                    self.resid_update(n, half, self.bankf(bk))
                    self.bank_read_done(bk, "vector")

        emit_xq(0, em.tok("tensor"))
        for grp in range(4):
            tiles = list(range(grp * 4, grp * 4 + 4))
            for nch in range(6):
                ncols = 512 if nch < 5 else 256
                wb = nch % 2
                wsl = wslice(wb)
                if wb == 0:
                    guard = [*self.tok_ffnprep, self._tok_gu]
                else:
                    guard = [*self.tok_ffnprep, self.em.tok("tensor")]
                tok_wg = self.dq(f"wsl{wb}").start(
                    wsl[:, 0:8 * ncols].rearrange("p (c q) -> p c q", q=ncols),
                    wg_r[:, :, nch * 512:nch * 512 + ncols], wait=guard)
                tok_wu = self.dq(f"wslu{wb}").start(
                    wsl[:, 4096:4096 + 8 * ncols].rearrange("p (c q) -> p c q", q=ncols),
                    wu_r[:, :, nch * 512:nch * 512 + ncols], wait=guard)
                self.w8("tensor", tok_wg)
                self.w8("tensor", tok_wu)
                self.w8("tensor", self._tok_xqT)
                for i, n in enumerate(tiles):
                    bkg, bku = 2 * i, 2 * i + 1
                    self.use_bank(bkg)
                    self.use_bank(bku)
                    for kc in range(8):
                        self.t(lambda t, bkg=bkg, kc=kc, i=i, wsl=wsl, ncols=ncols:
                               t.matmul(self.bankf(bkg, ncols), xqT(kc, i),
                                        wsl[:, kc * ncols:(kc + 1) * ncols],
                                        start=(kc == 0), stop=(kc == 7)))
                        self.t(lambda t, bku=bku, kc=kc, i=i, wsl=wsl, ncols=ncols:
                               t.matmul(self.bankf(bku, ncols), xqT(kc, i),
                                        wsl[:, 4096 + kc * ncols:4096 + (kc + 1) * ncols],
                                        start=(kc == 0), stop=(kc == 7)),
                               inc=(kc == 7))
                    self.wrote_bank(bkg)
                    self.wrote_bank(bku)
                    self.read_bank(bkg, "scalar")
                    self.w8("scalar", self.em.tok("vector"))
                    self.s(lambda s, bkg=bkg, ncols=ncols: s.activation(
                        out=sil[:, 0:ncols], in_=self.bankf(bkg, ncols),
                        func=AF.Silu), inc=True)
                    self.bank_read_done(bkg, "scalar")
                    self.read_bank(bku, "vector")
                    self.w8("vector", self.em.tok("scalar"))
                    self.v(lambda v, bku=bku, i=i, nch=nch, ncols=ncols:
                           v.tensor_tensor(
                               out=hgrp(i)[:, nch * 512:nch * 512 + ncols],
                               in0=self.bankf(bku, ncols), in1=sil[:, 0:ncols],
                               op=ALU.mult), inc=True)
                    self.bank_read_done(bku, "vector")
            self._tok_gu = self.em.tok("tensor")
            pair0 = [(0, 0, tiles[0]), (1, 1, tiles[1])]
            pair1 = [(0, 2, tiles[2]), (1, 3, tiles[3])]
            hquant(0, tiles[0])
            hquant(1, tiles[1])
            down_pass(pair0)
            hquant(2, tiles[2])
            hquant(3, tiles[3])
            resid_pass(pair0)
            down_pass(pair1)
            if grp < 3:
                emit_xq(grp + 1, self._ffn_gtok)
            resid_pass(pair1)

    def quant_tile(self, src, dst, F, src_f32=False, sbase=None, eng="vector",
                   qc=None):
        nb = F // 16
        ST, c0 = sbase if sbase is not None else (self.BA, 2816)
        f32A = ST[:, c0:c0 + 2 * F].bitcast(F32)
        f32B = ST[:, c0 + 2 * F:c0 + 4 * F].bitcast(F32)
        rB = ST[:, c0 + 2 * F:c0 + 3 * F]
        d1B = ST[:, c0 + 3 * F:c0 + 4 * F]
        d2B = ST[:, c0:c0 + F]
        magB = ST[:, c0 + F:c0 + 2 * F]
        habs = ST[:, c0 + 3 * F:c0 + 4 * F]
        QC = qc if qc is not None else self.QCOL

        def v(fn, inc=False):
            return self.em.emit(eng, fn, inc)
        src3 = src.rearrange("p (b k) -> p b k", k=16)
        self.v(lambda v_: v_.tensor_reduce(
            out=QC[:, 0:nb].rearrange("p (b o) -> p b o", o=1), in_=src3,
            axis=AX.X, op=ALU.max, apply_absolute_value=True))
        self.v(lambda v_: v_.tensor_scalar(out=QC[:, 0:nb], in0=QC[:, 0:nb],
                                           scalar1=6e-12, scalar2=None, op0=ALU.max))
        self.v(lambda v_: v_.reciprocal(out=QC[:, 176:176 + nb], in_=QC[:, 0:nb]))
        self.v(lambda v_: v_.tensor_scalar(out=QC[:, 176:176 + nb],
                                           in0=QC[:, 176:176 + nb],
                                           scalar1=12.0, scalar2=None, op0=ALU.mult))
        self.v(lambda v_: v_.tensor_scalar(out=QC[:, 352:352 + nb], in0=QC[:, 0:nb],
                                           scalar1=1.0 / 12.0, scalar2=None,
                                           op0=ALU.mult), inc=True)
        if eng != "vector":
            self.em.wait(eng, self.em.tok("vector"))
        if eng != "vector":
            assert src_f32
            self._quant_pool_body(src, dst, F, QC, ST, c0, nb, v)
            return
        inv3 = QC[:, 176:176 + nb].rearrange("p (b o) -> p b o", o=1) \
            .broadcast_to((128, nb, 16))
        sc3 = QC[:, 352:352 + nb].rearrange("p (b o) -> p b o", o=1) \
            .broadcast_to((128, nb, 16))
        if src_f32 and eng != "vector":
            v(lambda v_: v_.tensor_scalar(out=f32A, in0=src, scalar1=0.0,
                                          scalar2=2.0, op0=ALU.is_ge, op1=ALU.mult))
            v(lambda v_: v_.scalar_tensor_tensor(out=f32A, in0=f32A, scalar=-1.0,
                                                 in1=src, op0=ALU.add, op1=ALU.mult))
            v(lambda v_: v_.scalar_tensor_tensor(
                out=f32A.rearrange("p (b k) -> p b k", k=16),
                in0=f32A.rearrange("p (b k) -> p b k", k=16), scalar=12.0, in1=inv3,
                op0=ALU.mult, op1=ALU.mult))
        elif src_f32:
            v(lambda v_: v_.tensor_scalar(
                out=f32A.bitcast(mybir.dt.uint32), in0=src.bitcast(mybir.dt.uint32),
                scalar1=0x7FFFFFFF, scalar2=None, op0=ALU.bitwise_and))
            v(lambda v_: v_.tensor_tensor(
                out=f32A.rearrange("p (b k) -> p b k", k=16),
                in0=f32A.rearrange("p (b k) -> p b k", k=16), in1=inv3,
                op=ALU.mult))
        else:
            v(lambda v_: v_.tensor_scalar(out=habs.bitcast(U16), in0=src.bitcast(U16),
                                          scalar1=0x7FFF, scalar2=None,
                                          op0=ALU.bitwise_and))
            v(lambda v_: v_.scalar_tensor_tensor(
                out=f32A.rearrange("p (b k) -> p b k", k=16),
                in0=habs.rearrange("p (b k) -> p b k", k=16), scalar=1.0, in1=inv3,
                op0=ALU.mult, op1=ALU.mult))
        M23 = 8388608.0
        v(lambda v_: v_.tensor_scalar(out=f32B, in0=f32A, scalar1=8.0, scalar2=12.0,
                                      op0=ALU.max, op1=ALU.min))
        v(lambda v_: v_.scalar_tensor_tensor(out=f32B, in0=f32B, scalar=-0.75,
                                             in1=f32A, op0=ALU.mult, op1=ALU.add))
        v(lambda v_: v_.tensor_scalar(out=f32A, in0=f32A, scalar1=4.0, scalar2=8.0,
                                      op0=ALU.max, op1=ALU.min))
        v(lambda v_: v_.scalar_tensor_tensor(out=f32B, in0=f32A, scalar=-0.5,
                                             in1=f32B, op0=ALU.mult, op1=ALU.add))
        v(lambda v_: v_.tensor_scalar(out=f32A, in0=f32B, scalar1=16.0 + M23,
                                      scalar2=None, op0=ALU.add))
        v(lambda v_: v_.tensor_scalar(out=rB, in0=f32A, scalar1=-M23, scalar2=None,
                                      op0=ALU.add))
        v(lambda v_: v_.tensor_scalar(out=d1B, in0=rB, scalar1=12.0, scalar2=14.0,
                                      op0=ALU.max, op1=ALU.min))
        v(lambda v_: v_.tensor_scalar(out=d2B, in0=rB, scalar1=14.0, scalar2=15.0,
                                      op0=ALU.max, op1=ALU.min))
        v(lambda v_: v_.tensor_tensor(out=d1B, in0=d1B, in1=rB, op=ALU.add))
        v(lambda v_: v_.scalar_tensor_tensor(out=d1B, in0=d2B, scalar=3.0, in1=d1B,
                                             op0=ALU.mult, op1=ALU.add))
        v(lambda v_: v_.scalar_tensor_tensor(
            out=magB.rearrange("p (b k) -> p b k", k=16),
            in0=d1B.rearrange("p (b k) -> p b k", k=16), scalar=-62.0, in1=sc3,
            op0=ALU.add, op1=ALU.mult))
        if src_f32:
            shi = src.bitcast(U16).rearrange("p (k two) -> p k two", two=2)[:, :, 1:2] \
                .rearrange("p k o -> p (k o)")
        else:
            shi = src.bitcast(U16)
        if eng != "vector":
            self.w8("vector", self.em.tok(eng))
        self.v(lambda v_: v_.tensor_scalar(out=dst.bitcast(U16), in0=shi,
                                           scalar1=0x8000, scalar2=None,
                                           op0=ALU.bitwise_and))
        self.v(lambda v_: v_.tensor_tensor(out=dst.bitcast(U16),
                                           in0=dst.bitcast(U16),
                                           in1=magB.bitcast(U16),
                                           op=ALU.bitwise_or), inc=True)
        if eng != "vector":
            self.em.wait(eng, self.em.tok("vector"))

    def _quant_pool_body(self, src, dst, F, QC, ST, c0, nb, v, src_f32=True):
        f32A = ST[:, c0:c0 + 2 * F].bitcast(F32)
        f32B = ST[:, c0 + 2 * F:c0 + 4 * F].bitcast(F32)
        rB = ST[:, c0 + 2 * F:c0 + 3 * F]
        d1B = ST[:, c0 + 3 * F:c0 + 4 * F]
        d2B = ST[:, c0:c0 + F]
        magB = ST[:, c0 + F:c0 + 2 * F]
        inv3 = QC[:, 176:176 + nb].rearrange("p (b o) -> p b o", o=1) \
            .broadcast_to((128, nb, 16))
        sc3 = QC[:, 352:352 + nb].rearrange("p (b o) -> p b o", o=1) \
            .broadcast_to((128, nb, 16))
        M23 = 8388608.0
        v(lambda v_: v_.tensor_scalar(out=f32B, in0=src, scalar1=0.0, scalar2=2.0,
                                      op0=ALU.is_ge, op1=ALU.mult))
        v(lambda v_: v_.tensor_scalar(out=f32B, in0=f32B, scalar1=-1.0,
                                      scalar2=None, op0=ALU.add))
        v(lambda v_: v_.tensor_tensor(out=f32A, in0=f32B, in1=src, op=ALU.mult))
        v(lambda v_: v_.tensor_tensor(
            out=f32A.rearrange("p (b k) -> p b k", k=16),
            in0=f32A.rearrange("p (b k) -> p b k", k=16), in1=inv3, op=ALU.mult))
        v(lambda v_: v_.tensor_scalar(out=f32B, in0=f32A, scalar1=8.0, scalar2=12.0,
                                      op0=ALU.max, op1=ALU.min))
        v(lambda v_: v_.tensor_scalar(out=f32B, in0=f32B, scalar1=-0.75,
                                      scalar2=None, op0=ALU.mult))
        v(lambda v_: v_.tensor_tensor(out=f32B, in0=f32B, in1=f32A, op=ALU.add))
        v(lambda v_: v_.tensor_scalar(out=f32A, in0=f32A, scalar1=4.0, scalar2=8.0,
                                      op0=ALU.max, op1=ALU.min))
        v(lambda v_: v_.tensor_scalar(out=f32A, in0=f32A, scalar1=-0.5,
                                      scalar2=16.0, op0=ALU.mult, op1=ALU.add))
        v(lambda v_: v_.tensor_tensor(out=f32A, in0=f32A, in1=f32B, op=ALU.add),
          inc=True)
        self.w8("vector", self.em.tok("gpsimd"))
        self.v(lambda v_: v_.tensor_scalar(out=f32B, in0=f32A, scalar1=M23,
                                           scalar2=None, op0=ALU.add))
        self.v(lambda v_: v_.tensor_scalar(out=magB, in0=f32B, scalar1=-M23,
                                           scalar2=None, op0=ALU.add), inc=True)
        self.em.wait("gpsimd", self.em.tok("vector"))
        v(lambda v_: v_.tensor_scalar(out=d1B, in0=magB, scalar1=12.0, scalar2=14.0,
                                      op0=ALU.max, op1=ALU.min))
        v(lambda v_: v_.tensor_tensor(out=d1B, in0=d1B, in1=magB, op=ALU.add))
        v(lambda v_: v_.tensor_scalar(out=d2B, in0=magB, scalar1=14.0, scalar2=15.0,
                                      op0=ALU.max, op1=ALU.min))
        v(lambda v_: v_.tensor_scalar(out=d2B, in0=d2B, scalar1=3.0, scalar2=-62.0,
                                      op0=ALU.mult, op1=ALU.add))
        v(lambda v_: v_.tensor_tensor(out=d1B, in0=d1B, in1=d2B, op=ALU.add))
        v(lambda v_: v_.tensor_tensor(
            out=magB.rearrange("p (b k) -> p b k", k=16),
            in0=d1B.rearrange("p (b k) -> p b k", k=16), in1=sc3, op=ALU.mult),
          inc=True)
        if src_f32:
            shi = src.bitcast(U16).rearrange("p (k two) -> p k two", two=2) \
                [:, :, 1:2].rearrange("p k o -> p (k o)")
        else:
            shi = src.bitcast(U16)
        self.w8("vector", self.em.tok("gpsimd"))
        self.v(lambda v_: v_.tensor_scalar(out=dst.bitcast(U16), in0=shi,
                                           scalar1=0x8000, scalar2=None,
                                           op0=ALU.bitwise_and))
        self.v(lambda v_: v_.tensor_tensor(out=dst.bitcast(U16),
                                           in0=dst.bitcast(U16),
                                           in1=magB.bitcast(U16),
                                           op=ALU.bitwise_or), inc=True)
        self.em.wait("gpsimd", self.em.tok("vector"))

    def prep_overlay_init(self):
        tasks = []
        for nm_src, nm_dst in (("wg", "wgT"), ("wu", "wuT")):
            src = self.w[nm_src].rearrange("(c p) d -> c p d", p=128)
            dst = self.scr[nm_dst].rearrange("(c p) i -> p c i", p=128)
            for r in range(NI):
                tasks.append(dict(src=src[r], F=1024, cnt=8,
                                  dst=dst[:, :, r * 128:(r + 1) * 128]))
        srcd = self.w["wd"].rearrange("(c p) i -> c p i", p=128)
        dstd = self.scr["wdT"].rearrange("(c p) d -> p c d", p=128)
        for r in range(8):
            for lo, F in ((0, 1024), (1024, 1024), (2048, 768)):
                tasks.append(dict(
                    src=srcd[r][:, lo:lo + F], F=F, cnt=F // 128,
                    dst=dstd[:, lo // 128:(lo + F) // 128, r * 128:(r + 1) * 128]))
        self._ptasks = tasks
        self._ptok = {}
        self._pi = 0
        self._iv_queue = []
        self._iv_on = True
        self._iv_k = 1
        self._pring = [self.BC[:, 16384:17408], self.BB[:, 15360:16384],
                       self.MED2[:, 2048:3072]]
        self._prep_S1(0)
        self._prep_S1(1)
        self.bg_fill(0)

    def _pstage(self, i):
        return self.BC[:, 10240 + (i % 2) * 2048:10240 + (i % 2) * 2048 + 2048] \
            .bitcast(F32)

    def _pqdst(self, i, F):
        return self._pring[i % 3][:, 0:F]

    def _prep_S1(self, i):
        t = self._ptasks[i]
        F = t["F"]
        slot = self._pstage(i)[:, 0:F]
        prev = self._ptok.get(("o", i - 2), [])
        h = F // 2
        ta = self.dq(f"fstg{i % 2}a", "gpsimd").start(
            slot[:, 0:h], t["src"][:, 0:h], wait=prev)
        tb = self.dq(f"fstg{i % 2}b", "gpsimd").start(
            slot[:, h:F], t["src"][:, h:F], wait=prev)
        self._ptok[("l", i)] = [ta, tb]

    def _ptrp(self, i, F):
        stg0 = 10240 + (i % 2) * 2048
        return self.BC[:, stg0:stg0 + F]

    def _prep_batch(self, i):
        t = self._ptasks[i]
        F, cnt = t["F"], t["cnt"]
        qdst = self._pqdst(i, F)
        trp = self._ptrp(i, F)
        w = [self._ptok[("qv", i)], *self._ptok.get(("o", i - 2), [])]
        hc = (cnt + 1) // 2
        ta = self.dq("bgt0").start_t_batch(
            [(trp[:, c * 128:(c + 1) * 128], qdst[:, c * 128:(c + 1) * 128])
             for c in range(hc)], wait=w)
        tb = self.dq("bgt1").start_t_batch(
            [(trp[:, c * 128:(c + 1) * 128], qdst[:, c * 128:(c + 1) * 128])
             for c in range(hc, cnt)], wait=w)
        self._ptok[("t", i)] = [ta, tb]

    def _prep_store(self, i):
        t = self._ptasks[i]
        cnt = t["cnt"]
        trp3 = self._ptrp(i, t["F"]).rearrange("p (c q) -> p c q", q=128)
        tt = self._ptok[("t", i)]
        hc = (cnt + 1) // 2
        oa = self.dq("prepout0", "gpsimd").start(
            t["dst"][:, 0:hc, :], trp3[:, 0:hc], wait=[tt[0]])
        ob = self.dq("prepout1", "gpsimd").start(
            t["dst"][:, hc:cnt, :], trp3[:, hc:cnt], wait=[tt[1]])
        self._ptok[("o", i)] = [oa, ob]

    def _enqueue_task(self, i):
        q = self._iv_queue
        n = len(self._ptasks)
        q.extend(self._quant_ov_ops(i))
        q.append(lambda i=i: self._prep_batch(i))
        if i - 1 >= 0:
            q.append(lambda i=i: self._prep_store(i - 1))
        if 2 <= i + 1 < n:
            q.append(lambda i=i: self._prep_S1(i + 1))

    def bg_fill(self, k):
        if not hasattr(self, "_ptasks"):
            return
        n = len(self._ptasks)
        while len(self._iv_queue) < 48 and self._pi < n:
            self._enqueue_task(self._pi)
            self._pi += 1
        self._iv_on = False
        for _ in range(min(k, len(self._iv_queue))):
            self._iv_queue.pop(0)()
        self._iv_on = True

    def prep_drain(self):
        n = len(self._ptasks)
        self._iv_on = False
        while self._pi < n:
            self._enqueue_task(self._pi)
            self._pi += 1
        while self._iv_queue:
            self._iv_queue.pop(0)()
        self._prep_store(n - 1)
        for tk in self._ptok[("o", n - 1)]:
            self.w8("vector", tk)
        self.tok_ffnprep = [
            (self.dq("prepout0", "gpsimd"), self.dq("prepout0", "gpsimd").cnt),
            (self.dq("prepout1", "gpsimd"), self.dq("prepout1", "gpsimd").cnt)]

    def _quant_ov_ops(self, i):
        t = self._ptasks[i]
        F = t["F"]
        nb = F // 16
        QC = self.QCOL
        M23 = 8388608.0
        stg0 = 10240 + (i % 2) * 2048
        srcf = self._pstage(i)[:, 0:F]
        dst = self._pqdst(i, F)
        f32A = self.BC[:, 14336:14336 + 2 * F].bitcast(F32)
        f32B = srcf
        rB = self.BC[:, stg0:stg0 + F]
        d1B = self.BC[:, stg0 + F:stg0 + 2 * F]
        d2B = self.BC[:, 14336:14336 + F]
        magB = self.BC[:, 14336 + F:14336 + 2 * F]
        src3 = srcf.rearrange("p (b k) -> p b k", k=16)
        f32A3 = f32A.rearrange("p (b k) -> p b k", k=16)
        inv3 = QC[:, 176:176 + nb].rearrange("p (b o) -> p b o", o=1) \
            .broadcast_to((128, nb, 16))
        sc3 = QC[:, 352:352 + nb].rearrange("p (b o) -> p b o", o=1) \
            .broadcast_to((128, nb, 16))
        shi = srcf.bitcast(U16).rearrange("p (k two) -> p k two", two=2)[:, :, 1:2] \
            .rearrange("p k o -> p (k o)")
        U32 = mybir.dt.uint32
        ops = []

        def first():
            for tk in self._ptok.get(("l", i), []):
                self.w8("vector", tk)
            self.v(lambda v: v.tensor_reduce(
                out=QC[:, 0:nb].rearrange("p (b o) -> p b o", o=1), in_=src3,
                axis=AX.X, op=ALU.max, apply_absolute_value=True))
        ops.append(first)
        ops.append(lambda: self.v(lambda v: v.tensor_scalar(
            out=QC[:, 0:nb], in0=QC[:, 0:nb], scalar1=6e-12, scalar2=None,
            op0=ALU.max)))
        ops.append(lambda: self.v(lambda v: v.reciprocal(
            out=QC[:, 176:176 + nb], in_=QC[:, 0:nb])))
        ops.append(lambda: self.v(lambda v: v.tensor_scalar(
            out=QC[:, 176:176 + nb], in0=QC[:, 176:176 + nb], scalar1=12.0,
            scalar2=None, op0=ALU.mult)))
        ops.append(lambda: self.v(lambda v: v.tensor_scalar(
            out=QC[:, 352:352 + nb], in0=QC[:, 0:nb], scalar1=1.0 / 12.0,
            scalar2=None, op0=ALU.mult)))

        def sign():
            for tk in self._ptok.get(("t", i - 3), []):
                self.w8("vector", tk)
            self.v(lambda v: v.tensor_scalar(
                out=dst.bitcast(U16), in0=shi, scalar1=0x8000, scalar2=None,
                op0=ALU.bitwise_and))
        ops.append(sign)
        ops.append(lambda: self.v(lambda v: v.tensor_scalar(
            out=f32A.bitcast(U32), in0=srcf.bitcast(U32), scalar1=0x7FFFFFFF,
            scalar2=None, op0=ALU.bitwise_and)))
        ops.append(lambda: self.v(lambda v: v.tensor_tensor(
            out=f32A3, in0=f32A3, in1=inv3, op=ALU.mult)))
        ops.append(lambda: self.v(lambda v: v.tensor_scalar(
            out=f32B, in0=f32A, scalar1=8.0, scalar2=12.0, op0=ALU.max,
            op1=ALU.min)))
        ops.append(lambda: self.v(lambda v: v.scalar_tensor_tensor(
            out=f32B, in0=f32B, scalar=-0.75, in1=f32A, op0=ALU.mult,
            op1=ALU.add)))
        ops.append(lambda: self.v(lambda v: v.tensor_scalar(
            out=f32A, in0=f32A, scalar1=4.0, scalar2=8.0, op0=ALU.max,
            op1=ALU.min)))
        ops.append(lambda: self.v(lambda v: v.scalar_tensor_tensor(
            out=f32B, in0=f32A, scalar=-0.5, in1=f32B, op0=ALU.mult,
            op1=ALU.add)))
        ops.append(lambda: self.v(lambda v: v.tensor_scalar(
            out=f32A, in0=f32B, scalar1=16.0 + M23, scalar2=None, op0=ALU.add)))
        ops.append(lambda: self.v(lambda v: v.tensor_scalar(
            out=rB, in0=f32A, scalar1=-M23, scalar2=None, op0=ALU.add)))
        ops.append(lambda: self.v(lambda v: v.tensor_scalar(
            out=d1B, in0=rB, scalar1=12.0, scalar2=14.0, op0=ALU.max,
            op1=ALU.min)))
        ops.append(lambda: self.v(lambda v: v.tensor_scalar(
            out=d2B, in0=rB, scalar1=14.0, scalar2=15.0, op0=ALU.max,
            op1=ALU.min)))
        ops.append(lambda: self.v(lambda v: v.tensor_tensor(
            out=d1B, in0=d1B, in1=rB, op=ALU.add)))
        ops.append(lambda: self.v(lambda v: v.scalar_tensor_tensor(
            out=d1B, in0=d2B, scalar=3.0, in1=d1B, op0=ALU.mult, op1=ALU.add)))
        ops.append(lambda: self.v(lambda v: v.scalar_tensor_tensor(
            out=magB.rearrange("p (b k) -> p b k", k=16),
            in0=d1B.rearrange("p (b k) -> p b k", k=16), scalar=-62.0, in1=sc3,
            op0=ALU.add, op1=ALU.mult)))

        def last():
            self.v(lambda v: v.tensor_tensor(
                out=dst.bitcast(U16), in0=dst.bitcast(U16),
                in1=magB.bitcast(U16), op=ALU.bitwise_or), inc=True)
            self._ptok[("qv", i)] = self.em.tok("vector")
        ops.append(last)
        return ops

    def attn_b(self):
        em, CST, X, STAT = self.em, self.CST, self.X, self.STAT
        BA, BB, BC, MED, MED2 = self.BA, self.BB, self.BC, self.MED, self.MED2
        qkv_r = self.scr["qkv16"].rearrange("(c p) e -> p c e", p=128)
        wo_r64 = self.scr["wo16"].rearrange("(c p) d -> c p d", p=64)
        aob_r = self.scr["aob"].rearrange("(n p) d -> p n d", p=128)

        def qro(n):
            return BA[:, n * 512:(n + 1) * 512]

        def kro(n):
            return BB[:, n * 512:(n + 1) * 512]

        def vpk(kt, hh=None):
            base = 8192 + (kt // 2) * 520 + (hh * 65 if hh is not None else 0)
            buf = BB if kt % 2 == 0 else BA
            return buf[:, base:base + (65 if hh is not None else 520)]

        def aoT(hh):
            return BC[0:64, hh * 2048:(hh + 1) * 2048]
        qTh = MED2[:, 0:2048]
        kTh = MED2[:, 2048:4096]

        def pT(b):
            return MED[:, b * 1024:(b + 1) * 1024]
        xn_scr = BC[:, 16384:17408]
        ZB = BC[0:64, 16384:17408]
        ZR = BB[:, 14336:16384].bitcast(F32)[0:1, :]
        ZRB = BB[0:1, 13312:14336]
        self._aob_st = {}
        self.v(lambda v: v.memset(MED2[64:128, 0:4096], 0.0), inc=True)
        self._pend_epi = None
        self._pend_epi2 = None
        self._last_pv = None
        self._tok_repl = None
        self._tok_z = None
        for hp in range(2):
            self.w8("vector", self._aob_st.get(0))
            self.w8("vector", self._aob_st.get(1))
            _t0 = self.em.tok("tensor")
            _tt = {}
            for n in range(NT):
                xs = xn_scr if n % 2 == 0 else MED2[:, 0:1024]
                self.w8("vector", _tt.get(n - 2, _t0))
                tok_xn = self.ln_apply(n, xs)
                self.w8("tensor", tok_xn)
                dst = BC[:, 0:16384].rearrange("p (c q) -> p c q", q=2048)[:, :, n * 128:(n + 1) * 128]
                self.transpose8([xs[:, c * 128:(c + 1) * 128] for c in range(8)],
                                n % 4, dst, "vector")
                _tt[n] = self.em.tok("tensor")
            self.w8("vector", self.em.tok("tensor"))
            self.v(lambda v: v.memset(MED2[64:128, 0:1024], 0.0), inc=True)
            tok_xnT = self.em.tok("vector")

            def xnT(c):
                return BC[:, c * 2048:(c + 1) * 2048]
            for part in range(3):
                base_col = part * 1024 + hp * 512
                wsl = MED[:, 0:4096]
                _qg = [self.tok_cast["qkv16"], self.em.tok("tensor")]
                tok_w = self.dq("qkv").start(
                    wsl[:, 0:2048].rearrange("p (c q) -> p c q", q=512),
                    qkv_r[:, 0:4, base_col:base_col + 512], wait=_qg)
                tok_w2 = self.dq("qkv2").start(
                    wsl[:, 2048:4096].rearrange("p (c q) -> p c q", q=512),
                    qkv_r[:, 4:8, base_col:base_col + 512], wait=_qg)
                self.w8("tensor", tok_w)
                self.w8("tensor", tok_w2)
                self.w8("tensor", tok_xnT)
                for n in range(NT):
                    bk = n % 4
                    self.use_bank(bk)
                    for kc in range(8):
                        self.t(lambda t, bk=bk, kc=kc, n=n, wsl=wsl: t.matmul(
                            self.bankf(bk), xnT(kc)[:, n * 128:(n + 1) * 128],
                            wsl[:, kc * 512:(kc + 1) * 512],
                            start=(kc == 0), stop=(kc == 7)), inc=(kc == 7))
                    self.wrote_bank(bk)
                    self.read_bank(bk, "vector")
                    if part < 2:
                        dstt = qro(n) if part == 0 else kro(n)
                        ps4 = self.bankf(bk).rearrange("p (h s q) -> p h s q",
                                                       s=2, q=32)
                        cos4 = CST["ropec"][:, n * 32:(n + 1) * 32] \
                            .unsqueeze(1).unsqueeze(2).broadcast_to((128, 8, 2, 32))
                        sin3 = CST["ropes"][:, n * 32:(n + 1) * 32] \
                            .unsqueeze(1).broadcast_to((128, 8, 32))
                        t1 = BB[:, 14336:14848]
                        t2 = BB[:, 14848:15360]
                        t1v = t1.rearrange("p (h s q) -> p h s q", s=2, q=32)
                        t2v = t2.rearrange("p (h s q) -> p h s q", s=2, q=32)
                        dsv = dstt.rearrange("p (h s q) -> p h s q", s=2, q=32)
                        self.v(lambda v, t1v=t1v, ps4=ps4, cos4=cos4: v.tensor_tensor(
                            out=t1v, in0=ps4, in1=cos4, op=ALU.mult))
                        self.v(lambda v, t2v=t2v, ps4=ps4, sin3=sin3: v.tensor_tensor(
                            out=t2v[:, :, 0, :], in0=ps4[:, :, 1, :], in1=sin3,
                            op=ALU.mult))
                        self.v(lambda v, t2v=t2v, ps4=ps4, sin3=sin3: v.tensor_tensor(
                            out=t2v[:, :, 1, :], in0=ps4[:, :, 0, :], in1=sin3,
                            op=ALU.mult))
                        self.v(lambda v, dsv=dsv, t1v=t1v, t2v=t2v:
                               v.scalar_tensor_tensor(
                                   out=dsv[:, :, 0, :], in0=t2v[:, :, 0, :],
                                   scalar=-1.0, in1=t1v[:, :, 0, :],
                                   op0=ALU.mult, op1=ALU.add))
                        self.v(lambda v, dsv=dsv, t1v=t1v, t2v=t2v: v.tensor_tensor(
                            out=dsv[:, :, 1, :], in0=t1v[:, :, 1, :],
                            in1=t2v[:, :, 1, :], op=ALU.add), inc=True)
                    else:
                        self.v(lambda v, kt=n, bk=bk: v.tensor_copy(
                            out=vpk(kt).rearrange("p (h q) -> p h q", q=65)[:, :, 0:64],
                            in_=self.bankf(bk).rearrange("p (h q) -> p h q", q=64)))
                        self.v(lambda v, kt=n: v.memset(
                            vpk(kt).rearrange("p (h q) -> p h q", q=65)[:, :, 64:65],
                            1.0), inc=True)
                    self.bank_read_done(bk, "vector")
            tok_proj = self.em.tok("vector")
            self.v(lambda v: v.memset(BC[64:128, 0:16384], 0.0), inc=True)
            tok_bczero = self.em.tok("vector")
            for hh in range(8):
                for si, (srcf, dstf) in enumerate(((qro, qTh), (kro, kTh))):
                    self.w8("tensor", tok_proj)
                    for bi in range(2):
                        self.transpose8(
                            [srcf(nn)[:, hh * 64:(hh + 1) * 64]
                             for nn in range(bi * 8, bi * 8 + 8)],
                            2 + si * 2 + bi,
                            dstf[0:64, bi * 1024:(bi + 1) * 1024],
                            "vector", prow=64)
                tok_qkT = self.em.tok("vector")
                for qtc in range(2):
                    exp_toks = {}
                    pv_toks = {}
                    prev_pv = self._last_pv

                    def pv_mms(kt):
                        ptb = pT(kt % 3)
                        self.w8("tensor", exp_toks[kt])
                        for sub in range(2):
                            self.t(lambda t, kt=kt, sub=sub, ptb=ptb, hh=hh: t.matmul(
                                self.bankf(6 + sub)[0:65, :],
                                vpk(kt, hh),
                                ptb[:, sub * 512:(sub + 1) * 512],
                                start=(kt == 0), stop=(kt == NT - 1)),
                                inc=(sub == 1))
                        pv_toks[kt] = self.em.tok("tensor")
                    for mt in range(0, NT, 2):
                        for kt in (mt, mt + 1):
                            sb = 2 if kt % 2 == 0 else 4
                            self.use_bank(sb)
                            self.use_bank(sb + 1)
                            self.w8("tensor", tok_qkT)
                            for sub in range(2):
                                self.t(lambda t, sb=sb, sub=sub, kt=kt, qtc=qtc:
                                       t.matmul(
                                           self.bankf(sb + sub),
                                           kTh[:, kt * 128:(kt + 1) * 128],
                                           qTh[:, qtc * 1024 + sub * 512:
                                               qtc * 1024 + sub * 512 + 512],
                                           start=True, stop=True), inc=(sub == 1))
                            self.wrote_bank(sb)
                            self.wrote_bank(sb + 1)
                        if mt == 2:
                            if self._pend_epi is not None:
                                self._pend_epi()
                                self._pend_epi = None
                            self.use_bank(6)
                            self.use_bank(7)
                        if mt == 6 and self._pend_epi2 is not None:
                            self._pend_epi2()
                            self._pend_epi2 = None
                        if mt >= 2:
                            pv_mms(mt - 2)
                            pv_mms(mt - 1)
                        for kt in (mt, mt + 1):
                            sb = 2 if kt % 2 == 0 else 4
                            ptb = pT(kt % 3)
                            self.read_bank(sb, "scalar")
                            self.read_bank(sb + 1, "scalar")
                            if kt >= 3:
                                self.w8("scalar", pv_toks[kt - 3])
                            elif prev_pv is not None:
                                self.w8("scalar", prev_pv[{0: 15, 1: 13, 2: 14}[kt]])
                            self.s(lambda s, sb=sb, ptb=ptb: s.activation(
                                out=ptb, in_=self.PS[:, sb * 512:(sb + 2) * 512],
                                func=AF.Exp, scale=0.125), inc=True)
                            self.bank_read_done(sb, "scalar")
                            self.bank_read_done(sb + 1, "scalar")
                            exp_toks[kt] = self.em.tok("scalar")
                    pv_mms(NT - 2)
                    pv_mms(NT - 1)
                    self.wrote_bank(6)
                    self.wrote_bank(7)
                    self._last_pv = pv_toks

                    def epi(hh=hh, qtc=qtc):
                        pvb = BA[0:64, 12352:13376]
                        self.read_bank(6, "vector")
                        self.read_bank(7, "vector")
                        self.v(lambda v, pvb=pvb: v.tensor_copy(
                            out=pvb, in_=self.PS[0:64, 3072:4096]))
                        self.v(lambda v: v.tensor_copy(
                            out=ZR, in_=self.PS[64:65, 3072:4096]), inc=True)
                        self.bank_read_done(6, "vector")
                        self.bank_read_done(7, "vector")
                        self.w8("vector", self._tok_repl)
                        self.v(lambda v: v.reciprocal(out=ZR, in_=ZR))
                        self.v(lambda v: v.tensor_copy(out=ZRB, in_=ZR),
                               inc=True)
                        self._tok_z = self.em.tok("vector")

                        def epi2(hh=hh, qtc=qtc, pvb=pvb):
                            self.use_bank(0)
                            self.use_bank(1)
                            self.w8("tensor", self._tok_z)
                            for sub in range(2):
                                self.t(lambda t, sub=sub: t.matmul(
                                    self.bankf(0 + sub)[0:64, :],
                                    CST["ones_col"][:],
                                    ZRB[:, sub * 512:(sub + 1) * 512],
                                    start=True, stop=True), inc=(sub == 1))
                            self._tok_repl = self.em.tok("tensor")
                            self.wrote_bank(0)
                            self.wrote_bank(1)
                            self.read_bank(0, "vector")
                            self.read_bank(1, "vector")
                            self.v(lambda v: v.tensor_copy(
                                out=ZB, in_=self.PS[0:64, 0:1024]), inc=True)
                            self.bank_read_done(0, "vector")
                            self.bank_read_done(1, "vector")
                            self.v(lambda v, hh=hh, qtc=qtc, pvb=pvb:
                                   v.tensor_tensor(
                                       out=aoT(hh)[:, qtc * 1024:(qtc + 1) * 1024],
                                       in0=pvb, in1=ZB,
                                       op=ALU.mult), inc=True)
                        self._pend_epi2 = epi2
                    self._pend_epi = epi
            if self._pend_epi is not None:
                self._pend_epi()
                self._pend_epi = None
            if self._pend_epi2 is not None:
                self._pend_epi2()
                self._pend_epi2 = None
            wo_5d = self.scr["wo16"].rearrange(
                "(a c p) (hf q) -> a p c hf q", p=64, c=8, q=512)
            wosl = MED[:, 4096:8192]
            tok_ao = self.em.tok("vector")
            for half in range(2):
                tok_wo = self.dq("wo").start(
                    wosl[0:64, :].rearrange("p (c q) -> p c q", q=512),
                    wo_5d[hp, :, :, half],
                    wait=[self.tok_cast["wo16"], self.em.tok("tensor")])
                self.w8("tensor", tok_wo)
                self.w8("tensor", tok_ao)
                self.w8("tensor", tok_bczero)
                for n in range(NT):
                    bk = n % 2
                    self.use_bank(bk)
                    for hh in range(8):
                        self.t(lambda t, bk=bk, hh=hh, n=n: t.matmul(
                            self.bankf(bk),
                            BC[:, hh * 2048 + n * 128:hh * 2048 + (n + 1) * 128],
                            wosl[:, hh * 512:(hh + 1) * 512],
                            start=(hh == 0), stop=(hh == 7)), inc=(hh == 7))
                    self.wrote_bank(bk)
                    self.read_bank(bk, "vector")
                    if hp == 0:
                        stage = BC[:, 16384 + bk * 512:16384 + bk * 512 + 512]
                        self.w8("vector", self._aob_st.get(bk))
                        self.v(lambda v, bk=bk, stage=stage: v.tensor_copy(
                            out=stage, in_=self.bankf(bk)), inc=True)
                        self.bank_read_done(bk, "vector")
                        self._aob_st[bk] = self.dq(f"aos{bk}").start(
                            aob_r[:, n, half * 512:half * 512 + 512], stage,
                            wait=[self.em.tok("vector")])
                    else:
                        lslot = BC[:, 16384 + bk * 512:16384 + bk * 512 + 512]
                        ltok = self.dq(f"aol{bk}").start(
                            lslot, aob_r[:, n, half * 512:half * 512 + 512],
                            wait=[self.em.tok("vector"), self._aob_st.get(0),
                                  self._aob_st.get(1)])
                        self.w8("vector", ltok)
                        self.resid_update(n, half, self.bankf(bk), extra=lslot)
                        self.bank_read_done(bk, "vector")


def kernel(L0_attn_a_out, L0_attn_a_W_write, L0_attn_a_W_read, L0_attn_a_W_beta,
           L0_attn_a_decay, L0_attn_b_Wqkv, L0_attn_b_Wo, L0_ffn_w_gate,
           L0_ffn_w_up, L0_ffn_w_down, _parts=4):
    consts = host_constants()
    nc = build_program(parts=_parts)
    decay_t = np.ascontiguousarray(
        np.tile(np.asarray(L0_attn_a_decay, dtype=np.float32)[None, :], (128, 1)))
    shared = dict(
        Ww=np.ascontiguousarray(L0_attn_a_W_write, dtype=np.float32),
        Wr=np.ascontiguousarray(L0_attn_a_W_read, dtype=np.float32),
        Wb=np.ascontiguousarray(L0_attn_a_W_beta, dtype=np.float32),
        decay_t=decay_t,
        Wqkv=np.ascontiguousarray(L0_attn_b_Wqkv, dtype=np.float32),
        Wo=np.ascontiguousarray(L0_attn_b_Wo, dtype=np.float32),
        wg=np.ascontiguousarray(L0_ffn_w_gate, dtype=np.float32),
        wu=np.ascontiguousarray(L0_ffn_w_up, dtype=np.float32),
        wd=np.ascontiguousarray(L0_ffn_w_down, dtype=np.float32),
        **consts,
    )
    xs = np.ascontiguousarray(L0_attn_a_out, dtype=np.float32)
    in_maps = [dict(x=xs[i], **shared) for i in range(8)]
    res = run_bass_kernel_spmd(nc, in_maps, core_ids=list(range(8)))
    return np.stack([res.results[i]["y"] for i in range(8)], axis=0)
